# version 02
# baseline (speedup 1.0000x reference)
import contextlib
import numpy as np
import concourse.bass as bass
import concourse.mybir as mybir
from concourse.bass_utils import run_bass_kernel_spmd

F32 = mybir.dt.float32
F32R = mybir.dt.float32r
BF16 = mybir.dt.bfloat16
AF = mybir.ActivationFunctionType
AX = mybir.AxisListType
OP = mybir.AluOpType

B, SEQ, D = 4, 4096, 4096
NH, HD = 32, 128
INT, LAT, NE = 8192, 2048, 8
TOK = 2048
DC = D // 128
IC = INT // 128
LC = LAT // 128
NT_OWN = TOK // 128
NT_SEQ = SEQ // 128
LN_EPS = 1e-5
RMS_EPS = 1e-6
FLN_EPS = 1e-5
ISQ = float(1.0 / np.sqrt(HD))

DEBUG_STOP = None
DEBUG_TAPS = ()

DMA_SEMS = ("lx0", "lx1", "sx0", "sx1", "sb0", "sb1", "sb2", "sb3",
            "tx0", "tx1", "w0", "w1", "gb", "so")
ALL_SEMS = DMA_SEMS + ("pe", "dv", "ac", "gc")


class Led:
    def __init__(self, nc, ctx):
        self.nc, self.ctx = nc, ctx
        self.sem, self.cnt, self.hw = {}, {}, {}

    def mk(self, *names):
        for n in names:
            self.sem[n] = self.ctx.enter_context(self.nc.semaphore(n))
            self.cnt[n] = 0

    def inc(self, instr, s, n=1):
        self.cnt[s] += n
        instr.then_inc(self.sem[s], n)
        return self.cnt[s]

    def wait(self, eng, ekey, s, val=None):
        if val is None:
            val = self.cnt[s]
        if val <= 0:
            return
        assert val <= self.cnt[s], f"wait {s}>={val} but only {self.cnt[s]} planned"
        if self.hw.get((ekey, s), 0) < val:
            eng.wait_ge(self.sem[s], val)
            self.hw[(ekey, s)] = val


def build():
    nc = bass.Bass("TRN2", target_bir_lowering=False)

    x_own = nc.dram_tensor("x_own", [TOK, D], F32, kind="ExternalInput").ap()
    x_oth = nc.dram_tensor("x_oth", [TOK, D], F32, kind="ExternalInput").ap()
    W = {}
    for nm, shp in (("Wq", [D, D]), ("Wk", [D, D]), ("Wv", [D, D]), ("Wo", [D, D]),
                    ("gate_w", [D, NE]), ("up_w", [D, INT]), ("down_w", [INT, D]),
                    ("fc1_w", [D, LAT]), ("fc2_w", [LAT, D]), ("expert_w", [LAT, LAT])):
        W[nm] = nc.dram_tensor(nm, shp, F32, kind="ExternalInput").ap()
    VEC = {}
    for nm in ("bq", "bk", "bv", "bo", "gq", "gk", "ln_g", "ln_b"):
        VEC[nm] = nc.dram_tensor(nm, [1, D], F32, kind="ExternalInput").ap()
    out_hbm = nc.dram_tensor("out", [TOK, D], F32, kind="ExternalOutput").ap()

    xn_sc = nc.dram_tensor("xn_sc", [SEQ, D], BF16).ap()
    xnT_sc = nc.dram_tensor("xnT_sc", [DC, 128, SEQ], BF16).ap()
    qr_sc = nc.dram_tensor("qr_sc", [TOK, D], BF16).ap()
    kr_sc = nc.dram_tensor("kr_sc", [SEQ, D], BF16).ap()
    qn_sc = nc.dram_tensor("qn_sc", [TOK, D], BF16).ap()
    kn_sc = nc.dram_tensor("kn_sc", [SEQ, D], BF16).ap()
    v_sc = nc.dram_tensor("v_sc", [SEQ, D], BF16).ap()
    ot_sc = nc.dram_tensor("ot_sc", [NT_OWN, NH, 128, 128], BF16).ap()
    u_sc = nc.dram_tensor("u_sc", [NT_OWN, IC, 128, 128], BF16).ap()
    xl_sc = nc.dram_tensor("xl_sc", [LAT, TOK], BF16).ap()
    moe_sc = nc.dram_tensor("moe_sc", [NT_OWN, LC, 128, 128], BF16).ap()
    rt_sc = nc.dram_tensor("rt_sc", [1, TOK], F32).ap()
    z_sc = nc.dram_tensor("z_sc", [TOK, D], F32).ap()
    at_sc = nc.dram_tensor("at_sc", [TOK, D], F32).ap()
    scratch = {"xn_sc": xn_sc, "qr_sc": qr_sc, "kr_sc": kr_sc, "qn_sc": qn_sc,
               "kn_sc": kn_sc, "v_sc": v_sc, "xl_sc": xl_sc, "rt_sc": rt_sc,
               "z_sc": z_sc, "at_sc": at_sc,
               "ot_sc": ot_sc, "u_sc": u_sc, "moe_sc": moe_sc}
    dbg = {nm: nc.dram_tensor("dbg_" + nm, list(scratch[nm].shape), scratch[nm].dtype,
                              kind="ExternalOutput").ap() for nm in DEBUG_TAPS}

    ctx = contextlib.ExitStack()
    with ctx:
        L = Led(nc, ctx)
        L.mk(*ALL_SEMS)

        def sbuf(name, shape, dt):
            return ctx.enter_context(nc.sbuf_tensor(name, shape, dt))

        b0 = sbuf("b0", [128, 65536], BF16)
        b1 = sbuf("b1", [128, 32768], BF16)
        stage = [sbuf(f"stage{i}", [128, 512], BF16) for i in range(4)]
        ones_fr = sbuf("ones_fr", [1, 128], F32R)
        ones_f = sbuf("ones_f", [1, 128], F32)
        onescol_bf = sbuf("onescol_bf", [128, 1], BF16)
        ones_bfr = sbuf("ones_bfr", [1, 128], BF16)
        ident_f = sbuf("ident_f", [128, 128], F32)
        rowf = sbuf("rowf", [1, 512], F32)
        rowfr = sbuf("rowfr", [1, 512], F32R)
        negm_row = sbuf("negm_row", [1, 512], F32R)
        rs_row = sbuf("rs_row", [1, 512], F32R)
        m8 = sbuf("m8", [128, 8], F32)
        eps5 = sbuf("eps5", [128, 1], F32)
        eps6 = sbuf("eps6", [128, 1], F32)
        lnisq = sbuf("lnisq", [128, 1], F32)
        zero1 = sbuf("zero1", [128, 1], F32)
        lg8 = sbuf("lg8", [128, 8], F32)
        g2 = sbuf("g2", [128, 8], F32)
        small = {nm: sbuf(nm, [128, 1], F32) for nm in
                 ("s1", "s2", "mu", "msq", "var", "sd", "rstd", "nbias",
                  "m1", "m2", "rt1")}
        mparts = sbuf("mparts", [128, 8], F32)
        negm = sbuf("negm", [128, 1], F32)
        S = small

        def b0f(lo, n):
            return b0[:].bitcast(F32)[:, lo:lo + n]

        def b0b(lo, n):
            return b0[:, lo:lo + n]

        def b1f(lo, n):
            return b1[:].bitcast(F32)[:, lo:lo + n]

        def b1b(lo, n):
            return b1[:, lo:lo + n]

        ps = [ctx.enter_context(nc.psum_tensor(f"ps{i}", [128, 512], F32))
              for i in range(8)]

        block = ctx.enter_context(nc.Block())

        @block.sync
        def _(sy):
            gd = nc.gpsimd
            pe = nc.tensor
            dv = nc.vector
            ac = nc.scalar
            ENG = {"pe": pe, "dv": dv, "ac": ac, "gd": gd, "sy": sy}

            def barrier(*ekeys):
                for ek in ekeys:
                    e = ENG[ek]
                    for s in ALL_SEMS:
                        if s == "so":
                            continue
                        L.wait(e, ek, s, L.cnt[s])

            gd.memset(ones_f[:], 1.0)
            gd.memset(onescol_bf[:], 1.0)
            v0 = L.inc(gd.memset(ident_f[:], 0.0), "gc")
            L.wait(gd, "gd", "gc", v0)
            ins = gd.affine_select(out=ident_f[:], in_=ident_f[:],
                                   compare_op=OP.not_equal, fill=1.0, base=0,
                                   pattern=[[-1, 128]], channel_multiplier=1)
            L.inc(ins, "gc")
            dv.memset(ones_bfr[:], 1.0)
            dv.memset(m8[:], 3.0e38)
            dv.memset(eps5[:], LN_EPS)
            dv.memset(eps6[:], RMS_EPS)
            dv.memset(lnisq[:], float(np.log(ISQ)))
            dv.memset(zero1[:], 0.0)
            L.wait(dv, "dv", "gc", 1)
            L.inc(dv.tensor_copy(ones_fr[:], ones_f[:]), "dv")
            init_dv = L.cnt["dv"]

            def finish():
                for s in ALL_SEMS:
                    if s != "so":
                        L.wait(sy, "sy", s, L.cnt[s])
                for nm, dap in dbg.items():
                    L.inc(sy.dma_start(out=dap, in_=scratch[nm]), "so", 16)
                L.wait(sy, "sy", "so", L.cnt["so"])

            def ln_stats(rowx, rowsq, ld_sem, ld_val, sq_guard=0):
                L.wait(dv, "dv", ld_sem, ld_val)
                dv.reduce_sum(S["s1"][:], rowx, axis=AX.X)
                L.wait(ac, "ac", ld_sem, ld_val)
                if sq_guard:
                    L.wait(ac, "ac", "dv", sq_guard)
                sq = L.inc(ac.activation(rowsq, rowx, AF.Square), "ac")
                L.wait(dv, "dv", "ac", sq)
                sed = L.inc(dv.reduce_sum(S["s2"][:], rowsq, axis=AX.X), "dv")
                L.wait(dv, "dv", "dv", sed)
                dv.tensor_scalar_mul(S["mu"][:], S["s1"][:], 1.0 / D)
                v1 = L.inc(dv.tensor_scalar_mul(S["s2"][:], S["s2"][:], 1.0 / D), "dv")
                L.wait(dv, "dv", "dv", v1)
                v2 = L.inc(dv.tensor_mul(S["msq"][:], S["mu"][:], S["mu"][:]), "dv")
                L.wait(dv, "dv", "dv", v2)
                st = L.inc(dv.tensor_sub(S["var"][:], S["s2"][:], S["msq"][:]), "dv")
                L.wait(ac, "ac", "dv", st)
                L.wait(ac, "ac", "dv", init_dv)
                lnv = L.inc(ac.activation(S["sd"][:], S["var"][:], AF.Ln,
                                          bias=eps5[:]), "ac")
                L.wait(ac, "ac", "ac", lnv)
                sd = L.inc(ac.activation(S["rstd"][:], S["sd"][:], AF.Exp,
                                         scale=-0.5, bias=zero1[:]), "ac")
                L.wait(dv, "dv", "ac", sd)
                v3 = L.inc(dv.tensor_mul(S["nbias"][:], S["mu"][:], S["rstd"][:]), "dv")
                L.wait(dv, "dv", "dv", v3)
                stats = L.inc(dv.tensor_scalar_mul(S["nbias"][:], S["nbias"][:],
                                                   -1.0), "dv")
                return stats, sd

            rax = [b0f(0, 4096), b0f(4096, 4096)]
            rsq = [b0f(8192, 4096), b0f(12288, 4096)]
            rno = [b0b(32768, 4096), b0b(36864, 4096)]
            rax_g = [0, 0]
            rsq_g = [0, 0]
            rno_g = [0, 0]
            for i in range(NT_SEQ):
                p = i % 2
                src = x_own if i < NT_OWN else x_oth
                r0 = (i % NT_OWN) * 128
                lxs = f"lx{p}"
                L.wait(sy, "sy", "ac", rax_g[p])
                ld = L.inc(sy.dma_start(out=rax[p], in_=src[r0:r0 + 128, :]), lxs, 16)
                stats, sd = ln_stats(rax[p], rsq[p], lxs, ld, rsq_g[p])
                rsq_g[p] = L.cnt["dv"]
                L.wait(ac, "ac", "dv", stats)
                L.wait(ac, "ac", "ac", sd)
                L.wait(ac, "ac", f"sx{p}", rno_g[p])
                nm_ = L.inc(ac.activation(rno[p], rax[p], AF.Identity,
                                          bias=S["nbias"][:], scale=S["rstd"][:]),
                            "ac")
                rax_g[p] = nm_
                L.wait(sy, "sy", "ac", nm_)
                rno_g[p] = L.inc(
                    sy.dma_start(out=xn_sc[i * 128:(i + 1) * 128, :], in_=rno[p]),
                    f"sx{p}", 16)

            if DEBUG_STOP == "A":
                return finish()
            xnT = b0[:]
            barrier("sy", "gd", "pe", "dv", "ac")
            qkv_grp = 0
            grp_ev = {}
            b_store = {}
            jobs_by_half = {0: [("Wk", "bk"), ("Wv", "bv"), ("Wq", "bq")],
                            1: [("Wk", "bk"), ("Wv", "bv")]}
            jobidx = 0
            slab_use = {}
            last_rowfr_pe = 0
            for half in (0, 1):
                if half == 1:
                    L.wait(sy, "sy", "pe", L.cnt["pe"])
                    L.wait(sy, "sy", "sx0", L.cnt["sx0"])
                for ch in range(DC):
                    L.inc(sy.dma_start(
                        out=xnT[:, ch * TOK:(ch + 1) * TOK],
                        in_=xn_sc[half * TOK:(half + 1) * TOK,
                                  ch * 128:(ch + 1) * 128],
                        transpose=True), "tx0", 16)
                xnT_ld = L.cnt["tx0"]
                if half == 0:
                    L.wait(sy, "sy", "tx0", xnT_ld)
                    for ch in range(DC):
                        L.inc(sy.dma_start(
                            out=xnT_sc[ch, :, 0:TOK],
                            in_=xnT[:, ch * TOK:(ch + 1) * TOK]), "sx0", 16)
                for wnm, bnm in jobs_by_half[half]:
                    for ib in range(8):
                        wsem = f"w{jobidx % 2}"
                        if jobidx >= 2:
                            L.wait(gd, "gd", "pe", slab_use[jobidx - 2])
                        wsl = b1b(0, 16384) if jobidx % 2 == 0 else b1b(16384, 16384)
                        wld = L.inc(gd.dma_start(
                            out=wsl.rearrange("p (c f) -> p c f", f=512),
                            in_=W[wnm][:, ib * 512:(ib + 1) * 512].rearrange(
                                "(c p) f -> p c f", p=128)), wsem, 16)
                        L.wait(gd, "gd", "pe", last_rowfr_pe)
                        bld = L.inc(gd.dma_start(
                            out=rowf[:],
                            in_=VEC[bnm][0:1, ib * 512:(ib + 1) * 512]), "gb", 16)
                        L.wait(dv, "dv", "gb", bld)
                        brc = L.inc(dv.tensor_copy(rowfr[:], rowf[:]), "dv")
                        for tt in range(NT_OWN):
                            g = qkv_grp
                            bank = g % 4
                            L.wait(pe, "pe", "dv", max(grp_ev.get(g - 4, 0), brc))
                            L.wait(pe, "pe", wsem, wld)
                            L.wait(pe, "pe", "tx0", xnT_ld)
                            for ch in range(DC):
                                pe.matmul(ps[bank][:],
                                          xnT[:, ch * TOK + tt * 128:
                                              ch * TOK + (tt + 1) * 128],
                                          wsl[:, ch * 512:(ch + 1) * 512],
                                          start=(ch == 0), stop=False)
                            mm = pe.matmul(ps[bank][:], ones_fr[:], rowfr[:],
                                           start=False, stop=True)
                            gend = L.inc(mm, "pe")
                            L.wait(dv, "dv", "pe", gend)
                            L.wait(dv, "dv", f"sb{bank}", b_store.get(bank, 0))
                            ev = L.inc(dv.tensor_copy(stage[bank][:], ps[bank][:]),
                                       "dv")
                            grp_ev[g] = ev
                            L.wait(sy, "sy", "dv", ev)
                            trow = half * TOK + tt * 128
                            if wnm == "Wq":
                                dst = qr_sc[tt * 128:tt * 128 + 128,
                                            ib * 512:(ib + 1) * 512]
                            elif wnm == "Wk":
                                dst = kr_sc[trow:trow + 128, ib * 512:(ib + 1) * 512]
                            else:
                                dst = v_sc[trow:trow + 128, ib * 512:(ib + 1) * 512]
                            b_store[bank] = L.inc(
                                sy.dma_start(out=dst, in_=stage[bank][:]),
                                f"sb{bank}", 16)
                            qkv_grp += 1
                        slab_use[jobidx] = L.cnt["pe"]
                        last_rowfr_pe = L.cnt["pe"]
                        jobidx += 1

            if DEBUG_STOP == "B":
                return finish()
            rcin = [b0b(0, 4096), b0b(4096, 4096)]
            rtmp = b0f(4096, 4096)
            rsqc = [b0f(8192, 4096), b0f(12288, 4096)]
            rnoc = [b0b(32768, 4096), b0b(36864, 4096)]
            gbc = b1f(0, 4096)
            barrier("sy", "dv", "ac")
            c_jobs = [("q", qr_sc, qn_sc, NT_OWN, "gq", True),
                      ("k", kr_sc, kn_sc, NT_SEQ, "gk", False)]
            rc_g = [0, 0]
            rn_g = [0, 0]
            rq_g = [0, 0]
            ev_g = [0, 0]
            for jname, src_sc, dst_sc, ntl, gnm, isq in c_jobs:
                L.wait(sy, "sy", "dv", L.cnt["dv"])
                gld = L.inc(sy.dma_start(out=gbc,
                                         in_=VEC[gnm].partition_broadcast(128)),
                            "tx0", 16)
                for i in range(ntl):
                    p = i % 2
                    lxs = f"lx{p}"
                    L.wait(sy, "sy", "ac", rc_g[p])
                    ld = L.inc(sy.dma_start(out=rcin[p],
                                            in_=src_sc[i * 128:(i + 1) * 128, :]),
                               lxs, 16)
                    L.wait(ac, "ac", lxs, ld)
                    if rq_g[p]:
                        L.wait(ac, "ac", "dv", rq_g[p])
                    sq = L.inc(ac.activation(rsqc[p], rcin[p], AF.Square), "ac")
                    L.wait(dv, "dv", "ac", sq)
                    red = L.inc(dv.reduce_sum(S["s2"][:], rsqc[p], axis=AX.X), "dv")
                    rq_g[p] = red
                    L.wait(dv, "dv", "dv", red)
                    va = L.inc(dv.tensor_scalar_mul(S["var"][:], S["s2"][:], 1.0 / D),
                               "dv")
                    L.wait(ac, "ac", "dv", va)
                    lnv = L.inc(ac.activation(S["sd"][:], S["var"][:], AF.Ln,
                                              bias=eps6[:]), "ac")
                    L.wait(ac, "ac", "ac", lnv)
                    sd = L.inc(ac.activation(S["rstd"][:], S["sd"][:], AF.Exp,
                                             scale=-0.5,
                                             bias=(lnisq[:] if isq else zero1[:])),
                               "ac")
                    L.wait(ac, "ac", "ac", sd)
                    t_ = L.inc(ac.activation(rtmp, rcin[p], AF.Identity,
                                             scale=S["rstd"][:]), "ac")
                    rc_g[p] = t_
                    L.wait(dv, "dv", "ac", t_)
                    L.wait(dv, "dv", "tx0", gld)
                    L.wait(dv, "dv", f"sx{p}", rn_g[p])
                    ev = L.inc(dv.tensor_mul(rnoc[p], rtmp, gbc), "dv")
                    ev_g[p] = ev
                    L.wait(sy, "sy", "dv", ev)
                    rn_g[p] = L.inc(
                        sy.dma_start(out=dst_sc[i * 128:(i + 1) * 128, :],
                                     in_=rnoc[p]), f"sx{p}", 16)

            if DEBUG_STOP == "C":
                return finish()
            KT = [b0b(0, 4096), b0b(4096, 4096)]
            QT = [b0b(8192, 2048), b0b(10240, 2048)]
            VS = [b0b(12288, 4096), b0b(16384, 4096)]
            PT = [b0b(20480 + 512 * i, 512) for i in range(4)]
            rb_sb = b0f(14336, 512)
            otst = [b0b(30720, 512), b0b(31232, 512)]

            def nrow(hh, qb):
                off = 32768 + ((hh % 2) * 4 + qb) * 512
                return b0[0:1, off:off + 512]

            barrier("sy", "pe", "dv", "ac")
            d_head_pe = {-1: L.cnt["pe"], -2: L.cnt["pe"]}
            d_loads = {}
            bank_free = {b: ("dv", 0) for b in range(8)}
            pt_free = [0, 0, 0, 0]
            nr_ready = {}
            nr_user = {}
            ot_g = [0, 0]

            def pe_bank(b):
                s, v = bank_free[b]
                L.wait(pe, "pe", s, v)

            def d_load_head(h):
                if h >= NH:
                    return
                hp = h % 2
                lxs = f"lx{hp}"
                L.wait(sy, "sy", "pe", d_head_pe[h - 2])
                L.inc(sy.dma_start(out=KT[hp][:, :],
                                   in_=kn_sc[:, h * 128:(h + 1) * 128],
                                   transpose=True), lxs, 16)
                L.inc(sy.dma_start(out=QT[hp][:, :],
                                   in_=qn_sc[:, h * 128:(h + 1) * 128],
                                   transpose=True), lxs, 16)
                for part in range(4):
                    L.inc(sy.dma_start(
                        out=VS[hp][:, part * 1024:(part + 1) * 1024].rearrange(
                            "p (kt j) -> p kt j", j=128),
                        in_=v_sc[part * 1024:(part + 1) * 1024,
                                 h * 128:(h + 1) * 128].rearrange(
                            "(kt p) j -> p kt j", p=128)), lxs, 16)
                d_loads[h] = L.cnt[lxs]

            d_load_head(0)
            d_load_head(1)

            def emit_p1_slot(h, slot):
                hp = h % 2
                qb, r = divmod(slot, 32)
                qt, kb = divmod(r, 8)
                qcol = (qb * 4 + qt) * 128
                bank = slot % 2
                pe_bank(bank)
                L.wait(pe, "pe", f"lx{hp}", d_loads[h])
                mm = pe.matmul(ps[bank][:], QT[hp][:, qcol:qcol + 128],
                               KT[hp][:, kb * 512:(kb + 1) * 512],
                               start=True, stop=True)
                pend = L.inc(mm, "pe")
                L.wait(dv, "dv", "pe", pend)
                rr = L.inc(dv.reduce_max(mparts[:, kb:kb + 1], ps[bank][:],
                                         axis=AX.X), "dv")
                bank_free[bank] = ("dv", rr)
                if kb == 7:
                    L.wait(dv, "dv", "dv", rr)
                    nm_ = L.inc(dv.reduce_max(negm[:], mparts[:], axis=AX.X,
                                              negate=True), "dv")
                    L.wait(dv, "dv", "dv", nm_)
                    L.wait(pe, "pe", "dv", nm_)
                    L.wait(pe, "pe", "gc", 2)
                    pe_bank(7)
                    tr = L.inc(pe.transpose(ps[7][0:1, 0:128], negm[:],
                                            ident_f[:]), "pe")
                    L.wait(dv, "dv", "pe", tr)
                    L.wait(dv, "dv", "pe", nr_user.get((h - 2, qb), 0))
                    cp = L.inc(dv.tensor_copy(
                        nrow(h, qb)[:, qt * 128:(qt + 1) * 128],
                        ps[7][0:1, 0:128]), "dv")
                    bank_free[7] = ("dv", cp)
                    if qt == 3:
                        nr_ready[(h, qb)] = cp

            def emit_p2_kt(h, qb, kt):
                hp = h % 2
                qcol = qb * 512
                bank = 2 + (kt % 3)
                pe_bank(bank)
                L.wait(pe, "pe", "dv", nr_ready[(h, qb)])
                pe.matmul(ps[bank][:], KT[hp][:, kt * 128:(kt + 1) * 128],
                          QT[hp][:, qcol:qcol + 512], start=True, stop=False)
                mm = pe.matmul(ps[bank][:], ones_bfr[:], nrow(h, qb),
                               start=False, stop=True)
                gend = L.inc(mm, "pe")
                nr_user[(h, qb)] = gend
                L.wait(ac, "ac", "pe", max(gend, pt_free[kt % 4]))
                ex = L.inc(ac.activation(PT[kt % 4][:], ps[bank][:], AF.Exp), "ac")
                bank_free[bank] = ("ac", ex)
                return ex

            def emit_p2_sums_av(h, qb, kt, ex):
                hp = h % 2
                L.wait(pe, "pe", "ac", ex)
                if kt == 0:
                    pe_bank(5)
                    pe_bank(6)
                pe.matmul(ps[5][0:1, :], onescol_bf[:], PT[kt % 4][:],
                          start=(kt == 0), stop=(kt == NT_SEQ - 1))
                mm = pe.matmul(ps[6][:], VS[hp][:, kt * 128:(kt + 1) * 128],
                               PT[kt % 4][:],
                               start=(kt == 0), stop=(kt == NT_SEQ - 1))
                pt_free[kt % 4] = L.inc(mm, "pe")

            def emit_p2_tail(h, qb):
                hp = h % 2
                avend = L.cnt["pe"]
                L.wait(dv, "dv", "pe", avend)
                rc0 = L.inc(dv.reciprocal(rowf[:], ps[5][0:1, :]), "dv")
                bank_free[5] = ("dv", rc0)
                L.wait(dv, "dv", "dv", rc0)
                rc = L.inc(dv.tensor_copy(rs_row[:], rowf[:]), "dv")
                L.wait(dv, "dv", "dv", rc)
                L.wait(pe, "pe", "dv", rc)
                pe_bank(7)
                bc = L.inc(pe.matmul(ps[7][:], ones_fr[:], rs_row[:],
                                     start=True, stop=True), "pe")
                L.wait(dv, "dv", "pe", bc)
                vrb = L.inc(dv.tensor_copy(rb_sb, ps[7][:]), "dv")
                bank_free[7] = ("dv", vrb)
                L.wait(dv, "dv", "dv", vrb)
                L.wait(dv, "dv", f"sx{qb % 2}", ot_g[qb % 2])
                ot = L.inc(dv.tensor_mul(otst[qb % 2][:], ps[6][:], rb_sb), "dv")
                bank_free[6] = ("dv", ot)
                L.wait(sy, "sy", "dv", ot)
                for qt in range(4):
                    ins2 = sy.dma_start(
                        out=ot_sc[qb * 4 + qt, h, :, :],
                        in_=otst[qb % 2][:, qt * 128:(qt + 1) * 128])
                    L.inc(ins2, f"sx{qb % 2}", 16)
                ot_g[qb % 2] = L.cnt[f"sx{qb % 2}"]

            for h in range(NH + 1):
                p1_slots = list(range(128)) if h < NH else []
                p1_i = 0
                if h == 0:
                    for slot in p1_slots:
                        emit_p1_slot(h, slot)
                    continue
                h2 = h - 1
                for qb in range(4):
                    prev_ex = None
                    for kt in range(NT_SEQ):
                        ex = emit_p2_kt(h2, qb, kt)
                        if kt > 0:
                            emit_p2_sums_av(h2, qb, kt - 1, prev_ex)
                        prev_ex = ex
                        if p1_i < len(p1_slots):
                            emit_p1_slot(h, p1_slots[p1_i])
                            p1_i += 1
                    emit_p2_sums_av(h2, qb, NT_SEQ - 1, prev_ex)
                    emit_p2_tail(h2, qb)
                while p1_i < len(p1_slots):
                    emit_p1_slot(h, p1_slots[p1_i])
                    p1_i += 1
                d_head_pe[h2] = L.cnt["pe"]
                d_load_head(h + 1)

            if DEBUG_STOP == "D":
                return finish()
            exT = [b0b(0, 16384), b0b(16384, 16384)]
            wsl_e = [b1b(0, 16384), b1b(16384, 16384)]
            gslab = b0b(32768, 256)
            rtrow = b0[0:1, 36864:40960].bitcast(F32)
            fstage = [b0f(20480 + 512 * i, 512) for i in range(4)]
            barrier("sy", "gd", "pe", "dv", "ac")
            e_base_pe = L.cnt["pe"]
            gld = L.inc(gd.dma_start(
                out=gslab.rearrange("p (c f) -> p c f", f=NE),
                in_=W["gate_w"].rearrange("(c p) f -> p c f", p=128)), "gb", 16)
            spans = [("up", "up_w", i) for i in range(INT // 512)] + \
                    [("fc1", "fc1_w", i) for i in range(LAT // 512)]
            nsteps = len(spans) * 4
            ex_ld = {}
            step_pe = {}
            wld_e = {}
            e_ev = {}
            e_sq = {}
            e_store = {}
            rt_done = 0

            def e_load(step):
                if step >= nsteps:
                    return
                tb = step % 4
                txs = f"tx{step % 2}"
                L.wait(sy, "sy", "pe", step_pe.get(step - 2, e_base_pe))
                L.inc(sy.dma_start(
                    out=exT[step % 2][:].rearrange("p (c f) -> p c f", f=512),
                    in_=xnT_sc[:, :, tb * 512:(tb + 1) * 512].rearrange(
                        "c p f -> p c f")), txs, 16)
                ex_ld[step] = L.cnt[txs]

            def e_wload(sp):
                if sp >= len(spans):
                    return
                kind, wname, idx = spans[sp]
                wsem = f"w{sp % 2}"
                L.wait(gd, "gd", "pe", step_pe.get((sp - 2) * 4 + 3, e_base_pe))
                wld_e[sp] = L.inc(gd.dma_start(
                    out=wsl_e[sp % 2].rearrange("p (c f) -> p c f", f=512),
                    in_=W[wname][:, idx * 512:(idx + 1) * 512].rearrange(
                        "(c p) f -> p c f", p=128)), wsem, 16)

            e_load(0)
            e_load(1)
            e_wload(0)
            e_wload(1)
            e_grp = 0
            for sp, (kind, wname, idx) in enumerate(spans):
                wsem = f"w{sp % 2}"
                for tb in range(4):
                    step = sp * 4 + tb
                    txs = f"tx{step % 2}"
                    for it in range(4):
                        g = e_grp
                        bank = g % 4
                        L.wait(pe, "pe", "dv", e_ev.get(g - 4, 0))
                        L.wait(pe, "pe", wsem, wld_e[sp])
                        L.wait(pe, "pe", txs, ex_ld[step])
                        for ch in range(DC - 1):
                            pe.matmul(ps[bank][:],
                                      wsl_e[sp % 2][:, ch * 512 + it * 128:
                                                    ch * 512 + (it + 1) * 128],
                                      exT[step % 2][:, ch * 512:(ch + 1) * 512],
                                      start=(ch == 0), stop=False)
                        ch = DC - 1
                        mm = pe.matmul(ps[bank][:],
                                       wsl_e[sp % 2][:, ch * 512 + it * 128:
                                                     ch * 512 + (it + 1) * 128],
                                       exT[step % 2][:, ch * 512:(ch + 1) * 512],
                                       start=False, stop=True)
                        gend = L.inc(mm, "pe")
                        if kind == "up":
                            L.wait(dv, "dv", "pe", gend)
                            L.wait(dv, "dv", "ac", e_sq.get(bank, 0))
                            mx = L.inc(dv.tensor_scalar_max(fstage[bank],
                                                            ps[bank][:], 0.0), "dv")
                            e_ev[g] = mx
                            L.wait(ac, "ac", "dv", mx)
                            L.wait(ac, "ac", f"sb{bank}", e_store.get(bank, 0))
                            sq = L.inc(ac.activation(stage[bank][:], fstage[bank],
                                                     AF.Square), "ac")
                            e_sq[bank] = sq
                            L.wait(sy, "sy", "ac", sq)
                            itg = idx * 4 + it
                            for j in range(4):
                                ins2 = sy.dma_start(
                                    out=u_sc[tb * 4 + j, itg, :, :],
                                    in_=stage[bank][:, j * 128:(j + 1) * 128])
                                L.inc(ins2, f"sb{bank}", 16)
                            e_store[bank] = L.cnt[f"sb{bank}"]
                        else:
                            L.wait(dv, "dv", "pe", gend)
                            L.wait(dv, "dv", f"sb{bank}", e_store.get(bank, 0))
                            ev = L.inc(dv.tensor_copy(stage[bank][:], ps[bank][:]),
                                       "dv")
                            e_ev[g] = ev
                            L.wait(sy, "sy", "dv", ev)
                            lt = idx * 4 + it
                            ins2 = sy.dma_start(
                                out=xl_sc[lt * 128:(lt + 1) * 128,
                                          tb * 512:(tb + 1) * 512],
                                in_=stage[bank][:])
                            e_store[bank] = L.inc(ins2, f"sb{bank}", 16)
                        e_grp += 1
                    if sp == 0:
                        for q in range(4):
                            tt = tb * 4 + q
                            L.wait(pe, "pe", "gb", gld)
                            L.wait(pe, "pe", "dv", rt_done)
                            for ch in range(DC - 1):
                                pe.matmul(ps[4][:, 0:8],
                                          exT[step % 2][:, ch * 512 + q * 128:
                                                        ch * 512 + (q + 1) * 128],
                                          gslab[:, ch * 8:(ch + 1) * 8],
                                          start=(ch == 0), stop=False)
                            ch = DC - 1
                            mm = pe.matmul(ps[4][:, 0:8],
                                           exT[step % 2][:, ch * 512 + q * 128:
                                                         ch * 512 + (q + 1) * 128],
                                           gslab[:, ch * 8:(ch + 1) * 8],
                                           start=False, stop=True)
                            gend = L.inc(mm, "pe")
                            L.wait(dv, "dv", "pe", gend)
                            cp = L.inc(dv.tensor_copy(lg8[:], ps[4][:, 0:8]), "dv")
                            L.wait(dv, "dv", "dv", cp)
                            r1 = L.inc(dv.reduce_max(S["m1"][:], lg8[:], axis=AX.X),
                                       "dv")
                            L.wait(dv, "dv", "dv", r1)
                            vm8 = L.inc(dv.tensor_copy(m8[:, 0:1], S["m1"][:]), "dv")
                            L.wait(dv, "dv", "dv", vm8)
                            mr = L.inc(dv.match_replace(g2[:], m8[:], lg8[:], -1e30),
                                       "dv")
                            L.wait(dv, "dv", "dv", mr)
                            r2 = L.inc(dv.reduce_max(S["m2"][:], g2[:], axis=AX.X),
                                       "dv")
                            L.wait(dv, "dv", "dv", r2)
                            rt = L.inc(dv.tensor_add(S["rt1"][:], S["m1"][:],
                                                     S["m2"][:]), "dv")
                            L.wait(pe, "pe", "dv", rt)
                            L.wait(pe, "pe", "gc", 2)
                            tr = L.inc(pe.transpose(ps[7][0:1, 0:128], S["rt1"][:],
                                                    ident_f[:]), "pe")
                            L.wait(dv, "dv", "pe", tr)
                            rt_done = L.inc(
                                dv.tensor_copy(rtrow[:, tt * 128:(tt + 1) * 128],
                                               ps[7][0:1, 0:128]), "dv")
                    step_pe[step] = L.cnt["pe"]
                    e_load(step + 2)
                e_wload(sp + 2)
            L.wait(sy, "sy", "dv", rt_done)
            rt_st = L.inc(sy.dma_start(out=rt_sc, in_=rtrow), "sx0", 16)

            if DEBUG_STOP == "E":
                return finish()
            xlat = b0b(0, 32768)
            eslab = b1b(0, 32768)
            rb_full = b0f(16384, 2048)
            barrier("sy", "gd", "pe", "dv")
            xld = L.inc(sy.dma_start(
                out=xlat.rearrange("p (c t) -> p c t", t=TOK),
                in_=xl_sc.rearrange("(c p) t -> p c t", p=128)), "lx0", 16)
            rld = L.inc(sy.dma_start(out=rb_full,
                                     in_=rt_sc.partition_broadcast(128)), "lx1", 16)
            eld = L.inc(gd.dma_start(
                out=eslab.rearrange("p (c m) -> p c m", m=LAT),
                in_=W["expert_w"].rearrange("(c p) m -> p c m", p=128)), "w0", 16)
            f_grp = 0
            f_ev = {}
            f_store = {}
            for mt in range(LC):
                for tb in range(4):
                    g = f_grp
                    bank = g % 4
                    L.wait(pe, "pe", "dv", f_ev.get(g - 4, 0))
                    L.wait(pe, "pe", "lx0", xld)
                    L.wait(pe, "pe", "w0", eld)
                    gend = None
                    for ch in range(LC):
                        mm = pe.matmul(ps[bank][:],
                                       eslab[:, ch * 2048 + mt * 128:
                                             ch * 2048 + (mt + 1) * 128],
                                       xlat[:, ch * 2048 + tb * 512:
                                            ch * 2048 + (tb + 1) * 512],
                                       start=(ch == 0), stop=(ch == LC - 1))
                    gend = L.inc(mm, "pe")
                    L.wait(dv, "dv", "pe", gend)
                    L.wait(dv, "dv", "lx1", rld)
                    L.wait(dv, "dv", f"sb{bank}", f_store.get(bank, 0))
                    ev = L.inc(dv.scalar_tensor_tensor(
                        stage[bank][:], ps[bank][:], 1.0,
                        rb_full[:, tb * 512:(tb + 1) * 512],
                        OP.mult, OP.mult), "dv")
                    f_ev[g] = ev
                    L.wait(sy, "sy", "dv", ev)
                    for j in range(4):
                        ins2 = sy.dma_start(out=moe_sc[tb * 4 + j, mt, :, :],
                                            in_=stage[bank][:, j * 128:(j + 1) * 128])
                        L.inc(ins2, f"sb{bank}", 16)
                    f_store[bank] = L.cnt[f"sb{bank}"]
                    f_grp += 1

            if DEBUG_STOP == "F":
                return finish()
            dslab = b0b(0, 32768)
            f2slab = b0b(32768, 8192)
            woslab = b0b(40960, 16384)
            fstage = [b0f(28672 + 512 * i, 512) for i in range(4)]
            u_tt = [b1b(0, 8192), b1b(8192, 8192)]
            moe_tt = [b1b(16384, 2048), b1b(18432, 2048)]
            ot_tt = [b1b(20480, 4096), b1b(24576, 4096)]
            barrier("sy", "gd", "pe", "dv")
            g_base_pe = L.cnt["pe"]
            g_pe = {}
            g_ev = {}
            g_st = [0, 0]
            g_ld = {}
            for fb in range(8):
                L.wait(gd, "gd", "pe", g_pe.get((fb - 1, NT_OWN - 1), g_base_pe))
                L.inc(gd.dma_start(
                    out=dslab.rearrange("p (c f) -> p c f", f=512),
                    in_=W["down_w"][:, fb * 512:(fb + 1) * 512].rearrange(
                        "(c p) f -> p c f", p=128)), "w0", 16)
                L.inc(gd.dma_start(
                    out=f2slab.rearrange("p (c f) -> p c f", f=512),
                    in_=W["fc2_w"][:, fb * 512:(fb + 1) * 512].rearrange(
                        "(c p) f -> p c f", p=128)), "w0", 16)
                wl = L.inc(gd.dma_start(
                    out=woslab.rearrange("p (c f) -> p c f", f=512),
                    in_=W["Wo"][:, fb * 512:(fb + 1) * 512].rearrange(
                        "(c p) f -> p c f", p=128)), "w0", 16)
                bld = L.inc(gd.dma_start(
                    out=rowf[:], in_=VEC["bo"][0:1, fb * 512:(fb + 1) * 512]),
                    "gb", 16)
                L.wait(dv, "dv", "gb", bld)
                brc = L.inc(dv.tensor_copy(rowfr[:], rowf[:]), "dv")
                for tt in range(NT_OWN):
                    p = tt % 2
                    lxs = f"lx{p}"
                    L.wait(sy, "sy", "pe", g_pe.get((fb, tt - 2),
                                                    g_pe.get((fb - 1, NT_OWN - 1),
                                                             g_base_pe)))
                    L.inc(sy.dma_start(
                        out=u_tt[p].rearrange("p (c j) -> p c j", j=128),
                        in_=u_sc[tt].rearrange("c p j -> p c j")), lxs, 16)
                    L.inc(sy.dma_start(
                        out=moe_tt[p].rearrange("p (c j) -> p c j", j=128),
                        in_=moe_sc[tt].rearrange("c p j -> p c j")), lxs, 16)
                    l3 = L.inc(sy.dma_start(
                        out=ot_tt[p].rearrange("p (c j) -> p c j", j=128),
                        in_=ot_sc[tt].rearrange("c p j -> p c j")), lxs, 16)
                    zbank = 4 + p
                    abank = 6 + p
                    L.wait(pe, "pe", lxs, l3)
                    L.wait(pe, "pe", "w0", wl)
                    L.wait(pe, "pe", "dv", max(g_ev.get((fb, tt - 2), 0), brc))
                    for ch in range(IC):
                        pe.matmul(ps[zbank][:],
                                  u_tt[p][:, ch * 128:(ch + 1) * 128],
                                  dslab[:, ch * 512:(ch + 1) * 512],
                                  start=(ch == 0), stop=False)
                    for ch in range(LC):
                        mm = pe.matmul(ps[zbank][:],
                                       moe_tt[p][:, ch * 128:(ch + 1) * 128],
                                       f2slab[:, ch * 512:(ch + 1) * 512],
                                       start=False, stop=(ch == LC - 1))
                    zend = L.inc(mm, "pe")
                    for ch in range(NH):
                        pe.matmul(ps[abank][:],
                                  ot_tt[p][:, ch * 128:(ch + 1) * 128],
                                  woslab[:, ch * 512:(ch + 1) * 512],
                                  start=(ch == 0), stop=False)
                    mm = pe.matmul(ps[abank][:], ones_fr[:], rowfr[:],
                                   start=False, stop=True)
                    aend = L.inc(mm, "pe")
                    g_pe[(fb, tt)] = aend
                    L.wait(dv, "dv", "pe", aend)
                    L.wait(dv, "dv", f"sx{p}", g_st[p])
                    dv.tensor_copy(fstage[p], ps[zbank][:])
                    ev = L.inc(dv.tensor_copy(fstage[2 + p], ps[abank][:]), "dv")
                    g_ev[(fb, tt)] = ev
                    L.wait(sy, "sy", "dv", ev)
                    L.inc(sy.dma_start(out=z_sc[tt * 128:(tt + 1) * 128,
                                                fb * 512:(fb + 1) * 512],
                                       in_=fstage[p]), f"sx{p}", 16)
                    L.inc(sy.dma_start(out=at_sc[tt * 128:(tt + 1) * 128,
                                                 fb * 512:(fb + 1) * 512],
                                       in_=fstage[2 + p]), f"sx{p}", 16)
                    g_st[p] = L.cnt[f"sx{p}"]

            if DEBUG_STOP == "G":
                return finish()
            zrow = [b0f(0, 4096), b0f(4096, 4096)]
            arow = [b0f(8192, 4096), b0f(12288, 4096)]
            xrow = [b0f(16384, 4096), b0f(20480, 4096)]
            sqrow = b0f(24576, 4096)
            lnrow = b0f(28672, 4096)
            gbcH = b1f(0, 4096)
            gbcH2 = b1f(4096, 4096)
            barrier("sy", "dv", "ac")
            gl1 = L.inc(sy.dma_start(out=gbcH,
                                     in_=VEC["ln_g"].partition_broadcast(128)),
                        "tx0", 16)
            gl2 = L.inc(sy.dma_start(out=gbcH2,
                                     in_=VEC["ln_b"].partition_broadcast(128)),
                        "tx0", 16)
            h_out = {}
            h_st = [0, 0]
            for tt in range(NT_OWN):
                p = tt % 2
                lxs = f"lx{p}"
                if tt >= 2:
                    L.wait(sy, "sy", "dv", h_out[tt - 2])
                    L.wait(sy, "sy", "so", h_st[0])
                L.inc(sy.dma_start(out=zrow[p],
                                   in_=z_sc[tt * 128:(tt + 1) * 128, :]), lxs, 16)
                L.inc(sy.dma_start(out=arow[p],
                                   in_=at_sc[tt * 128:(tt + 1) * 128, :]), lxs, 16)
                l3 = L.inc(sy.dma_start(out=xrow[p],
                                        in_=x_own[tt * 128:(tt + 1) * 128, :]),
                           lxs, 16)
                stats, sd = ln_stats(zrow[p], sqrow, lxs, l3,
                                     h_out.get(tt - 1, 0))
                L.wait(ac, "ac", "dv", stats)
                L.wait(ac, "ac", "ac", sd)
                ln = L.inc(ac.activation(lnrow, zrow[p], AF.Identity,
                                         bias=S["nbias"][:], scale=S["rstd"][:]),
                           "ac")
                L.wait(dv, "dv", "ac", ln)
                L.wait(dv, "dv", "tx0", gl2)
                w1 = L.inc(dv.tensor_mul(sqrow, lnrow, gbcH), "dv")
                L.wait(dv, "dv", "dv", w1)
                w2 = L.inc(dv.tensor_add(lnrow, sqrow, gbcH2), "dv")
                L.wait(dv, "dv", "dv", w2)
                w3 = L.inc(dv.tensor_add(sqrow, lnrow, arow[p]), "dv")
                L.wait(dv, "dv", "dv", w3)
                fo = L.inc(dv.tensor_add(zrow[p], sqrow, xrow[p]), "dv")
                h_out[tt] = fo
                L.wait(sy, "sy", "dv", fo)
                L.inc(sy.dma_start(out=out_hbm[tt * 128:(tt + 1) * 128, :],
                                   in_=zrow[p]), "so", 16)
                h_st[0] = L.cnt["so"]
            return finish()

    return nc


def kernel(**inputs):
    nc = build()
    in_maps = []
    x = np.ascontiguousarray(inputs["L0_attn_hidden_states"])
    row = lambda a: np.ascontiguousarray(np.asarray(a, dtype=np.float32).reshape(1, -1))
    base = {
        "Wq": inputs["L0_attn_Wq"], "Wk": inputs["L0_attn_Wk"],
        "Wv": inputs["L0_attn_Wv"], "Wo": inputs["L0_attn_Wo"],
        "bq": row(inputs["L0_attn_bq"]), "bk": row(inputs["L0_attn_bk"]),
        "bv": row(inputs["L0_attn_bv"]), "bo": row(inputs["L0_attn_bo"]),
        "gq": row(inputs["L0_attn_gq"]), "gk": row(inputs["L0_attn_gk"]),
        "gate_w": inputs["L0_ffn_gate_w"], "up_w": inputs["L0_ffn_up_w"],
        "down_w": inputs["L0_ffn_down_w"], "fc1_w": inputs["L0_ffn_fc1_w"],
        "fc2_w": inputs["L0_ffn_fc2_w"], "expert_w": inputs["L0_ffn_expert_w"],
        "ln_g": row(inputs["L0_ffn_ln_gamma"]), "ln_b": row(inputs["L0_ffn_ln_beta"]),
    }
    base = {k: np.ascontiguousarray(np.asarray(v, dtype=np.float32))
            for k, v in base.items()}
    for c in range(8):
        b, half = c // 2, c % 2
        m = dict(base)
        m["x_own"] = np.ascontiguousarray(x[b, half * TOK:(half + 1) * TOK, :])
        m["x_oth"] = np.ascontiguousarray(x[b, (1 - half) * TOK:(2 - half) * TOK, :])
        in_maps.append(m)
    res = run_bass_kernel_spmd(nc, in_maps, core_ids=list(range(8)))
    outs = [res.results[c]["out"] for c in range(8)]
    full = np.empty((B, SEQ, D), dtype=np.float32)
    for c in range(8):
        b, half = c // 2, c % 2
        full[b, half * TOK:(half + 1) * TOK, :] = outs[c]
    return full


# version 03
# speedup vs baseline: 1.1671x; 1.1671x over previous
import contextlib
import numpy as np
import concourse.bass as bass
import concourse.mybir as mybir
from concourse.bass_utils import run_bass_kernel_spmd

F32 = mybir.dt.float32
F32R = mybir.dt.float32r
BF16 = mybir.dt.bfloat16
AF = mybir.ActivationFunctionType
AX = mybir.AxisListType
OP = mybir.AluOpType

B, SEQ, D = 4, 4096, 4096
NH, HD = 32, 128
INT, LAT, NE = 8192, 2048, 8
TOK = 2048
DC = D // 128
IC = INT // 128
LC = LAT // 128
NT_OWN = TOK // 128
NT_SEQ = SEQ // 128
LN_EPS = 1e-5
RMS_EPS = 1e-6
FLN_EPS = 1e-5
ISQ = float(1.0 / np.sqrt(HD))

DEBUG_STOP = None
DEBUG_TAPS = ()

DMA_SEMS = ("lx0", "lx1", "sx0", "sx1", "sb0", "sb1", "sb2", "sb3",
            "tx0", "tx1", "w0", "w1", "gb", "g0", "g1", "g2", "g3", "so")
ALL_SEMS = DMA_SEMS + ("pe", "dv", "ac", "gc")


class Led:
    def __init__(self, nc, ctx):
        self.nc, self.ctx = nc, ctx
        self.sem, self.cnt, self.hw = {}, {}, {}

    def mk(self, *names):
        for n in names:
            self.sem[n] = self.ctx.enter_context(self.nc.semaphore(n))
            self.cnt[n] = 0

    def inc(self, instr, s, n=1):
        self.cnt[s] += n
        instr.then_inc(self.sem[s], n)
        return self.cnt[s]

    def wait(self, eng, ekey, s, val=None):
        if val is None:
            val = self.cnt[s]
        if val <= 0:
            return
        assert val <= self.cnt[s], f"wait {s}>={val} but only {self.cnt[s]} planned"
        if self.hw.get((ekey, s), 0) < val:
            eng.wait_ge(self.sem[s], val)
            self.hw[(ekey, s)] = val


def build():
    nc = bass.Bass("TRN2", target_bir_lowering=False)

    x_own = nc.dram_tensor("x_own", [TOK, D], F32, kind="ExternalInput").ap()
    x_oth = nc.dram_tensor("x_oth", [TOK, D], F32, kind="ExternalInput").ap()
    W = {}
    for nm, shp in (("Wq", [D, D]), ("Wk", [D, D]), ("Wv", [D, D]), ("Wo", [D, D]),
                    ("gate_w", [D, NE]), ("up_w", [D, INT]), ("down_w", [INT, D]),
                    ("fc1_w", [D, LAT]), ("fc2_w", [LAT, D]), ("expert_w", [LAT, LAT])):
        W[nm] = nc.dram_tensor(nm, shp, F32, kind="ExternalInput").ap()
    VEC = {}
    for nm in ("bq", "bk", "bv", "bo", "gq", "gk", "ln_g", "ln_b"):
        VEC[nm] = nc.dram_tensor(nm, [1, D], F32, kind="ExternalInput").ap()
    out_hbm = nc.dram_tensor("out", [TOK, D], F32, kind="ExternalOutput").ap()

    xn_sc = nc.dram_tensor("xn_sc", [SEQ, D], BF16).ap()
    xnT_sc = nc.dram_tensor("xnT_sc", [DC, 128, SEQ], BF16).ap()
    qr_sc = nc.dram_tensor("qr_sc", [TOK, D], BF16).ap()
    kr_sc = nc.dram_tensor("kr_sc", [SEQ, D], BF16).ap()
    qn_sc = nc.dram_tensor("qn_sc", [TOK, D], BF16).ap()
    kn_sc = nc.dram_tensor("kn_sc", [SEQ, D], BF16).ap()
    v_sc = nc.dram_tensor("v_sc", [SEQ, D], BF16).ap()
    ot_sc = nc.dram_tensor("ot_sc", [NT_OWN, NH, 128, 128], BF16).ap()
    u_sc = nc.dram_tensor("u_sc", [NT_OWN, IC, 128, 128], BF16).ap()
    xl_sc = nc.dram_tensor("xl_sc", [LAT, TOK], BF16).ap()
    moe_sc = nc.dram_tensor("moe_sc", [NT_OWN, LC, 128, 128], BF16).ap()
    rt_sc = nc.dram_tensor("rt_sc", [1, TOK], F32).ap()
    z_sc = nc.dram_tensor("z_sc", [TOK, D], F32).ap()
    at_sc = nc.dram_tensor("at_sc", [TOK, D], F32).ap()
    scratch = {"xn_sc": xn_sc, "qr_sc": qr_sc, "kr_sc": kr_sc, "qn_sc": qn_sc,
               "kn_sc": kn_sc, "v_sc": v_sc, "xl_sc": xl_sc, "rt_sc": rt_sc,
               "z_sc": z_sc, "at_sc": at_sc,
               "ot_sc": ot_sc, "u_sc": u_sc, "moe_sc": moe_sc}
    dbg = {nm: nc.dram_tensor("dbg_" + nm, list(scratch[nm].shape), scratch[nm].dtype,
                              kind="ExternalOutput").ap() for nm in DEBUG_TAPS}

    ctx = contextlib.ExitStack()
    with ctx:
        L = Led(nc, ctx)
        L.mk(*ALL_SEMS)

        def sbuf(name, shape, dt):
            return ctx.enter_context(nc.sbuf_tensor(name, shape, dt))

        b0 = sbuf("b0", [128, 65536], BF16)
        b1 = sbuf("b1", [128, 32768], BF16)
        stage = [sbuf(f"stage{i}", [128, 512], BF16) for i in range(4)]
        ones_fr = sbuf("ones_fr", [1, 128], F32R)
        ones_f = sbuf("ones_f", [1, 128], F32)
        onescol_bf = sbuf("onescol_bf", [128, 1], BF16)
        ones_bfr = sbuf("ones_bfr", [1, 128], BF16)
        ident_f = sbuf("ident_f", [128, 128], F32)
        rowf = sbuf("rowf", [1, 512], F32)
        rowfr = sbuf("rowfr", [1, 512], F32R)
        negm_row = sbuf("negm_row", [1, 512], F32R)
        rs_row = sbuf("rs_row", [1, 512], F32R)
        m8 = sbuf("m8", [128, 8], F32)
        eps5 = sbuf("eps5", [128, 1], F32)
        eps6 = sbuf("eps6", [128, 1], F32)
        lnisq = sbuf("lnisq", [128, 1], F32)
        zero1 = sbuf("zero1", [128, 1], F32)
        lg8 = sbuf("lg8", [128, 8], F32)
        g2 = sbuf("g2", [128, 8], F32)
        small = {nm: sbuf(nm, [128, 1], F32) for nm in
                 ("s1", "s2", "mu", "msq", "var", "sd", "rstd", "nbias",
                  "m1", "m2", "rt1")}
        mparts = sbuf("mparts", [128, 8], F32)
        negm = sbuf("negm", [128, 1], F32)
        S = small

        def b0f(lo, n):
            return b0[:].bitcast(F32)[:, lo:lo + n]

        def b0b(lo, n):
            return b0[:, lo:lo + n]

        def b1f(lo, n):
            return b1[:].bitcast(F32)[:, lo:lo + n]

        def b1b(lo, n):
            return b1[:, lo:lo + n]

        ps = [ctx.enter_context(nc.psum_tensor(f"ps{i}", [128, 512], F32))
              for i in range(8)]

        block = ctx.enter_context(nc.Block())

        @block.sync
        def _(sy):
            gd = nc.gpsimd
            pe = nc.tensor
            dv = nc.vector
            ac = nc.scalar
            ENG = {"pe": pe, "dv": dv, "ac": ac, "gd": gd, "sy": sy}

            def barrier(*ekeys):
                for ek in ekeys:
                    e = ENG[ek]
                    for s in ALL_SEMS:
                        if s == "so":
                            continue
                        L.wait(e, ek, s, L.cnt[s])

            gd.memset(ones_f[:], 1.0)
            gd.memset(onescol_bf[:], 1.0)
            v0 = L.inc(gd.memset(ident_f[:], 0.0), "gc")
            L.wait(gd, "gd", "gc", v0)
            ins = gd.affine_select(out=ident_f[:], in_=ident_f[:],
                                   compare_op=OP.not_equal, fill=1.0, base=0,
                                   pattern=[[-1, 128]], channel_multiplier=1)
            L.inc(ins, "gc")
            dv.memset(ones_bfr[:], 1.0)
            dv.memset(m8[:], 3.0e38)
            dv.memset(eps5[:], LN_EPS)
            dv.memset(eps6[:], RMS_EPS)
            dv.memset(lnisq[:], float(np.log(ISQ)))
            dv.memset(zero1[:], 0.0)
            L.wait(dv, "dv", "gc", 1)
            L.inc(dv.tensor_copy(ones_fr[:], ones_f[:]), "dv")
            init_dv = L.cnt["dv"]

            def finish():
                for s in ALL_SEMS:
                    if s != "so":
                        L.wait(sy, "sy", s, L.cnt[s])
                for nm, dap in dbg.items():
                    L.inc(sy.dma_start(out=dap, in_=scratch[nm]), "so", 16)
                L.wait(sy, "sy", "so", L.cnt["so"])

            def ln_stats(rowx, rowsq, ld_sem, ld_val, sq_guard=0):
                L.wait(dv, "dv", ld_sem, ld_val)
                dv.reduce_sum(S["s1"][:], rowx, axis=AX.X)
                L.wait(ac, "ac", ld_sem, ld_val)
                if sq_guard:
                    L.wait(ac, "ac", "dv", sq_guard)
                sq = L.inc(ac.activation(rowsq, rowx, AF.Square), "ac")
                L.wait(dv, "dv", "ac", sq)
                sed = L.inc(dv.reduce_sum(S["s2"][:], rowsq, axis=AX.X), "dv")
                L.wait(dv, "dv", "dv", sed)
                dv.tensor_scalar_mul(S["mu"][:], S["s1"][:], 1.0 / D)
                v1 = L.inc(dv.tensor_scalar_mul(S["s2"][:], S["s2"][:], 1.0 / D), "dv")
                L.wait(dv, "dv", "dv", v1)
                v2 = L.inc(dv.tensor_mul(S["msq"][:], S["mu"][:], S["mu"][:]), "dv")
                L.wait(dv, "dv", "dv", v2)
                st = L.inc(dv.tensor_sub(S["var"][:], S["s2"][:], S["msq"][:]), "dv")
                L.wait(ac, "ac", "dv", st)
                L.wait(ac, "ac", "dv", init_dv)
                lnv = L.inc(ac.activation(S["sd"][:], S["var"][:], AF.Ln,
                                          bias=eps5[:]), "ac")
                L.wait(ac, "ac", "ac", lnv)
                sd = L.inc(ac.activation(S["rstd"][:], S["sd"][:], AF.Exp,
                                         scale=-0.5, bias=zero1[:]), "ac")
                L.wait(dv, "dv", "ac", sd)
                v3 = L.inc(dv.tensor_mul(S["nbias"][:], S["mu"][:], S["rstd"][:]), "dv")
                L.wait(dv, "dv", "dv", v3)
                stats = L.inc(dv.tensor_scalar_mul(S["nbias"][:], S["nbias"][:],
                                                   -1.0), "dv")
                return stats, sd

            rax = [b0f(0, 4096), b0f(4096, 4096)]
            rsq = [b0f(8192, 4096), b0f(12288, 4096)]
            rno = [b0b(32768, 4096), b0b(36864, 4096)]
            rax_g = [0, 0]
            rsq_g = [0, 0]
            rno_g = [0, 0]
            for i in range(NT_SEQ):
                p = i % 2
                src = x_own if i < NT_OWN else x_oth
                r0 = (i % NT_OWN) * 128
                lxs = f"lx{p}"
                L.wait(sy, "sy", "ac", rax_g[p])
                ld = L.inc(sy.dma_start(out=rax[p], in_=src[r0:r0 + 128, :]), lxs, 16)
                stats, sd = ln_stats(rax[p], rsq[p], lxs, ld, rsq_g[p])
                rsq_g[p] = L.cnt["dv"]
                L.wait(ac, "ac", "dv", stats)
                L.wait(ac, "ac", "ac", sd)
                L.wait(ac, "ac", f"sx{p}", rno_g[p])
                nm_ = L.inc(ac.activation(rno[p], rax[p], AF.Identity,
                                          bias=S["nbias"][:], scale=S["rstd"][:]),
                            "ac")
                rax_g[p] = nm_
                L.wait(sy, "sy", "ac", nm_)
                rno_g[p] = L.inc(
                    sy.dma_start(out=xn_sc[i * 128:(i + 1) * 128, :], in_=rno[p]),
                    f"sx{p}", 16)

            if DEBUG_STOP == "A":
                return finish()
            xnT = b0[:]
            barrier("sy", "gd", "pe", "dv", "ac")
            qkv_grp = 0
            grp_ev = {}
            b_store = {}
            jobs_by_half = {0: [("Wk", "bk"), ("Wv", "bv"), ("Wq", "bq")],
                            1: [("Wk", "bk"), ("Wv", "bv")]}
            jobidx = 0
            slab_use = {}
            last_rowfr_pe = 0
            for half in (0, 1):
                if half == 1:
                    L.wait(sy, "sy", "pe", L.cnt["pe"])
                    L.wait(sy, "sy", "sx0", L.cnt["sx0"])
                for ch in range(DC):
                    L.inc(sy.dma_start(
                        out=xnT[:, ch * TOK:(ch + 1) * TOK],
                        in_=xn_sc[half * TOK:(half + 1) * TOK,
                                  ch * 128:(ch + 1) * 128],
                        transpose=True), "tx0", 16)
                xnT_ld = L.cnt["tx0"]
                if half == 0:
                    L.wait(sy, "sy", "tx0", xnT_ld)
                    for ch in range(DC):
                        L.inc(sy.dma_start(
                            out=xnT_sc[ch, :, 0:TOK],
                            in_=xnT[:, ch * TOK:(ch + 1) * TOK]), "sx0", 16)
                for wnm, bnm in jobs_by_half[half]:
                    for ib in range(8):
                        wsem = f"w{jobidx % 2}"
                        if jobidx >= 2:
                            L.wait(gd, "gd", "pe", slab_use[jobidx - 2])
                        wsl = b1b(0, 16384) if jobidx % 2 == 0 else b1b(16384, 16384)
                        wld = L.inc(gd.dma_start(
                            out=wsl.rearrange("p (c f) -> p c f", f=512),
                            in_=W[wnm][:, ib * 512:(ib + 1) * 512].rearrange(
                                "(c p) f -> p c f", p=128)), wsem, 16)
                        L.wait(gd, "gd", "pe", last_rowfr_pe)
                        bld = L.inc(gd.dma_start(
                            out=rowf[:],
                            in_=VEC[bnm][0:1, ib * 512:(ib + 1) * 512]), "gb", 16)
                        L.wait(dv, "dv", "gb", bld)
                        brc = L.inc(dv.tensor_copy(rowfr[:], rowf[:]), "dv")
                        for tt in range(NT_OWN):
                            g = qkv_grp
                            bank = g % 4
                            L.wait(pe, "pe", "dv", max(grp_ev.get(g - 4, 0), brc))
                            L.wait(pe, "pe", wsem, wld)
                            L.wait(pe, "pe", "tx0", xnT_ld)
                            for ch in range(DC):
                                pe.matmul(ps[bank][:],
                                          xnT[:, ch * TOK + tt * 128:
                                              ch * TOK + (tt + 1) * 128],
                                          wsl[:, ch * 512:(ch + 1) * 512],
                                          start=(ch == 0), stop=False)
                            mm = pe.matmul(ps[bank][:], ones_fr[:], rowfr[:],
                                           start=False, stop=True)
                            gend = L.inc(mm, "pe")
                            L.wait(dv, "dv", "pe", gend)
                            L.wait(dv, "dv", f"sb{bank}", b_store.get(bank, 0))
                            ev = L.inc(dv.tensor_copy(stage[bank][:], ps[bank][:]),
                                       "dv")
                            grp_ev[g] = ev
                            L.wait(sy, "sy", "dv", ev)
                            trow = half * TOK + tt * 128
                            if wnm == "Wq":
                                dst = qr_sc[tt * 128:tt * 128 + 128,
                                            ib * 512:(ib + 1) * 512]
                            elif wnm == "Wk":
                                dst = kr_sc[trow:trow + 128, ib * 512:(ib + 1) * 512]
                            else:
                                dst = v_sc[trow:trow + 128, ib * 512:(ib + 1) * 512]
                            b_store[bank] = L.inc(
                                sy.dma_start(out=dst, in_=stage[bank][:]),
                                f"sb{bank}", 16)
                            qkv_grp += 1
                        slab_use[jobidx] = L.cnt["pe"]
                        last_rowfr_pe = L.cnt["pe"]
                        jobidx += 1

            if DEBUG_STOP == "B":
                return finish()
            rcin = [b0b(0, 4096), b0b(4096, 4096)]
            rtmp = b0f(4096, 4096)
            rsqc = [b0f(8192, 4096), b0f(12288, 4096)]
            rnoc = [b0b(32768, 4096), b0b(36864, 4096)]
            gbc = b1f(0, 4096)
            barrier("sy", "dv", "ac")
            c_jobs = [("q", qr_sc, qn_sc, NT_OWN, "gq", True),
                      ("k", kr_sc, kn_sc, NT_SEQ, "gk", False)]
            rc_g = [0, 0]
            rn_g = [0, 0]
            rq_g = [0, 0]
            ev_g = [0, 0]
            for jname, src_sc, dst_sc, ntl, gnm, isq in c_jobs:
                L.wait(sy, "sy", "dv", L.cnt["dv"])
                gld = L.inc(sy.dma_start(out=gbc,
                                         in_=VEC[gnm].partition_broadcast(128)),
                            "tx0", 16)
                for i in range(ntl):
                    p = i % 2
                    lxs = f"lx{p}"
                    L.wait(sy, "sy", "ac", rc_g[p])
                    ld = L.inc(sy.dma_start(out=rcin[p],
                                            in_=src_sc[i * 128:(i + 1) * 128, :]),
                               lxs, 16)
                    L.wait(ac, "ac", lxs, ld)
                    if rq_g[p]:
                        L.wait(ac, "ac", "dv", rq_g[p])
                    sq = L.inc(ac.activation(rsqc[p], rcin[p], AF.Square), "ac")
                    L.wait(dv, "dv", "ac", sq)
                    red = L.inc(dv.reduce_sum(S["s2"][:], rsqc[p], axis=AX.X), "dv")
                    rq_g[p] = red
                    L.wait(dv, "dv", "dv", red)
                    va = L.inc(dv.tensor_scalar_mul(S["var"][:], S["s2"][:], 1.0 / D),
                               "dv")
                    L.wait(ac, "ac", "dv", va)
                    lnv = L.inc(ac.activation(S["sd"][:], S["var"][:], AF.Ln,
                                              bias=eps6[:]), "ac")
                    L.wait(ac, "ac", "ac", lnv)
                    sd = L.inc(ac.activation(S["rstd"][:], S["sd"][:], AF.Exp,
                                             scale=-0.5,
                                             bias=(lnisq[:] if isq else zero1[:])),
                               "ac")
                    L.wait(ac, "ac", "ac", sd)
                    t_ = L.inc(ac.activation(rtmp, rcin[p], AF.Identity,
                                             scale=S["rstd"][:]), "ac")
                    rc_g[p] = t_
                    L.wait(dv, "dv", "ac", t_)
                    L.wait(dv, "dv", "tx0", gld)
                    L.wait(dv, "dv", f"sx{p}", rn_g[p])
                    ev = L.inc(dv.tensor_mul(rnoc[p], rtmp, gbc), "dv")
                    ev_g[p] = ev
                    L.wait(sy, "sy", "dv", ev)
                    rn_g[p] = L.inc(
                        sy.dma_start(out=dst_sc[i * 128:(i + 1) * 128, :],
                                     in_=rnoc[p]), f"sx{p}", 16)

            if DEBUG_STOP == "C":
                return finish()
            KT = [b0b(0, 4096), b0b(4096, 4096)]
            QT = [b0b(8192, 2048), b0b(10240, 2048)]
            VS = [b0b(12288, 4096), b0b(16384, 4096)]
            PT = [b0b(20480 + 512 * i, 512) for i in range(4)]
            rb_sb = b0f(14336, 512)
            otst = [b0b(30720, 512), b0b(31232, 512)]

            def nrow(hh, qb):
                off = 32768 + ((hh % 2) * 4 + qb) * 512
                return b0[0:1, off:off + 512]

            barrier("sy", "pe", "dv", "ac")
            d_head_pe = {-1: L.cnt["pe"], -2: L.cnt["pe"]}
            d_loads = {}
            bank_free = {b: ("dv", 0) for b in range(8)}
            pt_free = [0, 0, 0, 0]
            nr_ready = {}
            nr_user = {}
            ot_g = [0, 0]

            def pe_bank(b):
                s, v = bank_free[b]
                L.wait(pe, "pe", s, v)

            def d_load_head(h):
                if h >= NH:
                    return
                hp = h % 2
                lxs = f"lx{hp}"
                L.wait(sy, "sy", "pe", d_head_pe[h - 2])
                L.inc(sy.dma_start(out=KT[hp][:, :],
                                   in_=kn_sc[:, h * 128:(h + 1) * 128],
                                   transpose=True), lxs, 16)
                L.inc(sy.dma_start(out=QT[hp][:, :],
                                   in_=qn_sc[:, h * 128:(h + 1) * 128],
                                   transpose=True), lxs, 16)
                for part in range(4):
                    L.inc(sy.dma_start(
                        out=VS[hp][:, part * 1024:(part + 1) * 1024].rearrange(
                            "p (kt j) -> p kt j", j=128),
                        in_=v_sc[part * 1024:(part + 1) * 1024,
                                 h * 128:(h + 1) * 128].rearrange(
                            "(kt p) j -> p kt j", p=128)), lxs, 16)
                d_loads[h] = L.cnt[lxs]

            d_load_head(0)
            d_load_head(1)

            def emit_p1_slot(h, slot):
                hp = h % 2
                qb, r = divmod(slot, 32)
                qt, kb = divmod(r, 8)
                qcol = (qb * 4 + qt) * 128
                bank = slot % 2
                pe_bank(bank)
                L.wait(pe, "pe", f"lx{hp}", d_loads[h])
                mm = pe.matmul(ps[bank][:], QT[hp][:, qcol:qcol + 128],
                               KT[hp][:, kb * 512:(kb + 1) * 512],
                               start=True, stop=True)
                pend = L.inc(mm, "pe")
                L.wait(dv, "dv", "pe", pend)
                rr = L.inc(dv.reduce_max(mparts[:, kb:kb + 1], ps[bank][:],
                                         axis=AX.X), "dv")
                bank_free[bank] = ("dv", rr)
                if kb == 7:
                    L.wait(dv, "dv", "dv", rr)
                    nm_ = L.inc(dv.reduce_max(negm[:], mparts[:], axis=AX.X,
                                              negate=True), "dv")
                    L.wait(dv, "dv", "dv", nm_)
                    L.wait(pe, "pe", "dv", nm_)
                    L.wait(pe, "pe", "gc", 2)
                    pe_bank(7)
                    tr = L.inc(pe.transpose(ps[7][0:1, 0:128], negm[:],
                                            ident_f[:]), "pe")
                    L.wait(dv, "dv", "pe", tr)
                    L.wait(dv, "dv", "pe", nr_user.get((h - 2, qb), 0))
                    cp = L.inc(dv.tensor_copy(
                        nrow(h, qb)[:, qt * 128:(qt + 1) * 128],
                        ps[7][0:1, 0:128]), "dv")
                    bank_free[7] = ("dv", cp)
                    if qt == 3:
                        nr_ready[(h, qb)] = cp

            def emit_p2_kt(h, qb, kt):
                hp = h % 2
                qcol = qb * 512
                bank = 2 + (kt % 3)
                pe_bank(bank)
                L.wait(pe, "pe", "dv", nr_ready[(h, qb)])
                pe.matmul(ps[bank][:], KT[hp][:, kt * 128:(kt + 1) * 128],
                          QT[hp][:, qcol:qcol + 512], start=True, stop=False)
                mm = pe.matmul(ps[bank][:], ones_bfr[:], nrow(h, qb),
                               start=False, stop=True)
                gend = L.inc(mm, "pe")
                nr_user[(h, qb)] = gend
                L.wait(ac, "ac", "pe", max(gend, pt_free[kt % 4]))
                ex = L.inc(ac.activation(PT[kt % 4][:], ps[bank][:], AF.Exp), "ac")
                bank_free[bank] = ("ac", ex)
                return ex

            def emit_p2_sums_av(h, qb, kt, ex):
                hp = h % 2
                L.wait(pe, "pe", "ac", ex)
                if kt == 0:
                    pe_bank(5)
                    pe_bank(6)
                pe.matmul(ps[5][0:1, :], onescol_bf[:], PT[kt % 4][:],
                          start=(kt == 0), stop=(kt == NT_SEQ - 1))
                mm = pe.matmul(ps[6][:], VS[hp][:, kt * 128:(kt + 1) * 128],
                               PT[kt % 4][:],
                               start=(kt == 0), stop=(kt == NT_SEQ - 1))
                pt_free[kt % 4] = L.inc(mm, "pe")

            def emit_p2_tail(h, qb):
                hp = h % 2
                avend = L.cnt["pe"]
                L.wait(dv, "dv", "pe", avend)
                rc0 = L.inc(dv.reciprocal(rowf[:], ps[5][0:1, :]), "dv")
                bank_free[5] = ("dv", rc0)
                L.wait(dv, "dv", "dv", rc0)
                rc = L.inc(dv.tensor_copy(rs_row[:], rowf[:]), "dv")
                L.wait(dv, "dv", "dv", rc)
                L.wait(pe, "pe", "dv", rc)
                pe_bank(7)
                bc = L.inc(pe.matmul(ps[7][:], ones_fr[:], rs_row[:],
                                     start=True, stop=True), "pe")
                L.wait(dv, "dv", "pe", bc)
                vrb = L.inc(dv.tensor_copy(rb_sb, ps[7][:]), "dv")
                bank_free[7] = ("dv", vrb)
                L.wait(dv, "dv", "dv", vrb)
                L.wait(dv, "dv", f"g{qb % 2}", ot_g[qb % 2])
                ot = L.inc(dv.tensor_mul(otst[qb % 2][:], ps[6][:], rb_sb), "dv")
                bank_free[6] = ("dv", ot)
                L.wait(gd, "gd", "dv", ot)
                for qt in range(4):
                    ins2 = gd.dma_start(
                        out=ot_sc[qb * 4 + qt, h, :, :],
                        in_=otst[qb % 2][:, qt * 128:(qt + 1) * 128])
                    L.inc(ins2, f"g{qb % 2}", 16)
                ot_g[qb % 2] = L.cnt[f"g{qb % 2}"]

            for h in range(NH + 1):
                p1_slots = list(range(128)) if h < NH else []
                p1_i = 0
                if h == 0:
                    for slot in p1_slots:
                        emit_p1_slot(h, slot)
                    continue
                h2 = h - 1
                for qb in range(4):
                    prev_ex = None
                    for kt in range(NT_SEQ):
                        ex = emit_p2_kt(h2, qb, kt)
                        if kt > 0:
                            emit_p2_sums_av(h2, qb, kt - 1, prev_ex)
                        prev_ex = ex
                        if p1_i < len(p1_slots):
                            emit_p1_slot(h, p1_slots[p1_i])
                            p1_i += 1
                    emit_p2_sums_av(h2, qb, NT_SEQ - 1, prev_ex)
                    emit_p2_tail(h2, qb)
                while p1_i < len(p1_slots):
                    emit_p1_slot(h, p1_slots[p1_i])
                    p1_i += 1
                d_head_pe[h2] = L.cnt["pe"]
                d_load_head(h + 1)

            if DEBUG_STOP == "D":
                return finish()
            exT = [b0b(0, 16384), b0b(16384, 16384)]
            wsl_e = [b1b(0, 16384), b1b(16384, 16384)]
            gslab = b0b(32768, 256)
            rtrow = b0[0:1, 36864:40960].bitcast(F32)
            fstage = [b0f(20480 + 512 * i, 512) for i in range(4)]
            barrier("sy", "gd", "pe", "dv", "ac")
            e_base_pe = L.cnt["pe"]
            gld = L.inc(gd.dma_start(
                out=gslab.rearrange("p (c f) -> p c f", f=NE),
                in_=W["gate_w"].rearrange("(c p) f -> p c f", p=128)), "gb", 16)
            spans = [("up", "up_w", i) for i in range(INT // 512)] + \
                    [("fc1", "fc1_w", i) for i in range(LAT // 512)]
            nsteps = len(spans) * 4
            ex_ld = {}
            step_pe = {}
            wld_e = {}
            e_ev = {}
            e_sq = {}
            e_store = {}
            rt_done = 0

            def e_load(step):
                if step >= nsteps:
                    return
                tb = step % 4
                txs = f"tx{step % 2}"
                L.wait(sy, "sy", "pe", step_pe.get(step - 2, e_base_pe))
                L.inc(sy.dma_start(
                    out=exT[step % 2][:].rearrange("p (c f) -> p c f", f=512),
                    in_=xnT_sc[:, :, tb * 512:(tb + 1) * 512].rearrange(
                        "c p f -> p c f")), txs, 16)
                ex_ld[step] = L.cnt[txs]

            def e_wload(sp):
                if sp >= len(spans):
                    return
                kind, wname, idx = spans[sp]
                wsem = f"w{sp % 2}"
                L.wait(gd, "gd", "pe", step_pe.get((sp - 2) * 4 + 3, e_base_pe))
                wld_e[sp] = L.inc(gd.dma_start(
                    out=wsl_e[sp % 2].rearrange("p (c f) -> p c f", f=512),
                    in_=W[wname][:, idx * 512:(idx + 1) * 512].rearrange(
                        "(c p) f -> p c f", p=128)), wsem, 16)

            e_load(0)
            e_load(1)
            e_wload(0)
            e_wload(1)
            e_grp = 0
            for sp, (kind, wname, idx) in enumerate(spans):
                wsem = f"w{sp % 2}"
                for tb in range(4):
                    step = sp * 4 + tb
                    txs = f"tx{step % 2}"
                    for it in range(4):
                        g = e_grp
                        bank = g % 4
                        L.wait(pe, "pe", "dv", e_ev.get(g - 4, 0))
                        L.wait(pe, "pe", wsem, wld_e[sp])
                        L.wait(pe, "pe", txs, ex_ld[step])
                        for ch in range(DC - 1):
                            pe.matmul(ps[bank][:],
                                      wsl_e[sp % 2][:, ch * 512 + it * 128:
                                                    ch * 512 + (it + 1) * 128],
                                      exT[step % 2][:, ch * 512:(ch + 1) * 512],
                                      start=(ch == 0), stop=False)
                        ch = DC - 1
                        mm = pe.matmul(ps[bank][:],
                                       wsl_e[sp % 2][:, ch * 512 + it * 128:
                                                     ch * 512 + (it + 1) * 128],
                                       exT[step % 2][:, ch * 512:(ch + 1) * 512],
                                       start=False, stop=True)
                        gend = L.inc(mm, "pe")
                        if kind == "up":
                            L.wait(dv, "dv", "pe", gend)
                            L.wait(dv, "dv", "ac", e_sq.get(bank, 0))
                            mx = L.inc(dv.tensor_scalar_max(fstage[bank],
                                                            ps[bank][:], 0.0), "dv")
                            e_ev[g] = mx
                            L.wait(ac, "ac", "dv", mx)
                            L.wait(ac, "ac", f"g{bank}", e_store.get(bank, 0))
                            sq = L.inc(ac.activation(stage[bank][:], fstage[bank],
                                                     AF.Square), "ac")
                            e_sq[bank] = sq
                            L.wait(gd, "gd", "ac", sq)
                            itg = idx * 4 + it
                            for j in range(4):
                                ins2 = gd.dma_start(
                                    out=u_sc[tb * 4 + j, itg, :, :],
                                    in_=stage[bank][:, j * 128:(j + 1) * 128])
                                L.inc(ins2, f"g{bank}", 16)
                            e_store[bank] = L.cnt[f"g{bank}"]
                        else:
                            L.wait(dv, "dv", "pe", gend)
                            L.wait(dv, "dv", f"g{bank}", e_store.get(bank, 0))
                            ev = L.inc(dv.tensor_copy(stage[bank][:], ps[bank][:]),
                                       "dv")
                            e_ev[g] = ev
                            L.wait(gd, "gd", "dv", ev)
                            lt = idx * 4 + it
                            ins2 = gd.dma_start(
                                out=xl_sc[lt * 128:(lt + 1) * 128,
                                          tb * 512:(tb + 1) * 512],
                                in_=stage[bank][:])
                            e_store[bank] = L.inc(ins2, f"g{bank}", 16)
                        e_grp += 1
                    if sp == 0:
                        for q in range(4):
                            tt = tb * 4 + q
                            L.wait(pe, "pe", "gb", gld)
                            L.wait(pe, "pe", "dv", rt_done)
                            for ch in range(DC - 1):
                                pe.matmul(ps[4][:, 0:8],
                                          exT[step % 2][:, ch * 512 + q * 128:
                                                        ch * 512 + (q + 1) * 128],
                                          gslab[:, ch * 8:(ch + 1) * 8],
                                          start=(ch == 0), stop=False)
                            ch = DC - 1
                            mm = pe.matmul(ps[4][:, 0:8],
                                           exT[step % 2][:, ch * 512 + q * 128:
                                                         ch * 512 + (q + 1) * 128],
                                           gslab[:, ch * 8:(ch + 1) * 8],
                                           start=False, stop=True)
                            gend = L.inc(mm, "pe")
                            L.wait(dv, "dv", "pe", gend)
                            cp = L.inc(dv.tensor_copy(lg8[:], ps[4][:, 0:8]), "dv")
                            L.wait(dv, "dv", "dv", cp)
                            r1 = L.inc(dv.reduce_max(S["m1"][:], lg8[:], axis=AX.X),
                                       "dv")
                            L.wait(dv, "dv", "dv", r1)
                            vm8 = L.inc(dv.tensor_copy(m8[:, 0:1], S["m1"][:]), "dv")
                            L.wait(dv, "dv", "dv", vm8)
                            mr = L.inc(dv.match_replace(g2[:], m8[:], lg8[:], -1e30),
                                       "dv")
                            L.wait(dv, "dv", "dv", mr)
                            r2 = L.inc(dv.reduce_max(S["m2"][:], g2[:], axis=AX.X),
                                       "dv")
                            L.wait(dv, "dv", "dv", r2)
                            rt = L.inc(dv.tensor_add(S["rt1"][:], S["m1"][:],
                                                     S["m2"][:]), "dv")
                            L.wait(pe, "pe", "dv", rt)
                            L.wait(pe, "pe", "gc", 2)
                            tr = L.inc(pe.transpose(ps[7][0:1, 0:128], S["rt1"][:],
                                                    ident_f[:]), "pe")
                            L.wait(dv, "dv", "pe", tr)
                            rt_done = L.inc(
                                dv.tensor_copy(rtrow[:, tt * 128:(tt + 1) * 128],
                                               ps[7][0:1, 0:128]), "dv")
                    step_pe[step] = L.cnt["pe"]
                    e_load(step + 2)
                e_wload(sp + 2)
            L.wait(sy, "sy", "dv", rt_done)
            rt_st = L.inc(sy.dma_start(out=rt_sc, in_=rtrow), "sx0", 16)

            if DEBUG_STOP == "E":
                return finish()
            xlat = b0b(0, 32768)
            eslab = b1b(0, 32768)
            rb_full = b0f(16384, 2048)
            barrier("sy", "gd", "pe", "dv")
            xld = L.inc(sy.dma_start(
                out=xlat.rearrange("p (c t) -> p c t", t=TOK),
                in_=xl_sc.rearrange("(c p) t -> p c t", p=128)), "lx0", 16)
            rld = L.inc(sy.dma_start(out=rb_full,
                                     in_=rt_sc.partition_broadcast(128)), "lx1", 16)
            eld = L.inc(gd.dma_start(
                out=eslab.rearrange("p (c m) -> p c m", m=LAT),
                in_=W["expert_w"].rearrange("(c p) m -> p c m", p=128)), "w0", 16)
            f_grp = 0
            f_ev = {}
            f_store = {}
            for mt in range(LC):
                for tb in range(4):
                    g = f_grp
                    bank = g % 4
                    L.wait(pe, "pe", "dv", f_ev.get(g - 4, 0))
                    L.wait(pe, "pe", "lx0", xld)
                    L.wait(pe, "pe", "w0", eld)
                    gend = None
                    for ch in range(LC):
                        mm = pe.matmul(ps[bank][:],
                                       eslab[:, ch * 2048 + mt * 128:
                                             ch * 2048 + (mt + 1) * 128],
                                       xlat[:, ch * 2048 + tb * 512:
                                            ch * 2048 + (tb + 1) * 512],
                                       start=(ch == 0), stop=(ch == LC - 1))
                    gend = L.inc(mm, "pe")
                    L.wait(dv, "dv", "pe", gend)
                    L.wait(dv, "dv", "lx1", rld)
                    L.wait(dv, "dv", f"g{bank}", f_store.get(bank, 0))
                    ev = L.inc(dv.scalar_tensor_tensor(
                        stage[bank][:], ps[bank][:], 1.0,
                        rb_full[:, tb * 512:(tb + 1) * 512],
                        OP.mult, OP.mult), "dv")
                    f_ev[g] = ev
                    L.wait(gd, "gd", "dv", ev)
                    for j in range(4):
                        ins2 = gd.dma_start(out=moe_sc[tb * 4 + j, mt, :, :],
                                            in_=stage[bank][:, j * 128:(j + 1) * 128])
                        L.inc(ins2, f"g{bank}", 16)
                    f_store[bank] = L.cnt[f"g{bank}"]
                    f_grp += 1

            if DEBUG_STOP == "F":
                return finish()
            dslab = b0b(0, 32768)
            f2slab = b0b(32768, 8192)
            woslab = b0b(40960, 16384)
            fstage = [b0f(28672 + 512 * i, 512) for i in range(4)]
            u_tt = [b1b(0, 8192), b1b(8192, 8192)]
            moe_tt = [b1b(16384, 2048), b1b(18432, 2048)]
            ot_tt = [b1b(20480, 4096), b1b(24576, 4096)]
            barrier("sy", "gd", "pe", "dv")
            g_base_pe = L.cnt["pe"]
            g_pe = {}
            g_ev = {}
            g_st = [0, 0]
            g_ld = {}
            for fb in range(8):
                L.wait(gd, "gd", "pe", g_pe.get((fb - 1, NT_OWN - 1), g_base_pe))
                L.inc(gd.dma_start(
                    out=dslab[:, 0:16384].rearrange("p (c f) -> p c f", f=512),
                    in_=W["down_w"][0:4096, fb * 512:(fb + 1) * 512].rearrange(
                        "(c p) f -> p c f", p=128)), "w0", 16)
                wl0 = L.inc(gd.dma_start(
                    out=f2slab.rearrange("p (c f) -> p c f", f=512),
                    in_=W["fc2_w"][:, fb * 512:(fb + 1) * 512].rearrange(
                        "(c p) f -> p c f", p=128)), "w0", 16)
                L.inc(gd.dma_start(
                    out=dslab[:, 16384:32768].rearrange("p (c f) -> p c f", f=512),
                    in_=W["down_w"][4096:8192, fb * 512:(fb + 1) * 512].rearrange(
                        "(c p) f -> p c f", p=128)), "w1", 16)
                wl = L.inc(gd.dma_start(
                    out=woslab.rearrange("p (c f) -> p c f", f=512),
                    in_=W["Wo"][:, fb * 512:(fb + 1) * 512].rearrange(
                        "(c p) f -> p c f", p=128)), "w1", 16)
                bld = L.inc(gd.dma_start(
                    out=rowf[:], in_=VEC["bo"][0:1, fb * 512:(fb + 1) * 512]),
                    "gb", 16)
                L.wait(dv, "dv", "gb", bld)
                brc = L.inc(dv.tensor_copy(rowfr[:], rowf[:]), "dv")
                for tt in range(NT_OWN):
                    p = tt % 2
                    lxs = f"lx{p}"
                    L.wait(sy, "sy", "pe", g_pe.get((fb, tt - 2),
                                                    g_pe.get((fb - 1, NT_OWN - 1),
                                                             g_base_pe)))
                    L.inc(sy.dma_start(
                        out=u_tt[p].rearrange("p (c j) -> p c j", j=128),
                        in_=u_sc[tt].rearrange("c p j -> p c j")), lxs, 16)
                    L.inc(sy.dma_start(
                        out=moe_tt[p].rearrange("p (c j) -> p c j", j=128),
                        in_=moe_sc[tt].rearrange("c p j -> p c j")), lxs, 16)
                    l3 = L.inc(sy.dma_start(
                        out=ot_tt[p].rearrange("p (c j) -> p c j", j=128),
                        in_=ot_sc[tt].rearrange("c p j -> p c j")), lxs, 16)
                    zbank = 4 + p
                    abank = 6 + p
                    L.wait(pe, "pe", lxs, l3)
                    L.wait(pe, "pe", "w0", wl0)
                    L.wait(pe, "pe", "dv", max(g_ev.get((fb, tt - 2), 0), brc))
                    for ch in range(IC // 2):
                        pe.matmul(ps[zbank][:],
                                  u_tt[p][:, ch * 128:(ch + 1) * 128],
                                  dslab[:, ch * 512:(ch + 1) * 512],
                                  start=(ch == 0), stop=False)
                    L.wait(pe, "pe", "w1", wl)
                    for ch in range(IC // 2, IC):
                        pe.matmul(ps[zbank][:],
                                  u_tt[p][:, ch * 128:(ch + 1) * 128],
                                  dslab[:, ch * 512:(ch + 1) * 512],
                                  start=False, stop=False)
                    for ch in range(LC):
                        mm = pe.matmul(ps[zbank][:],
                                       moe_tt[p][:, ch * 128:(ch + 1) * 128],
                                       f2slab[:, ch * 512:(ch + 1) * 512],
                                       start=False, stop=(ch == LC - 1))
                    zend = L.inc(mm, "pe")
                    for ch in range(NH):
                        pe.matmul(ps[abank][:],
                                  ot_tt[p][:, ch * 128:(ch + 1) * 128],
                                  woslab[:, ch * 512:(ch + 1) * 512],
                                  start=(ch == 0), stop=False)
                    mm = pe.matmul(ps[abank][:], ones_fr[:], rowfr[:],
                                   start=False, stop=True)
                    aend = L.inc(mm, "pe")
                    g_pe[(fb, tt)] = aend
                    L.wait(dv, "dv", "pe", aend)
                    L.wait(dv, "dv", f"sx{p}", g_st[p])
                    dv.tensor_copy(fstage[p], ps[zbank][:])
                    ev = L.inc(dv.tensor_copy(fstage[2 + p], ps[abank][:]), "dv")
                    g_ev[(fb, tt)] = ev
                    L.wait(ac, "ac", "dv", ev)
                    L.inc(ac.dma_start(out=z_sc[tt * 128:(tt + 1) * 128,
                                                fb * 512:(fb + 1) * 512],
                                       in_=fstage[p]), f"sx{p}", 16)
                    L.inc(ac.dma_start(out=at_sc[tt * 128:(tt + 1) * 128,
                                                 fb * 512:(fb + 1) * 512],
                                       in_=fstage[2 + p]), f"sx{p}", 16)
                    g_st[p] = L.cnt[f"sx{p}"]

            if DEBUG_STOP == "G":
                return finish()
            zrow = [b0f(0, 4096), b0f(4096, 4096)]
            arow = [b0f(8192, 4096), b0f(12288, 4096)]
            xrow = [b0f(16384, 4096), b0f(20480, 4096)]
            sqrow = b0f(24576, 4096)
            lnrow = b0f(28672, 4096)
            gbcH = b1f(0, 4096)
            gbcH2 = b1f(4096, 4096)
            barrier("sy", "dv", "ac")
            gl1 = L.inc(sy.dma_start(out=gbcH,
                                     in_=VEC["ln_g"].partition_broadcast(128)),
                        "tx0", 16)
            gl2 = L.inc(sy.dma_start(out=gbcH2,
                                     in_=VEC["ln_b"].partition_broadcast(128)),
                        "tx0", 16)
            h_out = {}
            h_st = [0, 0]
            for tt in range(NT_OWN):
                p = tt % 2
                lxs = f"lx{p}"
                if tt >= 2:
                    L.wait(sy, "sy", "dv", h_out[tt - 2])
                    L.wait(sy, "sy", "so", h_st[0])
                L.inc(sy.dma_start(out=zrow[p],
                                   in_=z_sc[tt * 128:(tt + 1) * 128, :]), lxs, 16)
                L.inc(sy.dma_start(out=arow[p],
                                   in_=at_sc[tt * 128:(tt + 1) * 128, :]), lxs, 16)
                l3 = L.inc(sy.dma_start(out=xrow[p],
                                        in_=x_own[tt * 128:(tt + 1) * 128, :]),
                           lxs, 16)
                stats, sd = ln_stats(zrow[p], sqrow, lxs, l3,
                                     h_out.get(tt - 1, 0))
                L.wait(ac, "ac", "dv", stats)
                L.wait(ac, "ac", "ac", sd)
                ln = L.inc(ac.activation(lnrow, zrow[p], AF.Identity,
                                         bias=S["nbias"][:], scale=S["rstd"][:]),
                           "ac")
                L.wait(dv, "dv", "ac", ln)
                L.wait(dv, "dv", "tx0", gl2)
                w1 = L.inc(dv.tensor_mul(sqrow, lnrow, gbcH), "dv")
                L.wait(dv, "dv", "dv", w1)
                w2 = L.inc(dv.tensor_add(lnrow, sqrow, gbcH2), "dv")
                L.wait(dv, "dv", "dv", w2)
                w3 = L.inc(dv.tensor_add(sqrow, lnrow, arow[p]), "dv")
                L.wait(dv, "dv", "dv", w3)
                fo = L.inc(dv.tensor_add(zrow[p], sqrow, xrow[p]), "dv")
                h_out[tt] = fo
                L.wait(sy, "sy", "dv", fo)
                L.inc(sy.dma_start(out=out_hbm[tt * 128:(tt + 1) * 128, :],
                                   in_=zrow[p]), "so", 16)
                h_st[0] = L.cnt["so"]
            return finish()

    return nc


def kernel(**inputs):
    nc = build()
    in_maps = []
    x = np.ascontiguousarray(inputs["L0_attn_hidden_states"])
    row = lambda a: np.ascontiguousarray(np.asarray(a, dtype=np.float32).reshape(1, -1))
    base = {
        "Wq": inputs["L0_attn_Wq"], "Wk": inputs["L0_attn_Wk"],
        "Wv": inputs["L0_attn_Wv"], "Wo": inputs["L0_attn_Wo"],
        "bq": row(inputs["L0_attn_bq"]), "bk": row(inputs["L0_attn_bk"]),
        "bv": row(inputs["L0_attn_bv"]), "bo": row(inputs["L0_attn_bo"]),
        "gq": row(inputs["L0_attn_gq"]), "gk": row(inputs["L0_attn_gk"]),
        "gate_w": inputs["L0_ffn_gate_w"], "up_w": inputs["L0_ffn_up_w"],
        "down_w": inputs["L0_ffn_down_w"], "fc1_w": inputs["L0_ffn_fc1_w"],
        "fc2_w": inputs["L0_ffn_fc2_w"], "expert_w": inputs["L0_ffn_expert_w"],
        "ln_g": row(inputs["L0_ffn_ln_gamma"]), "ln_b": row(inputs["L0_ffn_ln_beta"]),
    }
    base = {k: np.ascontiguousarray(np.asarray(v, dtype=np.float32))
            for k, v in base.items()}
    for c in range(8):
        b, half = c // 2, c % 2
        m = dict(base)
        m["x_own"] = np.ascontiguousarray(x[b, half * TOK:(half + 1) * TOK, :])
        m["x_oth"] = np.ascontiguousarray(x[b, (1 - half) * TOK:(2 - half) * TOK, :])
        in_maps.append(m)
    res = run_bass_kernel_spmd(nc, in_maps, core_ids=list(range(8)))
    outs = [res.results[c]["out"] for c in range(8)]
    full = np.empty((B, SEQ, D), dtype=np.float32)
    for c in range(8):
        b, half = c // 2, c % 2
        full[b, half * TOK:(half + 1) * TOK, :] = outs[c]
    return full


# version 04
# speedup vs baseline: 1.1823x; 1.0130x over previous
import contextlib
import numpy as np
import concourse.bass as bass
import concourse.mybir as mybir
from concourse.bass_utils import run_bass_kernel_spmd

F32 = mybir.dt.float32
F32R = mybir.dt.float32r
BF16 = mybir.dt.bfloat16
AF = mybir.ActivationFunctionType
AX = mybir.AxisListType
OP = mybir.AluOpType

B, SEQ, D = 4, 4096, 4096
NH, HD = 32, 128
INT, LAT, NE = 8192, 2048, 8
TOK = 2048
DC = D // 128
IC = INT // 128
LC = LAT // 128
NT_OWN = TOK // 128
NT_SEQ = SEQ // 128
LN_EPS = 1e-5
RMS_EPS = 1e-6
FLN_EPS = 1e-5
ISQ = float(1.0 / np.sqrt(HD))

DEBUG_STOP = None
DEBUG_TAPS = ()

DMA_SEMS = ("lx0", "lx1", "lx2", "sx0", "sx1", "sb0", "sb1", "sb2", "sb3",
            "tx0", "tx1", "w0", "w1", "gb", "g0", "g1", "g2", "g3", "so")
ALL_SEMS = DMA_SEMS + ("pe", "dv", "ac", "gc")


class Led:
    def __init__(self, nc, ctx):
        self.nc, self.ctx = nc, ctx
        self.sem, self.cnt, self.hw = {}, {}, {}

    def mk(self, *names):
        for n in names:
            self.sem[n] = self.ctx.enter_context(self.nc.semaphore(n))
            self.cnt[n] = 0

    def inc(self, instr, s, n=1):
        self.cnt[s] += n
        instr.then_inc(self.sem[s], n)
        return self.cnt[s]

    def wait(self, eng, ekey, s, val=None):
        if val is None:
            val = self.cnt[s]
        if val <= 0:
            return
        assert val <= self.cnt[s], f"wait {s}>={val} but only {self.cnt[s]} planned"
        if self.hw.get((ekey, s), 0) < val:
            eng.wait_ge(self.sem[s], val)
            self.hw[(ekey, s)] = val


def build():
    nc = bass.Bass("TRN2", target_bir_lowering=False)

    x_own = nc.dram_tensor("x_own", [TOK, D], F32, kind="ExternalInput").ap()
    x_oth = nc.dram_tensor("x_oth", [TOK, D], F32, kind="ExternalInput").ap()
    W = {}
    for nm, shp in (("Wq", [D, D]), ("Wk", [D, D]), ("Wv", [D, D]), ("Wo", [D, D]),
                    ("gate_w", [D, NE]), ("up_w", [D, INT]), ("down_w", [INT, D]),
                    ("fc1_w", [D, LAT]), ("fc2_w", [LAT, D]), ("expert_w", [LAT, LAT])):
        W[nm] = nc.dram_tensor(nm, shp, F32, kind="ExternalInput").ap()
    VEC = {}
    for nm in ("bq", "bk", "bv", "bo", "gq", "gk", "ln_g", "ln_b"):
        VEC[nm] = nc.dram_tensor(nm, [1, D], F32, kind="ExternalInput").ap()
    out_hbm = nc.dram_tensor("out", [TOK, D], F32, kind="ExternalOutput").ap()

    xn_sc = nc.dram_tensor("xn_sc", [SEQ, D], BF16).ap()
    xnT_sc = nc.dram_tensor("xnT_sc", [DC, 128, SEQ], BF16).ap()
    qr_sc = nc.dram_tensor("qr_sc", [TOK, D], BF16).ap()
    kr_sc = nc.dram_tensor("kr_sc", [SEQ, D], BF16).ap()
    qn_sc = nc.dram_tensor("qn_sc", [TOK, D], BF16).ap()
    kn_sc = nc.dram_tensor("kn_sc", [SEQ, D], BF16).ap()
    v_sc = nc.dram_tensor("v_sc", [SEQ, D], BF16).ap()
    ot_sc = nc.dram_tensor("ot_sc", [NT_OWN, NH, 128, 128], BF16).ap()
    u_sc = nc.dram_tensor("u_sc", [NT_OWN, IC, 128, 128], BF16).ap()
    xl_sc = nc.dram_tensor("xl_sc", [LAT, TOK], BF16).ap()
    moe_sc = nc.dram_tensor("moe_sc", [NT_OWN, LC, 128, 128], BF16).ap()
    rt_sc = nc.dram_tensor("rt_sc", [1, TOK], F32).ap()
    rstd_sc = nc.dram_tensor("rstd_sc", [2, 1, SEQ], F32).ap()
    z_sc = nc.dram_tensor("z_sc", [TOK, D], F32).ap()
    at_sc = nc.dram_tensor("at_sc", [TOK, D], F32).ap()
    scratch = {"xn_sc": xn_sc, "qr_sc": qr_sc, "kr_sc": kr_sc, "qn_sc": qn_sc,
               "kn_sc": kn_sc, "v_sc": v_sc, "xl_sc": xl_sc, "rt_sc": rt_sc,
               "z_sc": z_sc, "at_sc": at_sc,
               "ot_sc": ot_sc, "u_sc": u_sc, "moe_sc": moe_sc}
    dbg = {nm: nc.dram_tensor("dbg_" + nm, list(scratch[nm].shape), scratch[nm].dtype,
                              kind="ExternalOutput").ap() for nm in DEBUG_TAPS}

    ctx = contextlib.ExitStack()
    with ctx:
        L = Led(nc, ctx)
        L.mk(*ALL_SEMS)

        def sbuf(name, shape, dt):
            return ctx.enter_context(nc.sbuf_tensor(name, shape, dt))

        b0 = sbuf("b0", [128, 65536], BF16)
        b1 = sbuf("b1", [128, 32768], BF16)
        stage = [sbuf(f"stage{i}", [128, 512], BF16) for i in range(4)]
        ones_fr = sbuf("ones_fr", [1, 128], F32R)
        ones_f = sbuf("ones_f", [1, 128], F32)
        onescol_bf = sbuf("onescol_bf", [128, 1], BF16)
        ones_bfr = sbuf("ones_bfr", [1, 128], BF16)
        ident_f = sbuf("ident_f", [128, 128], F32)
        rowf = sbuf("rowf", [1, 512], F32)
        rowfr = sbuf("rowfr", [1, 512], F32R)
        rs_row = sbuf("rs_row", [1, 512], F32R)
        fsq = sbuf("fsq", [128, 512], F32)
        acc_q = sbuf("acc_q", [128, NT_OWN], F32)
        acc_k = sbuf("acc_k", [128, NT_SEQ], F32)
        rcol = sbuf("rcol", [128, NT_SEQ], F32)
        gq_cols = sbuf("gq_cols", [128, NH], F32)
        gk_cols = sbuf("gk_cols", [128, NH], F32)
        m8 = sbuf("m8", [128, 8], F32)
        eps5 = sbuf("eps5", [128, 1], F32)
        eps6 = sbuf("eps6", [128, 1], F32)
        lnisq = sbuf("lnisq", [128, 1], F32)
        zero1 = sbuf("zero1", [128, 1], F32)
        lg8 = sbuf("lg8", [128, 8], F32)
        g2 = sbuf("g2", [128, 8], F32)
        small = {nm: sbuf(nm, [128, 1], F32) for nm in
                 ("s1", "s2", "mu", "msq", "var", "sd", "rstd", "nbias",
                  "m1", "m2", "rt1")}
        mparts = sbuf("mparts", [128, 8], F32)
        negm = sbuf("negm", [128, 1], F32)
        S = small

        def b0f(lo, n):
            return b0[:].bitcast(F32)[:, lo:lo + n]

        def b0b(lo, n):
            return b0[:, lo:lo + n]

        def b1f(lo, n):
            return b1[:].bitcast(F32)[:, lo:lo + n]

        def b1b(lo, n):
            return b1[:, lo:lo + n]

        ps = [ctx.enter_context(nc.psum_tensor(f"ps{i}", [128, 512], F32))
              for i in range(8)]

        block = ctx.enter_context(nc.Block())

        @block.sync
        def _(sy):
            gd = nc.gpsimd
            pe = nc.tensor
            dv = nc.vector
            ac = nc.scalar
            ENG = {"pe": pe, "dv": dv, "ac": ac, "gd": gd, "sy": sy}

            def barrier(*ekeys):
                for ek in ekeys:
                    e = ENG[ek]
                    for s in ALL_SEMS:
                        if s == "so":
                            continue
                        L.wait(e, ek, s, L.cnt[s])

            gd.memset(ones_f[:], 1.0)
            gd.memset(onescol_bf[:], 1.0)
            v0 = L.inc(gd.memset(ident_f[:], 0.0), "gc")
            L.wait(gd, "gd", "gc", v0)
            ins = gd.affine_select(out=ident_f[:], in_=ident_f[:],
                                   compare_op=OP.not_equal, fill=1.0, base=0,
                                   pattern=[[-1, 128]], channel_multiplier=1)
            L.inc(ins, "gc")
            dv.memset(ones_bfr[:], 1.0)
            dv.memset(m8[:], 3.0e38)
            dv.memset(eps5[:], LN_EPS)
            dv.memset(eps6[:], RMS_EPS)
            dv.memset(lnisq[:], float(np.log(ISQ)))
            dv.memset(zero1[:], 0.0)
            dv.memset(acc_q[:], 0.0)
            dv.memset(acc_k[:], 0.0)
            L.wait(dv, "dv", "gc", 1)
            L.inc(dv.tensor_copy(ones_fr[:], ones_f[:]), "dv")
            init_dv = L.cnt["dv"]

            def finish():
                for s in ALL_SEMS:
                    if s != "so":
                        L.wait(sy, "sy", s, L.cnt[s])
                for nm, dap in dbg.items():
                    L.inc(sy.dma_start(out=dap, in_=scratch[nm]), "so", 16)
                L.wait(sy, "sy", "so", L.cnt["so"])

            def ln_stats(rowx, rowsq, ld_sem, ld_val, sq_guard=0):
                L.wait(dv, "dv", ld_sem, ld_val)
                dv.reduce_sum(S["s1"][:], rowx, axis=AX.X)
                L.wait(ac, "ac", ld_sem, ld_val)
                if sq_guard:
                    L.wait(ac, "ac", "dv", sq_guard)
                sq = L.inc(ac.activation(rowsq, rowx, AF.Square), "ac")
                L.wait(dv, "dv", "ac", sq)
                sed = L.inc(dv.reduce_sum(S["s2"][:], rowsq, axis=AX.X), "dv")
                L.wait(dv, "dv", "dv", sed)
                dv.tensor_scalar_mul(S["mu"][:], S["s1"][:], 1.0 / D)
                v1 = L.inc(dv.tensor_scalar_mul(S["s2"][:], S["s2"][:], 1.0 / D), "dv")
                L.wait(dv, "dv", "dv", v1)
                v2 = L.inc(dv.tensor_mul(S["msq"][:], S["mu"][:], S["mu"][:]), "dv")
                L.wait(dv, "dv", "dv", v2)
                st = L.inc(dv.tensor_sub(S["var"][:], S["s2"][:], S["msq"][:]), "dv")
                L.wait(ac, "ac", "dv", st)
                L.wait(ac, "ac", "dv", init_dv)
                lnv = L.inc(ac.activation(S["sd"][:], S["var"][:], AF.Ln,
                                          bias=eps5[:]), "ac")
                L.wait(ac, "ac", "ac", lnv)
                sd = L.inc(ac.activation(S["rstd"][:], S["sd"][:], AF.Exp,
                                         scale=-0.5, bias=zero1[:]), "ac")
                L.wait(dv, "dv", "ac", sd)
                v3 = L.inc(dv.tensor_mul(S["nbias"][:], S["mu"][:], S["rstd"][:]), "dv")
                L.wait(dv, "dv", "dv", v3)
                stats = L.inc(dv.tensor_scalar_mul(S["nbias"][:], S["nbias"][:],
                                                   -1.0), "dv")
                return stats, sd

            rax = [b0f(0, 4096), b0f(4096, 4096)]
            rsq = [b0f(8192, 4096), b0f(12288, 4096)]
            rno = [b0b(32768, 4096), b0b(36864, 4096)]
            rax_g = [0, 0]
            rsq_g = [0, 0]
            rno_g = [0, 0]
            for i in range(NT_SEQ):
                p = i % 2
                src = x_own if i < NT_OWN else x_oth
                r0 = (i % NT_OWN) * 128
                lxs = f"lx{p}"
                L.wait(sy, "sy", "ac", rax_g[p])
                ld = L.inc(sy.dma_start(out=rax[p], in_=src[r0:r0 + 128, :]), lxs, 16)
                stats, sd = ln_stats(rax[p], rsq[p], lxs, ld, rsq_g[p])
                rsq_g[p] = L.cnt["dv"]
                L.wait(ac, "ac", "dv", stats)
                L.wait(ac, "ac", "ac", sd)
                L.wait(ac, "ac", f"sx{p}", rno_g[p])
                nm_ = L.inc(ac.activation(rno[p], rax[p], AF.Identity,
                                          bias=S["nbias"][:], scale=S["rstd"][:]),
                            "ac")
                rax_g[p] = nm_
                L.wait(sy, "sy", "ac", nm_)
                rno_g[p] = L.inc(
                    sy.dma_start(out=xn_sc[i * 128:(i + 1) * 128, :], in_=rno[p]),
                    f"sx{p}", 16)

            if DEBUG_STOP == "A":
                return finish()
            xnT = b0[:]
            barrier("sy", "gd", "pe", "dv", "ac")
            qkv_grp = 0
            grp_ev = {}
            b_store = {}
            jobs_by_half = {0: [("Wk", "bk"), ("Wv", "bv"), ("Wq", "bq")],
                            1: [("Wk", "bk"), ("Wv", "bv")]}
            jobidx = 0
            slab_use = {}
            last_rowfr_pe = 0
            grp_sq = {}
            last_rr = [0]
            last_ad = [0]
            for half in (0, 1):
                if half == 1:
                    L.wait(sy, "sy", "pe", L.cnt["pe"])
                    L.wait(sy, "sy", "sx0", L.cnt["sx0"])
                for ch in range(DC):
                    L.inc(sy.dma_start(
                        out=xnT[:, ch * TOK:(ch + 1) * TOK],
                        in_=xn_sc[half * TOK:(half + 1) * TOK,
                                  ch * 128:(ch + 1) * 128],
                        transpose=True), "tx0", 16)
                xnT_ld = L.cnt["tx0"]
                if half == 0:
                    L.wait(sy, "sy", "tx0", xnT_ld)
                    for ch in range(DC):
                        L.inc(sy.dma_start(
                            out=xnT_sc[ch, :, 0:TOK],
                            in_=xnT[:, ch * TOK:(ch + 1) * TOK]), "sx0", 16)
                for wnm, bnm in jobs_by_half[half]:
                    for ib in range(8):
                        wsem = f"w{jobidx % 2}"
                        if jobidx >= 2:
                            L.wait(gd, "gd", "pe", slab_use[jobidx - 2])
                        wsl = b1b(0, 16384) if jobidx % 2 == 0 else b1b(16384, 16384)
                        wld = L.inc(gd.dma_start(
                            out=wsl.rearrange("p (c f) -> p c f", f=512),
                            in_=W[wnm][:, ib * 512:(ib + 1) * 512].rearrange(
                                "(c p) f -> p c f", p=128)), wsem, 16)
                        L.wait(gd, "gd", "pe", last_rowfr_pe)
                        bld = L.inc(gd.dma_start(
                            out=rowf[:],
                            in_=VEC[bnm][0:1, ib * 512:(ib + 1) * 512]), "gb", 16)
                        L.wait(dv, "dv", "gb", bld)
                        brc = L.inc(dv.tensor_copy(rowfr[:], rowf[:]), "dv")
                        for tt in range(NT_OWN):
                            g = qkv_grp
                            bank = g % 4
                            L.wait(pe, "pe", "dv", max(grp_ev.get(g - 4, 0), brc))
                            L.wait(pe, "pe", "ac", grp_sq.get(g - 4, 0))
                            L.wait(pe, "pe", wsem, wld)
                            L.wait(pe, "pe", "tx0", xnT_ld)
                            for ch in range(DC):
                                pe.matmul(ps[bank][:],
                                          xnT[:, ch * TOK + tt * 128:
                                              ch * TOK + (tt + 1) * 128],
                                          wsl[:, ch * 512:(ch + 1) * 512],
                                          start=(ch == 0), stop=False)
                            mm = pe.matmul(ps[bank][:], ones_fr[:], rowfr[:],
                                           start=False, stop=True)
                            gend = L.inc(mm, "pe")
                            L.wait(dv, "dv", "pe", gend)
                            L.wait(dv, "dv", f"sb{bank}", b_store.get(bank, 0))
                            ev = L.inc(dv.tensor_copy(stage[bank][:], ps[bank][:]),
                                       "dv")
                            grp_ev[g] = ev
                            if wnm in ("Wq", "Wk"):
                                L.wait(ac, "ac", "pe", gend)
                                L.wait(ac, "ac", "dv", max(ev, last_rr[0]))
                                aq = L.inc(ac.activation(fsq[:], ps[bank][:],
                                                         AF.Square), "ac")
                                grp_sq[g] = aq
                                L.wait(dv, "dv", "ac", aq)
                                L.wait(dv, "dv", "dv", last_ad[0])
                                rr = L.inc(dv.reduce_sum(S["s2"][:], fsq[:],
                                                         axis=AX.X), "dv")
                                last_rr[0] = rr
                                L.wait(dv, "dv", "dv", rr)
                                col = tt if wnm == "Wq" else half * NT_OWN + tt
                                accT = acc_q if wnm == "Wq" else acc_k
                                ad = L.inc(dv.tensor_add(accT[:, col:col + 1],
                                                         accT[:, col:col + 1],
                                                         S["s2"][:]), "dv")
                                last_ad[0] = ad
                            L.wait(sy, "sy", "dv", ev)
                            trow = half * TOK + tt * 128
                            if wnm == "Wq":
                                dst = qr_sc[tt * 128:tt * 128 + 128,
                                            ib * 512:(ib + 1) * 512]
                            elif wnm == "Wk":
                                dst = kr_sc[trow:trow + 128, ib * 512:(ib + 1) * 512]
                            else:
                                dst = v_sc[trow:trow + 128, ib * 512:(ib + 1) * 512]
                            b_store[bank] = L.inc(
                                sy.dma_start(out=dst, in_=stage[bank][:]),
                                f"sb{bank}", 16)
                            qkv_grp += 1
                        slab_use[jobidx] = L.cnt["pe"]
                        last_rowfr_pe = L.cnt["pe"]
                        jobidx += 1

            if DEBUG_STOP == "B":
                return finish()
            barrier("pe", "dv", "ac", "sy")
            rowb = b1[0:1, :].bitcast(F32)
            rowb_g = 0
            for jj, (accT, ncol, use_isq) in enumerate(
                    ((acc_q, NT_OWN, True), (acc_k, NT_SEQ, False))):
                v1 = L.inc(dv.tensor_scalar_mul(accT[:, 0:ncol], accT[:, 0:ncol],
                                                1.0 / D), "dv")
                L.wait(ac, "ac", "dv", v1)
                lnv = L.inc(ac.activation(rcol[:, 0:ncol], accT[:, 0:ncol],
                                          AF.Ln, bias=eps6[:]), "ac")
                L.wait(ac, "ac", "ac", lnv)
                ex = L.inc(ac.activation(rcol[:, 0:ncol], rcol[:, 0:ncol], AF.Exp,
                                         scale=-0.5,
                                         bias=(lnisq[:] if use_isq else zero1[:])),
                           "ac")
                for j in range(ncol):
                    L.wait(pe, "pe", "ac", ex)
                    L.wait(pe, "pe", "gc", 2)
                    L.wait(pe, "pe", "dv", L.cnt["dv"])
                    tr = L.inc(pe.transpose(ps[7][0:1, 0:128], rcol[:, j:j + 1],
                                            ident_f[:]), "pe")
                    L.wait(dv, "dv", "pe", tr)
                    L.wait(dv, "dv", "sx0", rowb_g)
                    L.inc(dv.tensor_copy(rowb[0:1, j * 128:(j + 1) * 128],
                                         ps[7][0:1, 0:128]), "dv")
                L.wait(sy, "sy", "dv", L.cnt["dv"])
                rowb_g = L.inc(sy.dma_start(out=rstd_sc[jj][0:1, 0:ncol * 128],
                                            in_=rowb[0:1, 0:ncol * 128]), "sx0", 16)

            KT = [b0b(0, 4096), b0b(4096, 4096), b0b(8192, 4096)]
            QT = [b0b(12288, 2048), b0b(14336, 2048), b0b(16384, 2048)]
            VS = [b0b(18432, 4096), b0b(22528, 4096), b0b(26624, 4096)]
            PT = [b0b(30720 + 512 * i, 512) for i in range(4)]
            otst = [b0b(32768, 512), b0b(33280, 512)]

            def nrow(hh, qb):
                off = 33792 + ((hh % 2) * 4 + qb) * 512
                return b0[0:1, off:off + 512]

            rb_sb = b0f(19456, 512)
            rqb = b0f(20480, 2048)
            rkb = b0f(24576, 4096)
            grow = b0[0:1, :].bitcast(F32)[0:1, 24576:28672]
            barrier("sy", "pe", "dv", "ac", "gd")
            gld1 = L.inc(sy.dma_start(out=grow[0:1, 0:4096], in_=VEC["gq"]),
                         "tx0", 16)
            gcl = 0
            for h in range(NH):
                L.wait(pe, "pe", "tx0", gld1)
                L.wait(pe, "pe", "gc", 2)
                L.wait(pe, "pe", "dv", gcl)
                tr = L.inc(pe.transpose(ps[7][0:128, 0:1],
                                        grow[0:1, h * 128:(h + 1) * 128],
                                        ident_f[0:1, 0:1]), "pe")
                L.wait(dv, "dv", "pe", tr)
                gcl = L.inc(dv.tensor_copy(gq_cols[:, h:h + 1], ps[7][0:128, 0:1]),
                            "dv")
            L.wait(sy, "sy", "dv", gcl)
            gld2 = L.inc(sy.dma_start(out=grow[0:1, 0:4096], in_=VEC["gk"]),
                         "tx0", 16)
            for h in range(NH):
                L.wait(pe, "pe", "tx0", gld2)
                L.wait(pe, "pe", "dv", gcl)
                tr = L.inc(pe.transpose(ps[7][0:128, 0:1],
                                        grow[0:1, h * 128:(h + 1) * 128],
                                        ident_f[0:1, 0:1]), "pe")
                L.wait(dv, "dv", "pe", tr)
                gcl = L.inc(dv.tensor_copy(gk_cols[:, h:h + 1], ps[7][0:128, 0:1]),
                            "dv")
            L.wait(sy, "sy", "dv", gcl)
            L.wait(sy, "sy", "sx0", L.cnt["sx0"])
            rbl = L.inc(sy.dma_start(out=rqb,
                                     in_=rstd_sc[0][0:1, 0:TOK]
                                     .partition_broadcast(128)), "tx0", 16)
            rbl = L.inc(sy.dma_start(out=rkb,
                                     in_=rstd_sc[1].partition_broadcast(128)),
                        "tx0", 16)
            d_head_pe = {-1: L.cnt["pe"], -2: L.cnt["pe"], -3: L.cnt["pe"]}
            d_loads = {}
            d_apply = {}
            bank_free = {b: ("dv", 0) for b in range(8)}
            bank_free[7] = ("dv", gcl)
            pt_free = [0, 0, 0, 0]
            nr_ready = {}
            nr_user = {}
            ot_g = [0, 0]

            def pe_bank(b):
                s, v = bank_free[b]
                L.wait(pe, "pe", s, v)

            def d_load_head(h):
                if h >= NH:
                    return
                hp = h % 3
                lxs = f"lx{hp}"
                L.wait(sy, "sy", "pe", d_head_pe[h - 3])
                L.inc(sy.dma_start(out=KT[hp][:, :],
                                   in_=kr_sc[:, h * 128:(h + 1) * 128],
                                   transpose=True), lxs, 16)
                L.inc(sy.dma_start(out=QT[hp][:, :],
                                   in_=qr_sc[:, h * 128:(h + 1) * 128],
                                   transpose=True), lxs, 16)
                for part in range(4):
                    L.inc(sy.dma_start(
                        out=VS[hp][:, part * 1024:(part + 1) * 1024].rearrange(
                            "p (kt j) -> p kt j", j=128),
                        in_=v_sc[part * 1024:(part + 1) * 1024,
                                 h * 128:(h + 1) * 128].rearrange(
                            "(kt p) j -> p kt j", p=128)), lxs, 16)
                d_loads[h] = L.cnt[lxs]

            def d_apply_head(h):
                if h >= NH:
                    return
                hp = h % 3
                L.wait(dv, "dv", f"lx{hp}", d_loads[h])
                L.wait(dv, "dv", "tx0", rbl)
                L.inc(dv.scalar_tensor_tensor(KT[hp][:, :], KT[hp][:, :],
                                              gk_cols[:, h:h + 1], rkb,
                                              OP.mult, OP.mult), "dv")
                ap = L.inc(dv.scalar_tensor_tensor(QT[hp][:, :], QT[hp][:, :],
                                                   gq_cols[:, h:h + 1], rqb,
                                                   OP.mult, OP.mult), "dv")
                d_apply[h] = ap

            d_load_head(0)
            d_apply_head(0)
            d_load_head(1)
            d_apply_head(1)

            def emit_p1_slot(h, slot):
                hp = h % 3
                qb, r = divmod(slot, 32)
                qt, kb = divmod(r, 8)
                qcol = (qb * 4 + qt) * 128
                bank = slot % 2
                pe_bank(bank)
                L.wait(pe, "pe", "dv", d_apply[h])
                mm = pe.matmul(ps[bank][:], QT[hp][:, qcol:qcol + 128],
                               KT[hp][:, kb * 512:(kb + 1) * 512],
                               start=True, stop=True)
                pend = L.inc(mm, "pe")
                L.wait(dv, "dv", "pe", pend)
                rr = L.inc(dv.reduce_max(mparts[:, kb:kb + 1], ps[bank][:],
                                         axis=AX.X), "dv")
                bank_free[bank] = ("dv", rr)
                if kb == 7:
                    L.wait(dv, "dv", "dv", rr)
                    nm_ = L.inc(dv.reduce_max(negm[:], mparts[:], axis=AX.X,
                                              negate=True), "dv")
                    L.wait(dv, "dv", "dv", nm_)
                    L.wait(pe, "pe", "dv", nm_)
                    pe_bank(7)
                    tr = L.inc(pe.transpose(ps[7][0:1, 0:128], negm[:],
                                            ident_f[:]), "pe")
                    L.wait(dv, "dv", "pe", tr)
                    L.wait(dv, "dv", "pe", nr_user.get((h - 2, qb), 0))
                    cp = L.inc(dv.tensor_copy(
                        nrow(h, qb)[:, qt * 128:(qt + 1) * 128],
                        ps[7][0:1, 0:128]), "dv")
                    bank_free[7] = ("dv", cp)
                    if qt == 3:
                        nr_ready[(h, qb)] = cp

            def emit_p2_kt(h, qb, kt):
                hp = h % 3
                qcol = qb * 512
                bank = 2 + (kt % 3)
                pe_bank(bank)
                L.wait(pe, "pe", "dv", nr_ready[(h, qb)])
                pe.matmul(ps[bank][:], KT[hp][:, kt * 128:(kt + 1) * 128],
                          QT[hp][:, qcol:qcol + 512], start=True, stop=False)
                mm = pe.matmul(ps[bank][:], ones_bfr[:], nrow(h, qb),
                               start=False, stop=True)
                gend = L.inc(mm, "pe")
                nr_user[(h, qb)] = gend
                L.wait(ac, "ac", "pe", max(gend, pt_free[kt % 4]))
                ex = L.inc(ac.activation(PT[kt % 4][:], ps[bank][:], AF.Exp), "ac")
                bank_free[bank] = ("ac", ex)
                return ex

            def emit_p2_sums_av(h, qb, kt, ex):
                hp = h % 3
                L.wait(pe, "pe", "ac", ex)
                if kt == 0:
                    pe_bank(5)
                    pe_bank(6)
                pe.matmul(ps[5][0:1, :], onescol_bf[:], PT[kt % 4][:],
                          start=(kt == 0), stop=(kt == NT_SEQ - 1))
                mm = pe.matmul(ps[6][:], VS[hp][:, kt * 128:(kt + 1) * 128],
                               PT[kt % 4][:],
                               start=(kt == 0), stop=(kt == NT_SEQ - 1))
                pt_free[kt % 4] = L.inc(mm, "pe")

            def emit_tail_dv(h, qb):
                avend = L.cnt["pe"]
                L.wait(dv, "dv", "pe", avend)
                rc0 = L.inc(dv.reciprocal(rowf[:], ps[5][0:1, :]), "dv")
                bank_free[5] = ("dv", rc0)
                L.wait(dv, "dv", "dv", rc0)
                rc = L.inc(dv.tensor_copy(rs_row[:], rowf[:]), "dv")
                L.wait(dv, "dv", "dv", rc)
                return rc

            def flush_tail(h, qb, rc):
                L.wait(pe, "pe", "dv", rc)
                pe_bank(7)
                bc = L.inc(pe.matmul(ps[7][:], ones_fr[:], rs_row[:],
                                     start=True, stop=True), "pe")
                L.wait(dv, "dv", "pe", bc)
                vrb = L.inc(dv.tensor_copy(rb_sb, ps[7][:]), "dv")
                bank_free[7] = ("dv", vrb)
                L.wait(dv, "dv", "dv", vrb)
                L.wait(dv, "dv", f"g{qb % 2}", ot_g[qb % 2])
                ot = L.inc(dv.tensor_mul(otst[qb % 2][:], ps[6][:], rb_sb), "dv")
                bank_free[6] = ("dv", ot)
                L.wait(gd, "gd", "dv", ot)
                for qt in range(4):
                    ins2 = gd.dma_start(
                        out=ot_sc[qb * 4 + qt, h, :, :],
                        in_=otst[qb % 2][:, qt * 128:(qt + 1) * 128])
                    L.inc(ins2, f"g{qb % 2}", 16)
                ot_g[qb % 2] = L.cnt[f"g{qb % 2}"]

            pending = None
            for h in range(NH + 1):
                p1_slots = list(range(128)) if h < NH else []
                p1_i = 0
                if h == 0:
                    for slot in p1_slots:
                        emit_p1_slot(h, slot)
                    continue
                d_load_head(h + 1)
                h2 = h - 1
                for qb in range(4):
                    prev_ex = None
                    for kt in range(NT_SEQ):
                        ex = emit_p2_kt(h2, qb, kt)
                        if kt == 0 and pending is not None:
                            flush_tail(*pending)
                            pending = None
                        if kt > 0:
                            emit_p2_sums_av(h2, qb, kt - 1, prev_ex)
                        prev_ex = ex
                        if p1_i < len(p1_slots):
                            emit_p1_slot(h, p1_slots[p1_i])
                            p1_i += 1
                    emit_p2_sums_av(h2, qb, NT_SEQ - 1, prev_ex)
                    rc = emit_tail_dv(h2, qb)
                    pending = (h2, qb, rc)
                    if qb == 1:
                        d_apply_head(h + 1)
                while p1_i < len(p1_slots):
                    emit_p1_slot(h, p1_slots[p1_i])
                    p1_i += 1
                d_head_pe[h2] = L.cnt["pe"]
            flush_tail(*pending)
            pending = None

            if DEBUG_STOP == "D":
                return finish()
            exT = [b0b(0, 16384), b0b(16384, 16384)]
            wsl_e = [b1b(0, 16384), b1b(16384, 16384)]
            gslab = b0b(32768, 256)
            rtrow = b0[0:1, 36864:40960].bitcast(F32)
            fstage = [b0f(20480 + 512 * i, 512) for i in range(4)]
            barrier("sy", "gd", "pe", "dv", "ac")
            e_base_pe = L.cnt["pe"]
            gld = L.inc(gd.dma_start(
                out=gslab.rearrange("p (c f) -> p c f", f=NE),
                in_=W["gate_w"].rearrange("(c p) f -> p c f", p=128)), "gb", 16)
            spans = [("up", "up_w", i) for i in range(INT // 512)] + \
                    [("fc1", "fc1_w", i) for i in range(LAT // 512)]
            nsteps = len(spans) * 4
            ex_ld = {}
            step_pe = {}
            wld_e = {}
            e_ev = {}
            e_sq = {}
            e_store = {}
            rt_done = 0

            def e_load(step):
                if step >= nsteps:
                    return
                tb = step % 4
                txs = f"tx{step % 2}"
                L.wait(sy, "sy", "pe", step_pe.get(step - 2, e_base_pe))
                L.inc(sy.dma_start(
                    out=exT[step % 2][:].rearrange("p (c f) -> p c f", f=512),
                    in_=xnT_sc[:, :, tb * 512:(tb + 1) * 512].rearrange(
                        "c p f -> p c f")), txs, 16)
                ex_ld[step] = L.cnt[txs]

            def e_wload(sp):
                if sp >= len(spans):
                    return
                kind, wname, idx = spans[sp]
                wsem = f"w{sp % 2}"
                L.wait(gd, "gd", "pe", step_pe.get((sp - 2) * 4 + 3, e_base_pe))
                wld_e[sp] = L.inc(gd.dma_start(
                    out=wsl_e[sp % 2].rearrange("p (c f) -> p c f", f=512),
                    in_=W[wname][:, idx * 512:(idx + 1) * 512].rearrange(
                        "(c p) f -> p c f", p=128)), wsem, 16)

            e_load(0)
            e_load(1)
            e_wload(0)
            e_wload(1)
            e_grp = 0
            for sp, (kind, wname, idx) in enumerate(spans):
                wsem = f"w{sp % 2}"
                for tb in range(4):
                    step = sp * 4 + tb
                    txs = f"tx{step % 2}"
                    for it in range(4):
                        g = e_grp
                        bank = g % 4
                        L.wait(pe, "pe", "dv", e_ev.get(g - 4, 0))
                        L.wait(pe, "pe", wsem, wld_e[sp])
                        L.wait(pe, "pe", txs, ex_ld[step])
                        for ch in range(DC - 1):
                            pe.matmul(ps[bank][:],
                                      wsl_e[sp % 2][:, ch * 512 + it * 128:
                                                    ch * 512 + (it + 1) * 128],
                                      exT[step % 2][:, ch * 512:(ch + 1) * 512],
                                      start=(ch == 0), stop=False)
                        ch = DC - 1
                        mm = pe.matmul(ps[bank][:],
                                       wsl_e[sp % 2][:, ch * 512 + it * 128:
                                                     ch * 512 + (it + 1) * 128],
                                       exT[step % 2][:, ch * 512:(ch + 1) * 512],
                                       start=False, stop=True)
                        gend = L.inc(mm, "pe")
                        if kind == "up":
                            L.wait(dv, "dv", "pe", gend)
                            L.wait(dv, "dv", "ac", e_sq.get(bank, 0))
                            mx = L.inc(dv.tensor_scalar_max(fstage[bank],
                                                            ps[bank][:], 0.0), "dv")
                            e_ev[g] = mx
                            L.wait(ac, "ac", "dv", mx)
                            L.wait(ac, "ac", f"g{bank}", e_store.get(bank, 0))
                            sq = L.inc(ac.activation(stage[bank][:], fstage[bank],
                                                     AF.Square), "ac")
                            e_sq[bank] = sq
                            L.wait(gd, "gd", "ac", sq)
                            itg = idx * 4 + it
                            for j in range(4):
                                ins2 = gd.dma_start(
                                    out=u_sc[tb * 4 + j, itg, :, :],
                                    in_=stage[bank][:, j * 128:(j + 1) * 128])
                                L.inc(ins2, f"g{bank}", 16)
                            e_store[bank] = L.cnt[f"g{bank}"]
                        else:
                            L.wait(dv, "dv", "pe", gend)
                            L.wait(dv, "dv", f"g{bank}", e_store.get(bank, 0))
                            ev = L.inc(dv.tensor_copy(stage[bank][:], ps[bank][:]),
                                       "dv")
                            e_ev[g] = ev
                            L.wait(gd, "gd", "dv", ev)
                            lt = idx * 4 + it
                            ins2 = gd.dma_start(
                                out=xl_sc[lt * 128:(lt + 1) * 128,
                                          tb * 512:(tb + 1) * 512],
                                in_=stage[bank][:])
                            e_store[bank] = L.inc(ins2, f"g{bank}", 16)
                        e_grp += 1
                    if sp == 0:
                        for q in range(4):
                            tt = tb * 4 + q
                            L.wait(pe, "pe", "gb", gld)
                            L.wait(pe, "pe", "dv", rt_done)
                            for ch in range(DC - 1):
                                pe.matmul(ps[4][:, 0:8],
                                          exT[step % 2][:, ch * 512 + q * 128:
                                                        ch * 512 + (q + 1) * 128],
                                          gslab[:, ch * 8:(ch + 1) * 8],
                                          start=(ch == 0), stop=False)
                            ch = DC - 1
                            mm = pe.matmul(ps[4][:, 0:8],
                                           exT[step % 2][:, ch * 512 + q * 128:
                                                         ch * 512 + (q + 1) * 128],
                                           gslab[:, ch * 8:(ch + 1) * 8],
                                           start=False, stop=True)
                            gend = L.inc(mm, "pe")
                            L.wait(dv, "dv", "pe", gend)
                            cp = L.inc(dv.tensor_copy(lg8[:], ps[4][:, 0:8]), "dv")
                            L.wait(dv, "dv", "dv", cp)
                            r1 = L.inc(dv.reduce_max(S["m1"][:], lg8[:], axis=AX.X),
                                       "dv")
                            L.wait(dv, "dv", "dv", r1)
                            vm8 = L.inc(dv.tensor_copy(m8[:, 0:1], S["m1"][:]), "dv")
                            L.wait(dv, "dv", "dv", vm8)
                            mr = L.inc(dv.match_replace(g2[:], m8[:], lg8[:], -1e30),
                                       "dv")
                            L.wait(dv, "dv", "dv", mr)
                            r2 = L.inc(dv.reduce_max(S["m2"][:], g2[:], axis=AX.X),
                                       "dv")
                            L.wait(dv, "dv", "dv", r2)
                            rt = L.inc(dv.tensor_add(S["rt1"][:], S["m1"][:],
                                                     S["m2"][:]), "dv")
                            L.wait(pe, "pe", "dv", rt)
                            L.wait(pe, "pe", "gc", 2)
                            tr = L.inc(pe.transpose(ps[7][0:1, 0:128], S["rt1"][:],
                                                    ident_f[:]), "pe")
                            L.wait(dv, "dv", "pe", tr)
                            rt_done = L.inc(
                                dv.tensor_copy(rtrow[:, tt * 128:(tt + 1) * 128],
                                               ps[7][0:1, 0:128]), "dv")
                    step_pe[step] = L.cnt["pe"]
                    e_load(step + 2)
                e_wload(sp + 2)
            L.wait(sy, "sy", "dv", rt_done)
            rt_st = L.inc(sy.dma_start(out=rt_sc, in_=rtrow), "sx0", 16)

            if DEBUG_STOP == "E":
                return finish()
            xlat = b0b(0, 32768)
            eslab = b1b(0, 32768)
            rb_full = b0f(16384, 2048)
            barrier("sy", "gd", "pe", "dv")
            xld = L.inc(sy.dma_start(
                out=xlat.rearrange("p (c t) -> p c t", t=TOK),
                in_=xl_sc.rearrange("(c p) t -> p c t", p=128)), "lx0", 16)
            rld = L.inc(sy.dma_start(out=rb_full,
                                     in_=rt_sc.partition_broadcast(128)), "lx1", 16)
            eld = L.inc(gd.dma_start(
                out=eslab.rearrange("p (c m) -> p c m", m=LAT),
                in_=W["expert_w"].rearrange("(c p) m -> p c m", p=128)), "w0", 16)
            f_grp = 0
            f_ev = {}
            f_store = {}
            for mt in range(LC):
                for tb in range(4):
                    g = f_grp
                    bank = g % 4
                    L.wait(pe, "pe", "dv", f_ev.get(g - 4, 0))
                    L.wait(pe, "pe", "lx0", xld)
                    L.wait(pe, "pe", "w0", eld)
                    gend = None
                    for ch in range(LC):
                        mm = pe.matmul(ps[bank][:],
                                       eslab[:, ch * 2048 + mt * 128:
                                             ch * 2048 + (mt + 1) * 128],
                                       xlat[:, ch * 2048 + tb * 512:
                                            ch * 2048 + (tb + 1) * 512],
                                       start=(ch == 0), stop=(ch == LC - 1))
                    gend = L.inc(mm, "pe")
                    L.wait(dv, "dv", "pe", gend)
                    L.wait(dv, "dv", "lx1", rld)
                    L.wait(dv, "dv", f"g{bank}", f_store.get(bank, 0))
                    ev = L.inc(dv.scalar_tensor_tensor(
                        stage[bank][:], ps[bank][:], 1.0,
                        rb_full[:, tb * 512:(tb + 1) * 512],
                        OP.mult, OP.mult), "dv")
                    f_ev[g] = ev
                    L.wait(gd, "gd", "dv", ev)
                    for j in range(4):
                        ins2 = gd.dma_start(out=moe_sc[tb * 4 + j, mt, :, :],
                                            in_=stage[bank][:, j * 128:(j + 1) * 128])
                        L.inc(ins2, f"g{bank}", 16)
                    f_store[bank] = L.cnt[f"g{bank}"]
                    f_grp += 1

            if DEBUG_STOP == "F":
                return finish()
            dslab = b0b(0, 32768)
            f2slab = b0b(32768, 8192)
            woslab = b0b(40960, 16384)
            fstage = [b0f(28672 + 512 * i, 512) for i in range(4)]
            u_tt = [b1b(0, 8192), b1b(8192, 8192)]
            moe_tt = [b1b(16384, 2048), b1b(18432, 2048)]
            ot_tt = [b1b(20480, 4096), b1b(24576, 4096)]
            barrier("sy", "gd", "pe", "dv")
            g_base_pe = L.cnt["pe"]
            g_pe = {}
            g_ev = {}
            g_st = [0, 0]
            g_ld = {}
            for fb in range(8):
                L.wait(gd, "gd", "pe", g_pe.get((fb - 1, NT_OWN - 1), g_base_pe))
                L.inc(gd.dma_start(
                    out=dslab[:, 0:16384].rearrange("p (c f) -> p c f", f=512),
                    in_=W["down_w"][0:4096, fb * 512:(fb + 1) * 512].rearrange(
                        "(c p) f -> p c f", p=128)), "w0", 16)
                wl0 = L.inc(gd.dma_start(
                    out=f2slab.rearrange("p (c f) -> p c f", f=512),
                    in_=W["fc2_w"][:, fb * 512:(fb + 1) * 512].rearrange(
                        "(c p) f -> p c f", p=128)), "w0", 16)
                L.inc(gd.dma_start(
                    out=dslab[:, 16384:32768].rearrange("p (c f) -> p c f", f=512),
                    in_=W["down_w"][4096:8192, fb * 512:(fb + 1) * 512].rearrange(
                        "(c p) f -> p c f", p=128)), "w1", 16)
                wl = L.inc(gd.dma_start(
                    out=woslab.rearrange("p (c f) -> p c f", f=512),
                    in_=W["Wo"][:, fb * 512:(fb + 1) * 512].rearrange(
                        "(c p) f -> p c f", p=128)), "w1", 16)
                bld = L.inc(gd.dma_start(
                    out=rowf[:], in_=VEC["bo"][0:1, fb * 512:(fb + 1) * 512]),
                    "gb", 16)
                L.wait(dv, "dv", "gb", bld)
                brc = L.inc(dv.tensor_copy(rowfr[:], rowf[:]), "dv")
                for tt in range(NT_OWN):
                    p = tt % 2
                    lxs = f"lx{p}"
                    L.wait(sy, "sy", "pe", g_pe.get((fb, tt - 2),
                                                    g_pe.get((fb - 1, NT_OWN - 1),
                                                             g_base_pe)))
                    L.inc(sy.dma_start(
                        out=u_tt[p].rearrange("p (c j) -> p c j", j=128),
                        in_=u_sc[tt].rearrange("c p j -> p c j")), lxs, 16)
                    L.inc(sy.dma_start(
                        out=moe_tt[p].rearrange("p (c j) -> p c j", j=128),
                        in_=moe_sc[tt].rearrange("c p j -> p c j")), lxs, 16)
                    l3 = L.inc(sy.dma_start(
                        out=ot_tt[p].rearrange("p (c j) -> p c j", j=128),
                        in_=ot_sc[tt].rearrange("c p j -> p c j")), lxs, 16)
                    zbank = 4 + p
                    abank = 6 + p
                    L.wait(pe, "pe", lxs, l3)
                    L.wait(pe, "pe", "w0", wl0)
                    L.wait(pe, "pe", "dv", max(g_ev.get((fb, tt - 2), 0), brc))
                    for ch in range(IC // 2):
                        pe.matmul(ps[zbank][:],
                                  u_tt[p][:, ch * 128:(ch + 1) * 128],
                                  dslab[:, ch * 512:(ch + 1) * 512],
                                  start=(ch == 0), stop=False)
                    L.wait(pe, "pe", "w1", wl)
                    for ch in range(IC // 2, IC):
                        pe.matmul(ps[zbank][:],
                                  u_tt[p][:, ch * 128:(ch + 1) * 128],
                                  dslab[:, ch * 512:(ch + 1) * 512],
                                  start=False, stop=False)
                    for ch in range(LC):
                        mm = pe.matmul(ps[zbank][:],
                                       moe_tt[p][:, ch * 128:(ch + 1) * 128],
                                       f2slab[:, ch * 512:(ch + 1) * 512],
                                       start=False, stop=(ch == LC - 1))
                    zend = L.inc(mm, "pe")
                    for ch in range(NH):
                        pe.matmul(ps[abank][:],
                                  ot_tt[p][:, ch * 128:(ch + 1) * 128],
                                  woslab[:, ch * 512:(ch + 1) * 512],
                                  start=(ch == 0), stop=False)
                    mm = pe.matmul(ps[abank][:], ones_fr[:], rowfr[:],
                                   start=False, stop=True)
                    aend = L.inc(mm, "pe")
                    g_pe[(fb, tt)] = aend
                    L.wait(dv, "dv", "pe", aend)
                    L.wait(dv, "dv", f"sx{p}", g_st[p])
                    dv.tensor_copy(fstage[p], ps[zbank][:])
                    ev = L.inc(dv.tensor_copy(fstage[2 + p], ps[abank][:]), "dv")
                    g_ev[(fb, tt)] = ev
                    L.wait(ac, "ac", "dv", ev)
                    L.inc(ac.dma_start(out=z_sc[tt * 128:(tt + 1) * 128,
                                                fb * 512:(fb + 1) * 512],
                                       in_=fstage[p]), f"sx{p}", 16)
                    L.inc(ac.dma_start(out=at_sc[tt * 128:(tt + 1) * 128,
                                                 fb * 512:(fb + 1) * 512],
                                       in_=fstage[2 + p]), f"sx{p}", 16)
                    g_st[p] = L.cnt[f"sx{p}"]

            if DEBUG_STOP == "G":
                return finish()
            zrow = [b0f(0, 4096), b0f(4096, 4096)]
            arow = [b0f(8192, 4096), b0f(12288, 4096)]
            xrow = [b0f(16384, 4096), b0f(20480, 4096)]
            sqrow = b0f(24576, 4096)
            lnrow = b0f(28672, 4096)
            gbcH = b1f(0, 4096)
            gbcH2 = b1f(4096, 4096)
            barrier("sy", "dv", "ac")
            gl1 = L.inc(sy.dma_start(out=gbcH,
                                     in_=VEC["ln_g"].partition_broadcast(128)),
                        "tx0", 16)
            gl2 = L.inc(sy.dma_start(out=gbcH2,
                                     in_=VEC["ln_b"].partition_broadcast(128)),
                        "tx0", 16)
            h_out = {}
            h_st = [0, 0]
            for tt in range(NT_OWN):
                p = tt % 2
                lxs = f"lx{p}"
                if tt >= 2:
                    L.wait(sy, "sy", "dv", h_out[tt - 2])
                    L.wait(sy, "sy", "so", h_st[0])
                L.inc(sy.dma_start(out=zrow[p],
                                   in_=z_sc[tt * 128:(tt + 1) * 128, :]), lxs, 16)
                L.inc(sy.dma_start(out=arow[p],
                                   in_=at_sc[tt * 128:(tt + 1) * 128, :]), lxs, 16)
                l3 = L.inc(sy.dma_start(out=xrow[p],
                                        in_=x_own[tt * 128:(tt + 1) * 128, :]),
                           lxs, 16)
                stats, sd = ln_stats(zrow[p], sqrow, lxs, l3,
                                     h_out.get(tt - 1, 0))
                L.wait(ac, "ac", "dv", stats)
                L.wait(ac, "ac", "ac", sd)
                ln = L.inc(ac.activation(lnrow, zrow[p], AF.Identity,
                                         bias=S["nbias"][:], scale=S["rstd"][:]),
                           "ac")
                L.wait(dv, "dv", "ac", ln)
                L.wait(dv, "dv", "tx0", gl2)
                w1 = L.inc(dv.tensor_mul(sqrow, lnrow, gbcH), "dv")
                L.wait(dv, "dv", "dv", w1)
                w2 = L.inc(dv.tensor_add(lnrow, sqrow, gbcH2), "dv")
                L.wait(dv, "dv", "dv", w2)
                w3 = L.inc(dv.tensor_add(sqrow, lnrow, arow[p]), "dv")
                L.wait(dv, "dv", "dv", w3)
                fo = L.inc(dv.tensor_add(zrow[p], sqrow, xrow[p]), "dv")
                h_out[tt] = fo
                L.wait(sy, "sy", "dv", fo)
                L.inc(sy.dma_start(out=out_hbm[tt * 128:(tt + 1) * 128, :],
                                   in_=zrow[p]), "so", 16)
                h_st[0] = L.cnt["so"]
            return finish()

    return nc


def kernel(**inputs):
    nc = build()
    in_maps = []
    x = np.ascontiguousarray(inputs["L0_attn_hidden_states"])
    row = lambda a: np.ascontiguousarray(np.asarray(a, dtype=np.float32).reshape(1, -1))
    base = {
        "Wq": inputs["L0_attn_Wq"], "Wk": inputs["L0_attn_Wk"],
        "Wv": inputs["L0_attn_Wv"], "Wo": inputs["L0_attn_Wo"],
        "bq": row(inputs["L0_attn_bq"]), "bk": row(inputs["L0_attn_bk"]),
        "bv": row(inputs["L0_attn_bv"]), "bo": row(inputs["L0_attn_bo"]),
        "gq": row(inputs["L0_attn_gq"]), "gk": row(inputs["L0_attn_gk"]),
        "gate_w": inputs["L0_ffn_gate_w"], "up_w": inputs["L0_ffn_up_w"],
        "down_w": inputs["L0_ffn_down_w"], "fc1_w": inputs["L0_ffn_fc1_w"],
        "fc2_w": inputs["L0_ffn_fc2_w"], "expert_w": inputs["L0_ffn_expert_w"],
        "ln_g": row(inputs["L0_ffn_ln_gamma"]), "ln_b": row(inputs["L0_ffn_ln_beta"]),
    }
    base = {k: np.ascontiguousarray(np.asarray(v, dtype=np.float32))
            for k, v in base.items()}
    for c in range(8):
        b, half = c // 2, c % 2
        m = dict(base)
        m["x_own"] = np.ascontiguousarray(x[b, half * TOK:(half + 1) * TOK, :])
        m["x_oth"] = np.ascontiguousarray(x[b, (1 - half) * TOK:(2 - half) * TOK, :])
        in_maps.append(m)
    res = run_bass_kernel_spmd(nc, in_maps, core_ids=list(range(8)))
    outs = [res.results[c]["out"] for c in range(8)]
    full = np.empty((B, SEQ, D), dtype=np.float32)
    for c in range(8):
        b, half = c // 2, c % 2
        full[b, half * TOK:(half + 1) * TOK, :] = outs[c]
    return full


# version 05
# speedup vs baseline: 1.2843x; 1.0863x over previous
import contextlib
import numpy as np
import concourse.bass as bass
import concourse.mybir as mybir
from concourse.bass_utils import run_bass_kernel_spmd

F32 = mybir.dt.float32
F32R = mybir.dt.float32r
BF16 = mybir.dt.bfloat16
AF = mybir.ActivationFunctionType
AX = mybir.AxisListType
OP = mybir.AluOpType

B, SEQ, D = 4, 4096, 4096
NH, HD = 32, 128
INT, LAT, NE = 8192, 2048, 8
TOK = 2048
DC = D // 128
IC = INT // 128
LC = LAT // 128
NT_OWN = TOK // 128
NT_SEQ = SEQ // 128
LN_EPS = 1e-5
RMS_EPS = 1e-6
FLN_EPS = 1e-5
ISQ = float(1.0 / np.sqrt(HD))

DEBUG_STOP = None
DEBUG_TAPS = ()

DMA_SEMS = ("lx0", "lx1", "lx2", "sx0", "sx1", "sb0", "sb1", "sb2", "sb3",
            "tx0", "tx1", "w0", "w1", "gb", "g0", "g1", "g2", "g3", "so")
ALL_SEMS = DMA_SEMS + ("pe", "dv", "ac", "gc")


class Led:
    def __init__(self, nc, ctx):
        self.nc, self.ctx = nc, ctx
        self.sem, self.cnt, self.hw = {}, {}, {}

    def mk(self, *names):
        for n in names:
            self.sem[n] = self.ctx.enter_context(self.nc.semaphore(n))
            self.cnt[n] = 0

    def inc(self, instr, s, n=1):
        self.cnt[s] += n
        instr.then_inc(self.sem[s], n)
        return self.cnt[s]

    def wait(self, eng, ekey, s, val=None):
        if val is None:
            val = self.cnt[s]
        if val <= 0:
            return
        assert val <= self.cnt[s], f"wait {s}>={val} but only {self.cnt[s]} planned"
        if self.hw.get((ekey, s), 0) < val:
            eng.wait_ge(self.sem[s], val)
            self.hw[(ekey, s)] = val


def build():
    nc = bass.Bass("TRN2", target_bir_lowering=False)

    x_own = nc.dram_tensor("x_own", [TOK, D], F32, kind="ExternalInput").ap()
    x_oth = nc.dram_tensor("x_oth", [TOK, D], F32, kind="ExternalInput").ap()
    W = {}
    for nm, shp in (("Wq", [D, D]), ("Wk", [D, D]), ("Wv", [D, D]), ("Wo", [D, D]),
                    ("gate_w", [D, NE]), ("up_w", [D, INT]), ("down_w", [INT, D]),
                    ("fc1_w", [D, LAT]), ("fc2_w", [LAT, D]), ("expert_w", [LAT, LAT])):
        W[nm] = nc.dram_tensor(nm, shp, F32, kind="ExternalInput").ap()
    VEC = {}
    for nm in ("bq", "bk", "bv", "bo", "gq", "gk", "ln_g", "ln_b"):
        VEC[nm] = nc.dram_tensor(nm, [1, D], F32, kind="ExternalInput").ap()
    out_hbm = nc.dram_tensor("out", [TOK, D], F32, kind="ExternalOutput").ap()

    xn_sc = nc.dram_tensor("xn_sc", [SEQ, D], BF16).ap()
    xnT_sc = nc.dram_tensor("xnT_sc", [DC, 128, SEQ], BF16).ap()
    qr_sc = nc.dram_tensor("qr_sc", [TOK, D], BF16).ap()
    kr_sc = nc.dram_tensor("kr_sc", [SEQ, D], BF16).ap()
    qn_sc = nc.dram_tensor("qn_sc", [TOK, D], BF16).ap()
    kn_sc = nc.dram_tensor("kn_sc", [SEQ, D], BF16).ap()
    v_sc = nc.dram_tensor("v_sc", [SEQ, D], BF16).ap()
    ot_sc = nc.dram_tensor("ot_sc", [NT_OWN, NH, 128, 128], BF16).ap()
    u_sc = nc.dram_tensor("u_sc", [NT_OWN, IC, 128, 128], BF16).ap()
    xl_sc = nc.dram_tensor("xl_sc", [LAT, TOK], BF16).ap()
    moe_sc = nc.dram_tensor("moe_sc", [NT_OWN, LC, 128, 128], BF16).ap()
    rt_sc = nc.dram_tensor("rt_sc", [1, TOK], F32).ap()
    rstd_sc = nc.dram_tensor("rstd_sc", [2, 1, SEQ], F32).ap()
    z_sc = nc.dram_tensor("z_sc", [TOK, D], F32).ap()
    at_sc = nc.dram_tensor("at_sc", [TOK, D], F32).ap()
    scratch = {"xn_sc": xn_sc, "qr_sc": qr_sc, "kr_sc": kr_sc, "qn_sc": qn_sc,
               "kn_sc": kn_sc, "v_sc": v_sc, "xl_sc": xl_sc, "rt_sc": rt_sc,
               "z_sc": z_sc, "at_sc": at_sc,
               "ot_sc": ot_sc, "u_sc": u_sc, "moe_sc": moe_sc}
    dbg = {nm: nc.dram_tensor("dbg_" + nm, list(scratch[nm].shape), scratch[nm].dtype,
                              kind="ExternalOutput").ap() for nm in DEBUG_TAPS}

    ctx = contextlib.ExitStack()
    with ctx:
        L = Led(nc, ctx)
        L.mk(*ALL_SEMS)

        def sbuf(name, shape, dt):
            return ctx.enter_context(nc.sbuf_tensor(name, shape, dt))

        b0 = sbuf("b0", [128, 65536], BF16)
        b1 = sbuf("b1", [128, 32768], BF16)
        stage = [sbuf(f"stage{i}", [128, 512], BF16) for i in range(4)]
        ones_fr = sbuf("ones_fr", [1, 128], F32R)
        ones_f = sbuf("ones_f", [1, 128], F32)
        onescol_bf = sbuf("onescol_bf", [128, 1], BF16)
        ones_bfr = sbuf("ones_bfr", [1, 128], BF16)
        ident_f = sbuf("ident_f", [128, 128], F32)
        rowf = sbuf("rowf", [1, 512], F32)
        rowfr = sbuf("rowfr", [1, 1024], F32R)
        rs_row = sbuf("rs_row", [1, 512], F32R)
        acc_q = sbuf("acc_q", [128, NT_OWN], F32)
        acc_k = sbuf("acc_k", [128, NT_SEQ], F32)
        rcol = sbuf("rcol", [128, NT_SEQ], F32)
        gq_cols = sbuf("gq_cols", [128, NH], F32)
        gk_cols = sbuf("gk_cols", [128, NH], F32)
        m8 = sbuf("m8", [128, 8], F32)
        eps5 = sbuf("eps5", [128, 1], F32)
        eps6 = sbuf("eps6", [128, 1], F32)
        lnisq = sbuf("lnisq", [128, 1], F32)
        zero1 = sbuf("zero1", [128, 1], F32)
        lg8 = sbuf("lg8", [128, 8], F32)
        g2 = sbuf("g2", [128, 8], F32)
        small = {nm: sbuf(nm, [128, 1], F32) for nm in
                 ("s1", "s2", "mu", "msq", "var", "sd", "rstd", "nbias",
                  "m1", "m2", "rt1")}
        mparts = sbuf("mparts", [128, 8], F32)
        negm = sbuf("negm", [128, 1], F32)
        S = small

        def b0f(lo, n):
            return b0[:].bitcast(F32)[:, lo:lo + n]

        def b0b(lo, n):
            return b0[:, lo:lo + n]

        def b1f(lo, n):
            return b1[:].bitcast(F32)[:, lo:lo + n]

        def b1b(lo, n):
            return b1[:, lo:lo + n]

        ps = [ctx.enter_context(nc.psum_tensor(f"ps{i}", [128, 512], F32))
              for i in range(8)]

        block = ctx.enter_context(nc.Block())

        @block.sync
        def _(sy):
            gd = nc.gpsimd
            pe = nc.tensor
            dv = nc.vector
            ac = nc.scalar
            ENG = {"pe": pe, "dv": dv, "ac": ac, "gd": gd, "sy": sy}

            def barrier(*ekeys):
                for ek in ekeys:
                    e = ENG[ek]
                    for s in ALL_SEMS:
                        if s == "so":
                            continue
                        L.wait(e, ek, s, L.cnt[s])

            gd.memset(ones_f[:], 1.0)
            gd.memset(onescol_bf[:], 1.0)
            v0 = L.inc(gd.memset(ident_f[:], 0.0), "gc")
            L.wait(gd, "gd", "gc", v0)
            ins = gd.affine_select(out=ident_f[:], in_=ident_f[:],
                                   compare_op=OP.not_equal, fill=1.0, base=0,
                                   pattern=[[-1, 128]], channel_multiplier=1)
            L.inc(ins, "gc")
            dv.memset(ones_bfr[:], 1.0)
            dv.memset(m8[:], 3.0e38)
            dv.memset(eps5[:], LN_EPS)
            dv.memset(eps6[:], RMS_EPS)
            dv.memset(lnisq[:], float(np.log(ISQ)))
            dv.memset(zero1[:], 0.0)
            dv.memset(acc_q[:], 0.0)
            dv.memset(acc_k[:], 0.0)
            L.wait(dv, "dv", "gc", 1)
            L.inc(dv.tensor_copy(ones_fr[:], ones_f[:]), "dv")
            init_dv = L.cnt["dv"]

            def finish():
                for s in ALL_SEMS:
                    if s != "so":
                        L.wait(sy, "sy", s, L.cnt[s])
                for nm, dap in dbg.items():
                    L.inc(sy.dma_start(out=dap, in_=scratch[nm]), "so", 16)
                L.wait(sy, "sy", "so", L.cnt["so"])

            def ln_stats(rowx, rowsq, ld_sem, ld_val, sq_guard=0):
                L.wait(dv, "dv", ld_sem, ld_val)
                dv.reduce_sum(S["s1"][:], rowx, axis=AX.X)
                L.wait(ac, "ac", ld_sem, ld_val)
                if sq_guard:
                    L.wait(ac, "ac", "dv", sq_guard)
                sq = L.inc(ac.activation(rowsq, rowx, AF.Square), "ac")
                L.wait(dv, "dv", "ac", sq)
                sed = L.inc(dv.reduce_sum(S["s2"][:], rowsq, axis=AX.X), "dv")
                L.wait(dv, "dv", "dv", sed)
                dv.tensor_scalar_mul(S["mu"][:], S["s1"][:], 1.0 / D)
                v1 = L.inc(dv.tensor_scalar_mul(S["s2"][:], S["s2"][:], 1.0 / D), "dv")
                L.wait(dv, "dv", "dv", v1)
                v2 = L.inc(dv.tensor_mul(S["msq"][:], S["mu"][:], S["mu"][:]), "dv")
                L.wait(dv, "dv", "dv", v2)
                st = L.inc(dv.tensor_sub(S["var"][:], S["s2"][:], S["msq"][:]), "dv")
                L.wait(ac, "ac", "dv", st)
                L.wait(ac, "ac", "dv", init_dv)
                lnv = L.inc(ac.activation(S["sd"][:], S["var"][:], AF.Ln,
                                          bias=eps5[:]), "ac")
                L.wait(ac, "ac", "ac", lnv)
                sd = L.inc(ac.activation(S["rstd"][:], S["sd"][:], AF.Exp,
                                         scale=-0.5, bias=zero1[:]), "ac")
                L.wait(dv, "dv", "ac", sd)
                v3 = L.inc(dv.tensor_mul(S["nbias"][:], S["mu"][:], S["rstd"][:]), "dv")
                L.wait(dv, "dv", "dv", v3)
                stats = L.inc(dv.tensor_scalar_mul(S["nbias"][:], S["nbias"][:],
                                                   -1.0), "dv")
                return stats, sd

            rax = [b0f(0, 4096), b0f(4096, 4096)]
            rsq = [b0f(8192, 4096), b0f(12288, 4096)]
            rno = [b0b(32768, 4096), b0b(36864, 4096)]
            rax_g = [0, 0]
            rsq_g = [0, 0]
            rno_g = [0, 0]
            for i in range(NT_SEQ):
                p = i % 2
                src = x_own if i < NT_OWN else x_oth
                r0 = (i % NT_OWN) * 128
                lxs = f"lx{p}"
                L.wait(sy, "sy", "ac", rax_g[p])
                ld = L.inc(sy.dma_start(out=rax[p], in_=src[r0:r0 + 128, :]), lxs, 16)
                stats, sd = ln_stats(rax[p], rsq[p], lxs, ld, rsq_g[p])
                rsq_g[p] = L.cnt["dv"]
                L.wait(ac, "ac", "dv", stats)
                L.wait(ac, "ac", "ac", sd)
                L.wait(ac, "ac", f"sx{p}", rno_g[p])
                nm_ = L.inc(ac.activation(rno[p], rax[p], AF.Identity,
                                          bias=S["nbias"][:], scale=S["rstd"][:]),
                            "ac")
                rax_g[p] = nm_
                L.wait(sy, "sy", "ac", nm_)
                rno_g[p] = L.inc(
                    sy.dma_start(out=xn_sc[i * 128:(i + 1) * 128, :], in_=rno[p]),
                    f"sx{p}", 16)

            if DEBUG_STOP == "A":
                return finish()
            xnT = b0[:]
            barrier("sy", "gd", "pe", "dv", "ac")
            qkv_grp = 0
            grp_ev = {}
            b_store = {}
            jobs_by_half = {0: [("Wk", "bk"), ("Wv", "bv"), ("Wq", "bq")],
                            1: [("Wk", "bk"), ("Wv", "bv")]}
            jobidx = 0
            slab_use = {}
            last_rowfr_pe = 0
            grp_sq = {}
            last_rr = [0]
            last_ad = [0]
            for half in (0, 1):
                if half == 1:
                    L.wait(sy, "sy", "pe", L.cnt["pe"])
                    L.wait(sy, "sy", "sx0", L.cnt["sx0"])
                for ch in range(DC):
                    L.inc(sy.dma_start(
                        out=xnT[:, ch * TOK:(ch + 1) * TOK],
                        in_=xn_sc[half * TOK:(half + 1) * TOK,
                                  ch * 128:(ch + 1) * 128],
                        transpose=True), "tx0", 16)
                xnT_ld = L.cnt["tx0"]
                if half == 0:
                    L.wait(sy, "sy", "tx0", xnT_ld)
                    for ch in range(DC):
                        L.inc(sy.dma_start(
                            out=xnT_sc[ch, :, 0:TOK],
                            in_=xnT[:, ch * TOK:(ch + 1) * TOK]), "sx0", 16)
                for wnm, bnm in jobs_by_half[half]:
                    for ib in range(8):
                        wsem = f"w{jobidx % 2}"
                        if jobidx >= 2:
                            L.wait(gd, "gd", "pe", slab_use[jobidx - 2])
                        wsl = b1b(0, 16384) if jobidx % 2 == 0 else b1b(16384, 16384)
                        wld = L.inc(gd.dma_start(
                            out=wsl.rearrange("p (c f) -> p c f", f=512),
                            in_=W[wnm][:, ib * 512:(ib + 1) * 512].rearrange(
                                "(c p) f -> p c f", p=128)), wsem, 16)
                        L.wait(gd, "gd", "pe", last_rowfr_pe)
                        bld = L.inc(gd.dma_start(
                            out=rowf[:],
                            in_=VEC[bnm][0:1, ib * 512:(ib + 1) * 512]), "gb", 16)
                        L.wait(dv, "dv", "gb", bld)
                        brc = L.inc(dv.tensor_copy(rowfr[0:1, 0:512], rowf[:]),
                                    "dv")
                        for tt in range(NT_OWN):
                            g = qkv_grp
                            bank = g % 4
                            L.wait(pe, "pe", "dv", max(grp_ev.get(g - 4, 0), brc))
                            L.wait(pe, "pe", "ac", grp_sq.get(g - 4, 0))
                            L.wait(pe, "pe", wsem, wld)
                            L.wait(pe, "pe", "tx0", xnT_ld)
                            for ch in range(DC):
                                pe.matmul(ps[bank][:],
                                          xnT[:, ch * TOK + tt * 128:
                                              ch * TOK + (tt + 1) * 128],
                                          wsl[:, ch * 512:(ch + 1) * 512],
                                          start=(ch == 0), stop=False)
                            mm = pe.matmul(ps[bank][:], ones_fr[:],
                                           rowfr[0:1, 0:512],
                                           start=False, stop=True)
                            gend = L.inc(mm, "pe")
                            L.wait(dv, "dv", "pe", gend)
                            L.wait(dv, "dv", f"sb{bank}", b_store.get(bank, 0))
                            ev = L.inc(dv.tensor_copy(stage[bank][:], ps[bank][:]),
                                       "dv")
                            grp_ev[g] = ev
                            if wnm in ("Wq", "Wk"):
                                L.wait(ac, "ac", "pe", gend)
                                L.wait(ac, "ac", "dv", max(ev, last_rr[0]))
                                aq = L.inc(ac.activation(ps[4][:], ps[bank][:],
                                                         AF.Square), "ac")
                                grp_sq[g] = aq
                                L.wait(dv, "dv", "ac", aq)
                                L.wait(dv, "dv", "dv", last_ad[0])
                                rr = L.inc(dv.reduce_sum(S["s2"][:], ps[4][:],
                                                         axis=AX.X), "dv")
                                last_rr[0] = rr
                                L.wait(dv, "dv", "dv", rr)
                                col = tt if wnm == "Wq" else half * NT_OWN + tt
                                accT = acc_q if wnm == "Wq" else acc_k
                                ad = L.inc(dv.tensor_add(accT[:, col:col + 1],
                                                         accT[:, col:col + 1],
                                                         S["s2"][:]), "dv")
                                last_ad[0] = ad
                            L.wait(sy, "sy", "dv", ev)
                            trow = half * TOK + tt * 128
                            if wnm == "Wq":
                                dst = qr_sc[tt * 128:tt * 128 + 128,
                                            ib * 512:(ib + 1) * 512]
                            elif wnm == "Wk":
                                dst = kr_sc[trow:trow + 128, ib * 512:(ib + 1) * 512]
                            else:
                                dst = v_sc[trow:trow + 128, ib * 512:(ib + 1) * 512]
                            b_store[bank] = L.inc(
                                sy.dma_start(out=dst, in_=stage[bank][:]),
                                f"sb{bank}", 16)
                            qkv_grp += 1
                        slab_use[jobidx] = L.cnt["pe"]
                        last_rowfr_pe = L.cnt["pe"]
                        jobidx += 1

            if DEBUG_STOP == "B":
                return finish()
            barrier("pe", "dv", "ac", "sy")
            rowb = b1[0:1, :].bitcast(F32)
            rowb_g = 0
            for jj, (accT, ncol, use_isq) in enumerate(
                    ((acc_q, NT_OWN, True), (acc_k, NT_SEQ, False))):
                v1 = L.inc(dv.tensor_scalar_mul(accT[:, 0:ncol], accT[:, 0:ncol],
                                                1.0 / D), "dv")
                L.wait(ac, "ac", "dv", v1)
                lnv = L.inc(ac.activation(rcol[:, 0:ncol], accT[:, 0:ncol],
                                          AF.Ln, bias=eps6[:]), "ac")
                L.wait(ac, "ac", "ac", lnv)
                ex = L.inc(ac.activation(rcol[:, 0:ncol], rcol[:, 0:ncol], AF.Exp,
                                         scale=-0.5,
                                         bias=(lnisq[:] if use_isq else zero1[:])),
                           "ac")
                for j in range(ncol):
                    L.wait(pe, "pe", "ac", ex)
                    L.wait(pe, "pe", "gc", 2)
                    L.wait(pe, "pe", "dv", L.cnt["dv"])
                    tr = L.inc(pe.transpose(ps[7][0:1, 0:128], rcol[:, j:j + 1],
                                            ident_f[:]), "pe")
                    L.wait(dv, "dv", "pe", tr)
                    L.wait(dv, "dv", "sx0", rowb_g)
                    L.inc(dv.tensor_copy(rowb[0:1, j * 128:(j + 1) * 128],
                                         ps[7][0:1, 0:128]), "dv")
                L.wait(sy, "sy", "dv", L.cnt["dv"])
                rowb_g = L.inc(sy.dma_start(out=rstd_sc[jj][0:1, 0:ncol * 128],
                                            in_=rowb[0:1, 0:ncol * 128]), "sx0", 16)

            KT = [b0b(0, 4096), b0b(4096, 4096), b0b(8192, 4096)]
            QT = [b0b(12288, 2048), b0b(14336, 2048), b0b(16384, 2048)]
            VS = [b0b(18432, 4096), b0b(22528, 4096), b0b(26624, 4096)]
            PT = [b0b(30720 + 512 * i, 512) for i in range(4)]
            otst = [b0b(32768, 512), b0b(33280, 512)]

            def nrow(hh, qb):
                off = 33792 + ((hh % 2) * 4 + qb) * 512
                return b0[0:1, off:off + 512]

            rb_sb = b0f(19456, 512)
            rqb = b0f(20480, 2048)
            rkb = b0f(24576, 4096)
            grow = b0[0:1, :].bitcast(F32)[0:1, 24576:28672]
            barrier("sy", "pe", "dv", "ac", "gd")
            gld1 = L.inc(sy.dma_start(out=grow[0:1, 0:4096], in_=VEC["gq"]),
                         "tx0", 16)
            gcl = 0
            for h in range(NH):
                L.wait(pe, "pe", "tx0", gld1)
                L.wait(pe, "pe", "gc", 2)
                L.wait(pe, "pe", "dv", gcl)
                tr = L.inc(pe.transpose(ps[7][0:128, 0:1],
                                        grow[0:1, h * 128:(h + 1) * 128],
                                        ident_f[0:1, 0:1]), "pe")
                L.wait(dv, "dv", "pe", tr)
                gcl = L.inc(dv.tensor_copy(gq_cols[:, h:h + 1], ps[7][0:128, 0:1]),
                            "dv")
            L.wait(sy, "sy", "dv", gcl)
            gld2 = L.inc(sy.dma_start(out=grow[0:1, 0:4096], in_=VEC["gk"]),
                         "tx0", 16)
            for h in range(NH):
                L.wait(pe, "pe", "tx0", gld2)
                L.wait(pe, "pe", "dv", gcl)
                tr = L.inc(pe.transpose(ps[7][0:128, 0:1],
                                        grow[0:1, h * 128:(h + 1) * 128],
                                        ident_f[0:1, 0:1]), "pe")
                L.wait(dv, "dv", "pe", tr)
                gcl = L.inc(dv.tensor_copy(gk_cols[:, h:h + 1], ps[7][0:128, 0:1]),
                            "dv")
            L.wait(sy, "sy", "dv", gcl)
            L.wait(sy, "sy", "sx0", L.cnt["sx0"])
            rbl = L.inc(sy.dma_start(out=rqb,
                                     in_=rstd_sc[0][0:1, 0:TOK]
                                     .partition_broadcast(128)), "tx0", 16)
            rbl = L.inc(sy.dma_start(out=rkb,
                                     in_=rstd_sc[1].partition_broadcast(128)),
                        "tx0", 16)
            d_head_pe = {-1: L.cnt["pe"], -2: L.cnt["pe"], -3: L.cnt["pe"]}
            d_loads = {}
            d_apply = {}
            bank_free = {b: ("dv", 0) for b in range(8)}
            bank_free[7] = ("dv", gcl)
            pt_free = [0, 0, 0, 0]
            nr_ready = {}
            nr_user = {}
            ot_g = [0, 0]
            p1_pending = []
            negm_tr = [0]

            def pe_bank(b):
                s, v = bank_free[b]
                L.wait(pe, "pe", s, v)

            def d_load_head(h):
                if h >= NH:
                    return
                hp = h % 3
                lxs = f"lx{hp}"
                L.wait(sy, "sy", "pe", d_head_pe[h - 3])
                L.inc(sy.dma_start(out=KT[hp][:, :],
                                   in_=kr_sc[:, h * 128:(h + 1) * 128],
                                   transpose=True), lxs, 16)
                L.inc(sy.dma_start(out=QT[hp][:, :],
                                   in_=qr_sc[:, h * 128:(h + 1) * 128],
                                   transpose=True), lxs, 16)
                for part in range(4):
                    L.inc(sy.dma_start(
                        out=VS[hp][:, part * 1024:(part + 1) * 1024].rearrange(
                            "p (kt j) -> p kt j", j=128),
                        in_=v_sc[part * 1024:(part + 1) * 1024,
                                 h * 128:(h + 1) * 128].rearrange(
                            "(kt p) j -> p kt j", p=128)), lxs, 16)
                d_loads[h] = L.cnt[lxs]

            def d_apply_head(h):
                if h >= NH:
                    return
                hp = h % 3
                L.wait(dv, "dv", f"lx{hp}", d_loads[h])
                L.wait(dv, "dv", "tx0", rbl)
                L.inc(dv.scalar_tensor_tensor(KT[hp][:, :], KT[hp][:, :],
                                              gk_cols[:, h:h + 1], rkb,
                                              OP.mult, OP.mult), "dv")
                ap = L.inc(dv.scalar_tensor_tensor(QT[hp][:, :], QT[hp][:, :],
                                                   gq_cols[:, h:h + 1], rqb,
                                                   OP.mult, OP.mult), "dv")
                d_apply[h] = ap

            d_load_head(0)
            d_apply_head(0)
            d_load_head(1)
            d_apply_head(1)

            def emit_p1_slot(h, slot):
                hp = h % 3
                qb, r = divmod(slot, 32)
                qt, kb = divmod(r, 8)
                qcol = (qb * 4 + qt) * 128
                bank = slot % 2
                pe_bank(bank)
                L.wait(pe, "pe", "dv", d_apply[h])
                mm = pe.matmul(ps[bank][:], QT[hp][:, qcol:qcol + 128],
                               KT[hp][:, kb * 512:(kb + 1) * 512],
                               start=True, stop=True)
                pend = L.inc(mm, "pe")
                L.wait(dv, "dv", "pe", pend)
                rr = L.inc(dv.reduce_max(mparts[:, kb:kb + 1], ps[bank][:],
                                         axis=AX.X), "dv")
                bank_free[bank] = ("dv", rr)
                if kb == 7:
                    L.wait(dv, "dv", "dv", rr)
                    L.wait(dv, "dv", "pe", negm_tr[0])
                    nm_ = L.inc(dv.reduce_max(negm[:], mparts[:], axis=AX.X,
                                              negate=True), "dv")
                    L.wait(dv, "dv", "dv", nm_)
                    p1_pending.append((h, qb, qt, nm_))

            def flush_p1(h, qb, qt, nm_):
                L.wait(pe, "pe", "dv", nm_)
                pe_bank(7)
                tr = L.inc(pe.transpose(ps[7][0:1, 0:128], negm[:],
                                        ident_f[:]), "pe")
                negm_tr[0] = tr
                L.wait(dv, "dv", "pe", tr)
                L.wait(dv, "dv", "pe", nr_user.get((h - 2, qb), 0))
                cp = L.inc(dv.tensor_copy(
                    nrow(h, qb)[:, qt * 128:(qt + 1) * 128],
                    ps[7][0:1, 0:128]), "dv")
                bank_free[7] = ("dv", cp)
                if qt == 3:
                    nr_ready[(h, qb)] = cp

            def maybe_flush_p1(slack=2):
                while p1_pending and p1_pending[0][3] <= L.cnt["dv"] - slack:
                    flush_p1(*p1_pending.pop(0))

            def drain_p1():
                while p1_pending:
                    flush_p1(*p1_pending.pop(0))

            def emit_p2_kt(h, qb, kt):
                hp = h % 3
                qcol = qb * 512
                bank = 2 + (kt % 3)
                pe_bank(bank)
                L.wait(pe, "pe", "dv", nr_ready[(h, qb)])
                pe.matmul(ps[bank][:], KT[hp][:, kt * 128:(kt + 1) * 128],
                          QT[hp][:, qcol:qcol + 512], start=True, stop=False)
                mm = pe.matmul(ps[bank][:], ones_bfr[:], nrow(h, qb),
                               start=False, stop=True)
                gend = L.inc(mm, "pe")
                nr_user[(h, qb)] = gend
                L.wait(ac, "ac", "pe", max(gend, pt_free[kt % 4]))
                ex = L.inc(ac.activation(PT[kt % 4][:], ps[bank][:], AF.Exp), "ac")
                bank_free[bank] = ("ac", ex)
                return ex

            def emit_p2_sums_av(h, qb, kt, ex):
                hp = h % 3
                L.wait(pe, "pe", "ac", ex)
                if kt == 0:
                    pe_bank(5)
                    pe_bank(6)
                pe.matmul(ps[5][0:1, :], onescol_bf[:], PT[kt % 4][:],
                          start=(kt == 0), stop=(kt == NT_SEQ - 1))
                mm = pe.matmul(ps[6][:], VS[hp][:, kt * 128:(kt + 1) * 128],
                               PT[kt % 4][:],
                               start=(kt == 0), stop=(kt == NT_SEQ - 1))
                pt_free[kt % 4] = L.inc(mm, "pe")

            def emit_tail_dv(h, qb):
                avend = L.cnt["pe"]
                L.wait(dv, "dv", "pe", avend)
                rc0 = L.inc(dv.reciprocal(rowf[:], ps[5][0:1, :]), "dv")
                bank_free[5] = ("dv", rc0)
                L.wait(dv, "dv", "dv", rc0)
                rc = L.inc(dv.tensor_copy(rs_row[:], rowf[:]), "dv")
                L.wait(dv, "dv", "dv", rc)
                return rc

            def flush_tail(h, qb, rc):
                L.wait(pe, "pe", "dv", rc)
                pe_bank(7)
                bc = L.inc(pe.matmul(ps[7][:], ones_fr[:], rs_row[:],
                                     start=True, stop=True), "pe")
                L.wait(dv, "dv", "pe", bc)
                vrb = L.inc(dv.tensor_copy(rb_sb, ps[7][:]), "dv")
                bank_free[7] = ("dv", vrb)
                L.wait(dv, "dv", "dv", vrb)
                L.wait(dv, "dv", f"g{qb % 2}", ot_g[qb % 2])
                ot = L.inc(dv.tensor_mul(otst[qb % 2][:], ps[6][:], rb_sb), "dv")
                bank_free[6] = ("dv", ot)
                L.wait(gd, "gd", "dv", ot)
                for qt in range(4):
                    ins2 = gd.dma_start(
                        out=ot_sc[qb * 4 + qt, h, :, :],
                        in_=otst[qb % 2][:, qt * 128:(qt + 1) * 128])
                    L.inc(ins2, f"g{qb % 2}", 16)
                ot_g[qb % 2] = L.cnt[f"g{qb % 2}"]

            pending = None
            for h in range(NH + 1):
                p1_slots = list(range(128)) if h < NH else []
                p1_i = 0
                if h == 0:
                    for slot in p1_slots:
                        emit_p1_slot(h, slot)
                        maybe_flush_p1()
                    drain_p1()
                    continue
                d_load_head(h + 1)
                h2 = h - 1
                for qb in range(4):
                    prev_ex = None
                    for kt in range(NT_SEQ):
                        ex = emit_p2_kt(h2, qb, kt)
                        if kt == 0 and pending is not None:
                            flush_tail(*pending)
                            pending = None
                        if kt > 0:
                            emit_p2_sums_av(h2, qb, kt - 1, prev_ex)
                        prev_ex = ex
                        if p1_i < len(p1_slots):
                            emit_p1_slot(h, p1_slots[p1_i])
                            p1_i += 1
                        maybe_flush_p1()
                    emit_p2_sums_av(h2, qb, NT_SEQ - 1, prev_ex)
                    rc = emit_tail_dv(h2, qb)
                    pending = (h2, qb, rc)
                    if qb == 1:
                        d_apply_head(h + 1)
                while p1_i < len(p1_slots):
                    emit_p1_slot(h, p1_slots[p1_i])
                    p1_i += 1
                drain_p1()
                d_head_pe[h2] = L.cnt["pe"]
            flush_tail(*pending)
            pending = None

            if DEBUG_STOP == "D":
                return finish()
            wo2 = [b1b(0, 16384), b1b(16384, 16384)]
            ot2 = [b0b(0, 4096), b0b(4096, 4096)]
            fst2 = [b0f(4096, 512), b0f(4608, 512)]
            barrier("sy", "gd", "pe", "dv", "ac")
            d2_base = L.cnt["pe"]
            d2_pe = {}
            d2_ev = {}
            d2_st = [0, 0]
            d2_wl = {}
            d2_brc = {}

            def d2_wload(fb):
                if fb >= 8:
                    return
                L.wait(gd, "gd", "pe", d2_pe.get((fb - 2, NT_OWN - 1), d2_base))
                d2_wl[fb] = L.inc(gd.dma_start(
                    out=wo2[fb % 2].rearrange("p (c f) -> p c f", f=512),
                    in_=W["Wo"][:, fb * 512:(fb + 1) * 512].rearrange(
                        "(c p) f -> p c f", p=128)), f"w{fb % 2}", 16)
                L.wait(gd, "gd", "dv", d2_brc.get(fb - 1, 0))
                bld = L.inc(gd.dma_start(
                    out=rowf[:], in_=VEC["bo"][0:1, fb * 512:(fb + 1) * 512]),
                    "gb", 16)
                L.wait(dv, "dv", "gb", bld)
                L.wait(dv, "dv", "pe", d2_pe.get((fb - 2, NT_OWN - 1), d2_base))
                d2_brc[fb] = L.inc(dv.tensor_copy(
                    rowfr[0:1, (fb % 2) * 512:(fb % 2) * 512 + 512], rowf[:]), "dv")

            d2_wload(0)
            d2_wload(1)
            for fb in range(8):
                for tt in range(NT_OWN):
                    p = tt % 2
                    lxs = f"lx{p}"
                    L.wait(sy, "sy", "pe", d2_pe.get((fb, tt - 2),
                                                     d2_pe.get((fb - 1, NT_OWN - 1),
                                                               d2_base)))
                    l3 = L.inc(sy.dma_start(
                        out=ot2[p].rearrange("p (c j) -> p c j", j=128),
                        in_=ot_sc[tt].rearrange("c p j -> p c j")), lxs, 16)
                    bank = p
                    L.wait(pe, "pe", lxs, l3)
                    L.wait(pe, "pe", f"w{fb % 2}", d2_wl[fb])
                    prev_ev = d2_ev.get((fb, tt - 2),
                                        d2_ev.get((fb - 1, NT_OWN - 2 + tt), 0))
                    L.wait(pe, "pe", "dv", max(prev_ev, d2_brc[fb]))
                    for ch in range(NH):
                        pe.matmul(ps[bank][:],
                                  ot2[p][:, ch * 128:(ch + 1) * 128],
                                  wo2[fb % 2][:, ch * 512:(ch + 1) * 512],
                                  start=(ch == 0), stop=False)
                    mm = pe.matmul(ps[bank][:], ones_fr[:],
                                   rowfr[0:1, (fb % 2) * 512:(fb % 2) * 512 + 512],
                                   start=False, stop=True)
                    aend = L.inc(mm, "pe")
                    d2_pe[(fb, tt)] = aend
                    L.wait(dv, "dv", "pe", aend)
                    L.wait(dv, "dv", f"sx{p}", d2_st[p])
                    ev = L.inc(dv.tensor_copy(fst2[p], ps[bank][:]), "dv")
                    d2_ev[(fb, tt)] = ev
                    L.wait(ac, "ac", "dv", ev)
                    L.inc(ac.dma_start(out=at_sc[tt * 128:(tt + 1) * 128,
                                                 fb * 512:(fb + 1) * 512],
                                       in_=fst2[p]), f"sx{p}", 16)
                    d2_st[p] = L.cnt[f"sx{p}"]
                d2_wload(fb + 2)

            exT = [b0b(0, 16384), b0b(16384, 16384)]
            wsl_e = [b1b(0, 16384), b1b(16384, 16384)]
            gslab = b0b(32768, 256)
            rtrow = b0[0:1, 36864:40960].bitcast(F32)
            fstage = [b0f(20480 + 512 * i, 512) for i in range(4)]
            barrier("sy", "gd", "pe", "dv", "ac")
            e_base_pe = L.cnt["pe"]
            gld = L.inc(gd.dma_start(
                out=gslab.rearrange("p (c f) -> p c f", f=NE),
                in_=W["gate_w"].rearrange("(c p) f -> p c f", p=128)), "gb", 16)
            spans = [("up", "up_w", i) for i in range(INT // 512)] + \
                    [("fc1", "fc1_w", i) for i in range(LAT // 512)]
            nsteps = len(spans) * 4
            ex_ld = {}
            step_pe = {}
            wld_e = {}
            e_ev = {}
            e_sq = {}
            e_store = {}
            rt_done = 0

            def e_load(step):
                if step >= nsteps:
                    return
                tb = step % 4
                txs = f"tx{step % 2}"
                L.wait(sy, "sy", "pe", step_pe.get(step - 2, e_base_pe))
                L.inc(sy.dma_start(
                    out=exT[step % 2][:].rearrange("p (c f) -> p c f", f=512),
                    in_=xnT_sc[:, :, tb * 512:(tb + 1) * 512].rearrange(
                        "c p f -> p c f")), txs, 16)
                ex_ld[step] = L.cnt[txs]

            def e_wload(sp):
                if sp >= len(spans):
                    return
                kind, wname, idx = spans[sp]
                wsem = f"w{sp % 2}"
                L.wait(gd, "gd", "pe", step_pe.get((sp - 2) * 4 + 3, e_base_pe))
                wld_e[sp] = L.inc(gd.dma_start(
                    out=wsl_e[sp % 2].rearrange("p (c f) -> p c f", f=512),
                    in_=W[wname][:, idx * 512:(idx + 1) * 512].rearrange(
                        "(c p) f -> p c f", p=128)), wsem, 16)

            e_load(0)
            e_load(1)
            e_wload(0)
            e_wload(1)
            e_grp = 0
            for sp, (kind, wname, idx) in enumerate(spans):
                wsem = f"w{sp % 2}"
                for tb in range(4):
                    step = sp * 4 + tb
                    txs = f"tx{step % 2}"
                    for it in range(4):
                        g = e_grp
                        bank = g % 4
                        L.wait(pe, "pe", "dv", e_ev.get(g - 4, 0))
                        L.wait(pe, "pe", wsem, wld_e[sp])
                        L.wait(pe, "pe", txs, ex_ld[step])
                        for ch in range(DC - 1):
                            pe.matmul(ps[bank][:],
                                      wsl_e[sp % 2][:, ch * 512 + it * 128:
                                                    ch * 512 + (it + 1) * 128],
                                      exT[step % 2][:, ch * 512:(ch + 1) * 512],
                                      start=(ch == 0), stop=False)
                        ch = DC - 1
                        mm = pe.matmul(ps[bank][:],
                                       wsl_e[sp % 2][:, ch * 512 + it * 128:
                                                     ch * 512 + (it + 1) * 128],
                                       exT[step % 2][:, ch * 512:(ch + 1) * 512],
                                       start=False, stop=True)
                        gend = L.inc(mm, "pe")
                        if kind == "up":
                            L.wait(dv, "dv", "pe", gend)
                            L.wait(dv, "dv", "ac", e_sq.get(bank, 0))
                            mx = L.inc(dv.tensor_scalar_max(fstage[bank],
                                                            ps[bank][:], 0.0), "dv")
                            e_ev[g] = mx
                            L.wait(ac, "ac", "dv", mx)
                            L.wait(ac, "ac", f"g{bank}", e_store.get(bank, 0))
                            sq = L.inc(ac.activation(stage[bank][:], fstage[bank],
                                                     AF.Square), "ac")
                            e_sq[bank] = sq
                            L.wait(gd, "gd", "ac", sq)
                            itg = idx * 4 + it
                            for j in range(4):
                                ins2 = gd.dma_start(
                                    out=u_sc[tb * 4 + j, itg, :, :],
                                    in_=stage[bank][:, j * 128:(j + 1) * 128])
                                L.inc(ins2, f"g{bank}", 16)
                            e_store[bank] = L.cnt[f"g{bank}"]
                        else:
                            L.wait(dv, "dv", "pe", gend)
                            L.wait(dv, "dv", f"g{bank}", e_store.get(bank, 0))
                            ev = L.inc(dv.tensor_copy(stage[bank][:], ps[bank][:]),
                                       "dv")
                            e_ev[g] = ev
                            L.wait(gd, "gd", "dv", ev)
                            lt = idx * 4 + it
                            ins2 = gd.dma_start(
                                out=xl_sc[lt * 128:(lt + 1) * 128,
                                          tb * 512:(tb + 1) * 512],
                                in_=stage[bank][:])
                            e_store[bank] = L.inc(ins2, f"g{bank}", 16)
                        e_grp += 1
                    if sp == 0:
                        for q in range(4):
                            tt = tb * 4 + q
                            L.wait(pe, "pe", "gb", gld)
                            L.wait(pe, "pe", "dv", rt_done)
                            for ch in range(DC - 1):
                                pe.matmul(ps[4][:, 0:8],
                                          exT[step % 2][:, ch * 512 + q * 128:
                                                        ch * 512 + (q + 1) * 128],
                                          gslab[:, ch * 8:(ch + 1) * 8],
                                          start=(ch == 0), stop=False)
                            ch = DC - 1
                            mm = pe.matmul(ps[4][:, 0:8],
                                           exT[step % 2][:, ch * 512 + q * 128:
                                                         ch * 512 + (q + 1) * 128],
                                           gslab[:, ch * 8:(ch + 1) * 8],
                                           start=False, stop=True)
                            gend = L.inc(mm, "pe")
                            L.wait(dv, "dv", "pe", gend)
                            cp = L.inc(dv.tensor_copy(lg8[:], ps[4][:, 0:8]), "dv")
                            L.wait(dv, "dv", "dv", cp)
                            r1 = L.inc(dv.reduce_max(S["m1"][:], lg8[:], axis=AX.X),
                                       "dv")
                            L.wait(dv, "dv", "dv", r1)
                            vm8 = L.inc(dv.tensor_copy(m8[:, 0:1], S["m1"][:]), "dv")
                            L.wait(dv, "dv", "dv", vm8)
                            mr = L.inc(dv.match_replace(g2[:], m8[:], lg8[:], -1e30),
                                       "dv")
                            L.wait(dv, "dv", "dv", mr)
                            r2 = L.inc(dv.reduce_max(S["m2"][:], g2[:], axis=AX.X),
                                       "dv")
                            L.wait(dv, "dv", "dv", r2)
                            rt = L.inc(dv.tensor_add(S["rt1"][:], S["m1"][:],
                                                     S["m2"][:]), "dv")
                            L.wait(pe, "pe", "dv", rt)
                            L.wait(pe, "pe", "gc", 2)
                            tr = L.inc(pe.transpose(ps[7][0:1, 0:128], S["rt1"][:],
                                                    ident_f[:]), "pe")
                            L.wait(dv, "dv", "pe", tr)
                            rt_done = L.inc(
                                dv.tensor_copy(rtrow[:, tt * 128:(tt + 1) * 128],
                                               ps[7][0:1, 0:128]), "dv")
                    step_pe[step] = L.cnt["pe"]
                    e_load(step + 2)
                e_wload(sp + 2)
            L.wait(sy, "sy", "dv", rt_done)
            rt_st = L.inc(sy.dma_start(out=rt_sc, in_=rtrow), "sx0", 16)

            if DEBUG_STOP == "E":
                return finish()
            xlat = b0b(0, 32768)
            eslab = b1b(0, 32768)
            rb_full = b0f(16384, 2048)
            barrier("sy", "gd", "pe", "dv")
            xld = L.inc(sy.dma_start(
                out=xlat.rearrange("p (c t) -> p c t", t=TOK),
                in_=xl_sc.rearrange("(c p) t -> p c t", p=128)), "lx0", 16)
            rld = L.inc(sy.dma_start(out=rb_full,
                                     in_=rt_sc.partition_broadcast(128)), "lx1", 16)
            eld = L.inc(gd.dma_start(
                out=eslab.rearrange("p (c m) -> p c m", m=LAT),
                in_=W["expert_w"].rearrange("(c p) m -> p c m", p=128)), "w0", 16)
            f_grp = 0
            f_ev = {}
            f_store = {}
            for mt in range(LC):
                for tb in range(4):
                    g = f_grp
                    bank = g % 4
                    L.wait(pe, "pe", "dv", f_ev.get(g - 4, 0))
                    L.wait(pe, "pe", "lx0", xld)
                    L.wait(pe, "pe", "w0", eld)
                    gend = None
                    for ch in range(LC):
                        mm = pe.matmul(ps[bank][:],
                                       eslab[:, ch * 2048 + mt * 128:
                                             ch * 2048 + (mt + 1) * 128],
                                       xlat[:, ch * 2048 + tb * 512:
                                            ch * 2048 + (tb + 1) * 512],
                                       start=(ch == 0), stop=(ch == LC - 1))
                    gend = L.inc(mm, "pe")
                    L.wait(dv, "dv", "pe", gend)
                    L.wait(dv, "dv", "lx1", rld)
                    L.wait(dv, "dv", f"g{bank}", f_store.get(bank, 0))
                    ev = L.inc(dv.scalar_tensor_tensor(
                        stage[bank][:], ps[bank][:], 1.0,
                        rb_full[:, tb * 512:(tb + 1) * 512],
                        OP.mult, OP.mult), "dv")
                    f_ev[g] = ev
                    L.wait(gd, "gd", "dv", ev)
                    for j in range(4):
                        ins2 = gd.dma_start(out=moe_sc[tb * 4 + j, mt, :, :],
                                            in_=stage[bank][:, j * 128:(j + 1) * 128])
                        L.inc(ins2, f"g{bank}", 16)
                    f_store[bank] = L.cnt[f"g{bank}"]
                    f_grp += 1

            if DEBUG_STOP == "F":
                return finish()
            dsl2 = [b0b(0, 32768), b0b(32768, 32768)]
            u_tt = [b1b(0, 8192), b1b(8192, 8192)]
            moe_tt = [b1b(16384, 2048), b1b(18432, 2048)]
            f2slab = b1b(20480, 8192)
            fstage = [b1f(14336, 512), b1f(14848, 512)]
            barrier("sy", "gd", "pe", "dv", "ac")
            g_base_pe = L.cnt["pe"]
            g_pe = {}
            g_ev = {}
            g_st = [0, 0]
            g_wl = {}
            g_f2 = {}

            def g_wload(fb):
                if fb >= 8:
                    return
                L.wait(gd, "gd", "pe", g_pe.get((fb - 2, NT_OWN - 1), g_base_pe))
                g_wl[fb] = L.inc(gd.dma_start(
                    out=dsl2[fb % 2].rearrange("p (c f) -> p c f", f=512),
                    in_=W["down_w"][:, fb * 512:(fb + 1) * 512].rearrange(
                        "(c p) f -> p c f", p=128)), f"w{fb % 2}", 16)

            def g_f2load(fb):
                if fb >= 8:
                    return
                L.wait(gd, "gd", "pe", g_pe.get((fb - 1, NT_OWN - 1), g_base_pe))
                g_f2[fb] = L.inc(gd.dma_start(
                    out=f2slab.rearrange("p (c f) -> p c f", f=512),
                    in_=W["fc2_w"][:, fb * 512:(fb + 1) * 512].rearrange(
                        "(c p) f -> p c f", p=128)), "gb", 16)

            g_wload(0)
            g_f2load(0)
            g_wload(1)
            for fb in range(8):
                for tt in range(NT_OWN):
                    p = tt % 2
                    lxs = f"lx{p}"
                    L.wait(sy, "sy", "pe", g_pe.get((fb, tt - 2),
                                                    g_pe.get((fb - 1, NT_OWN - 1),
                                                             g_base_pe)))
                    L.inc(sy.dma_start(
                        out=u_tt[p].rearrange("p (c j) -> p c j", j=128),
                        in_=u_sc[tt].rearrange("c p j -> p c j")), lxs, 16)
                    l3 = L.inc(sy.dma_start(
                        out=moe_tt[p].rearrange("p (c j) -> p c j", j=128),
                        in_=moe_sc[tt].rearrange("c p j -> p c j")), lxs, 16)
                    zbank = 4 + p
                    L.wait(pe, "pe", lxs, l3)
                    L.wait(pe, "pe", f"w{fb % 2}", g_wl[fb])
                    L.wait(pe, "pe", "gb", g_f2[fb])
                    L.wait(pe, "pe", "dv", g_ev.get((fb, tt - 2),
                                                    g_ev.get((fb - 1,
                                                              NT_OWN - 2 + tt), 0)))
                    for ch in range(IC):
                        pe.matmul(ps[zbank][:],
                                  u_tt[p][:, ch * 128:(ch + 1) * 128],
                                  dsl2[fb % 2][:, ch * 512:(ch + 1) * 512],
                                  start=(ch == 0), stop=False)
                    for ch in range(LC):
                        mm = pe.matmul(ps[zbank][:],
                                       moe_tt[p][:, ch * 128:(ch + 1) * 128],
                                       f2slab[:, ch * 512:(ch + 1) * 512],
                                       start=False, stop=(ch == LC - 1))
                    zend = L.inc(mm, "pe")
                    g_pe[(fb, tt)] = zend
                    L.wait(dv, "dv", "pe", zend)
                    L.wait(dv, "dv", f"sx{p}", g_st[p])
                    ev = L.inc(dv.tensor_copy(fstage[p], ps[zbank][:]), "dv")
                    g_ev[(fb, tt)] = ev
                    L.wait(ac, "ac", "dv", ev)
                    L.inc(ac.dma_start(out=z_sc[tt * 128:(tt + 1) * 128,
                                                fb * 512:(fb + 1) * 512],
                                       in_=fstage[p]), f"sx{p}", 16)
                    g_st[p] = L.cnt[f"sx{p}"]
                g_wload(fb + 2)
                g_f2load(fb + 1)

            if DEBUG_STOP == "G":
                return finish()
            zrow = [b0f(0, 4096), b0f(4096, 4096)]
            arow = [b0f(8192, 4096), b0f(12288, 4096)]
            xrow = [b0f(16384, 4096), b0f(20480, 4096)]
            sqrow = b0f(24576, 4096)
            lnrow = b0f(28672, 4096)
            gbcH = b1f(0, 4096)
            gbcH2 = b1f(4096, 4096)
            barrier("sy", "dv", "ac")
            gl1 = L.inc(sy.dma_start(out=gbcH,
                                     in_=VEC["ln_g"].partition_broadcast(128)),
                        "tx0", 16)
            gl2 = L.inc(sy.dma_start(out=gbcH2,
                                     in_=VEC["ln_b"].partition_broadcast(128)),
                        "tx0", 16)
            h_out = {}
            h_st = [0, 0]
            for tt in range(NT_OWN):
                p = tt % 2
                lxs = f"lx{p}"
                if tt >= 2:
                    L.wait(sy, "sy", "dv", h_out[tt - 2])
                    L.wait(sy, "sy", "so", h_st[0])
                L.inc(sy.dma_start(out=zrow[p],
                                   in_=z_sc[tt * 128:(tt + 1) * 128, :]), lxs, 16)
                L.inc(sy.dma_start(out=arow[p],
                                   in_=at_sc[tt * 128:(tt + 1) * 128, :]), lxs, 16)
                l3 = L.inc(sy.dma_start(out=xrow[p],
                                        in_=x_own[tt * 128:(tt + 1) * 128, :]),
                           lxs, 16)
                stats, sd = ln_stats(zrow[p], sqrow, lxs, l3,
                                     h_out.get(tt - 1, 0))
                L.wait(ac, "ac", "dv", stats)
                L.wait(ac, "ac", "ac", sd)
                ln = L.inc(ac.activation(lnrow, zrow[p], AF.Identity,
                                         bias=S["nbias"][:], scale=S["rstd"][:]),
                           "ac")
                L.wait(dv, "dv", "ac", ln)
                L.wait(dv, "dv", "tx0", gl2)
                w1 = L.inc(dv.tensor_mul(sqrow, lnrow, gbcH), "dv")
                L.wait(dv, "dv", "dv", w1)
                w2 = L.inc(dv.tensor_add(lnrow, sqrow, gbcH2), "dv")
                L.wait(dv, "dv", "dv", w2)
                w3 = L.inc(dv.tensor_add(sqrow, lnrow, arow[p]), "dv")
                L.wait(dv, "dv", "dv", w3)
                fo = L.inc(dv.tensor_add(zrow[p], sqrow, xrow[p]), "dv")
                h_out[tt] = fo
                L.wait(sy, "sy", "dv", fo)
                L.inc(sy.dma_start(out=out_hbm[tt * 128:(tt + 1) * 128, :],
                                   in_=zrow[p]), "so", 16)
                h_st[0] = L.cnt["so"]
            return finish()

    return nc


def kernel(**inputs):
    nc = build()
    in_maps = []
    x = np.ascontiguousarray(inputs["L0_attn_hidden_states"])
    row = lambda a: np.ascontiguousarray(np.asarray(a, dtype=np.float32).reshape(1, -1))
    base = {
        "Wq": inputs["L0_attn_Wq"], "Wk": inputs["L0_attn_Wk"],
        "Wv": inputs["L0_attn_Wv"], "Wo": inputs["L0_attn_Wo"],
        "bq": row(inputs["L0_attn_bq"]), "bk": row(inputs["L0_attn_bk"]),
        "bv": row(inputs["L0_attn_bv"]), "bo": row(inputs["L0_attn_bo"]),
        "gq": row(inputs["L0_attn_gq"]), "gk": row(inputs["L0_attn_gk"]),
        "gate_w": inputs["L0_ffn_gate_w"], "up_w": inputs["L0_ffn_up_w"],
        "down_w": inputs["L0_ffn_down_w"], "fc1_w": inputs["L0_ffn_fc1_w"],
        "fc2_w": inputs["L0_ffn_fc2_w"], "expert_w": inputs["L0_ffn_expert_w"],
        "ln_g": row(inputs["L0_ffn_ln_gamma"]), "ln_b": row(inputs["L0_ffn_ln_beta"]),
    }
    base = {k: np.ascontiguousarray(np.asarray(v, dtype=np.float32))
            for k, v in base.items()}
    for c in range(8):
        b, half = c // 2, c % 2
        m = dict(base)
        m["x_own"] = np.ascontiguousarray(x[b, half * TOK:(half + 1) * TOK, :])
        m["x_oth"] = np.ascontiguousarray(x[b, (1 - half) * TOK:(2 - half) * TOK, :])
        in_maps.append(m)
    res = run_bass_kernel_spmd(nc, in_maps, core_ids=list(range(8)))
    outs = [res.results[c]["out"] for c in range(8)]
    full = np.empty((B, SEQ, D), dtype=np.float32)
    for c in range(8):
        b, half = c // 2, c % 2
        full[b, half * TOK:(half + 1) * TOK, :] = outs[c]
    return full


# version 06 (final)
# speedup vs baseline: 1.2962x; 1.0092x over previous
import contextlib
import numpy as np
import concourse.bass as bass
import concourse.mybir as mybir
from concourse.bass_utils import run_bass_kernel_spmd

F32 = mybir.dt.float32
F32R = mybir.dt.float32r
BF16 = mybir.dt.bfloat16
AF = mybir.ActivationFunctionType
AX = mybir.AxisListType
OP = mybir.AluOpType

B, SEQ, D = 4, 4096, 4096
NH, HD = 32, 128
INT, LAT, NE = 8192, 2048, 8
TOK = 2048
DC = D // 128
IC = INT // 128
LC = LAT // 128
NT_OWN = TOK // 128
NT_SEQ = SEQ // 128
LN_EPS = 1e-5
RMS_EPS = 1e-6
FLN_EPS = 1e-5
ISQ = float(1.0 / np.sqrt(HD))

DEBUG_STOP = None
DEBUG_TAPS = ()

DMA_SEMS = ("lx0", "lx1", "lx2", "sx0", "sx1", "sb0", "sb1", "sb2", "sb3",
            "tx0", "tx1", "w0", "w1", "gb", "g0", "g1", "g2", "g3", "so")
ALL_SEMS = DMA_SEMS + ("pe", "dv", "ac", "gc")


class Led:
    def __init__(self, nc, ctx):
        self.nc, self.ctx = nc, ctx
        self.sem, self.cnt, self.hw = {}, {}, {}

    def mk(self, *names):
        for n in names:
            self.sem[n] = self.ctx.enter_context(self.nc.semaphore(n))
            self.cnt[n] = 0

    def inc(self, instr, s, n=1):
        self.cnt[s] += n
        instr.then_inc(self.sem[s], n)
        return self.cnt[s]

    def wait(self, eng, ekey, s, val=None):
        if val is None:
            val = self.cnt[s]
        if val <= 0:
            return
        assert val <= self.cnt[s], f"wait {s}>={val} but only {self.cnt[s]} planned"
        if self.hw.get((ekey, s), 0) < val:
            eng.wait_ge(self.sem[s], val)
            self.hw[(ekey, s)] = val


def build():
    nc = bass.Bass("TRN2", target_bir_lowering=False)

    x_own = nc.dram_tensor("x_own", [TOK, D], F32, kind="ExternalInput").ap()
    x_oth = nc.dram_tensor("x_oth", [TOK, D], F32, kind="ExternalInput").ap()
    W = {}
    for nm, shp in (("Wq", [D, D]), ("Wk", [D, D]), ("Wv", [D, D]), ("Wo", [D, D]),
                    ("gate_w", [D, NE]), ("up_w", [D, INT]), ("down_w", [INT, D]),
                    ("fc1_w", [D, LAT]), ("fc2_w", [LAT, D]), ("expert_w", [LAT, LAT])):
        W[nm] = nc.dram_tensor(nm, shp, F32, kind="ExternalInput").ap()
    VEC = {}
    for nm in ("bq", "bk", "bv", "bo", "gq", "gk", "ln_g", "ln_b"):
        VEC[nm] = nc.dram_tensor(nm, [1, D], F32, kind="ExternalInput").ap()
    out_hbm = nc.dram_tensor("out", [TOK, D], F32, kind="ExternalOutput").ap()

    xn_sc = nc.dram_tensor("xn_sc", [SEQ, D], BF16).ap()
    xnT_sc = nc.dram_tensor("xnT_sc", [DC, 128, SEQ], BF16).ap()
    qr_sc = nc.dram_tensor("qr_sc", [TOK, D], BF16).ap()
    kr_sc = nc.dram_tensor("kr_sc", [SEQ, D], BF16).ap()
    qn_sc = nc.dram_tensor("qn_sc", [TOK, D], BF16).ap()
    kn_sc = nc.dram_tensor("kn_sc", [SEQ, D], BF16).ap()
    v_sc = nc.dram_tensor("v_sc", [SEQ, D], BF16).ap()
    ot_sc = nc.dram_tensor("ot_sc", [NT_OWN, NH, 128, 128], BF16).ap()
    u_sc = nc.dram_tensor("u_sc", [NT_OWN, IC, 128, 128], BF16).ap()
    xl_sc = nc.dram_tensor("xl_sc", [LAT, TOK], BF16).ap()
    moe_sc = nc.dram_tensor("moe_sc", [NT_OWN, LC, 128, 128], BF16).ap()
    rt_sc = nc.dram_tensor("rt_sc", [1, TOK], F32).ap()
    rstd_sc = nc.dram_tensor("rstd_sc", [2, 1, SEQ], F32).ap()
    z_sc = nc.dram_tensor("z_sc", [TOK, D], F32).ap()
    at_sc = nc.dram_tensor("at_sc", [TOK, D], F32).ap()
    scratch = {"xn_sc": xn_sc, "qr_sc": qr_sc, "kr_sc": kr_sc, "qn_sc": qn_sc,
               "kn_sc": kn_sc, "v_sc": v_sc, "xl_sc": xl_sc, "rt_sc": rt_sc,
               "z_sc": z_sc, "at_sc": at_sc,
               "ot_sc": ot_sc, "u_sc": u_sc, "moe_sc": moe_sc}
    dbg = {nm: nc.dram_tensor("dbg_" + nm, list(scratch[nm].shape), scratch[nm].dtype,
                              kind="ExternalOutput").ap() for nm in DEBUG_TAPS}

    ctx = contextlib.ExitStack()
    with ctx:
        L = Led(nc, ctx)
        L.mk(*ALL_SEMS)

        def sbuf(name, shape, dt):
            return ctx.enter_context(nc.sbuf_tensor(name, shape, dt))

        b0 = sbuf("b0", [128, 65536], BF16)
        b1 = sbuf("b1", [128, 32768], BF16)
        stage = [sbuf(f"stage{i}", [128, 512], BF16) for i in range(4)]
        ones_fr = sbuf("ones_fr", [1, 128], F32R)
        ones_f = sbuf("ones_f", [1, 128], F32)
        onescol_bf = sbuf("onescol_bf", [128, 1], BF16)
        ones_bfr = sbuf("ones_bfr", [1, 128], BF16)
        ident_f = sbuf("ident_f", [128, 128], F32)
        rowf = sbuf("rowf", [1, 512], F32)
        rowfr = sbuf("rowfr", [1, 1024], F32R)
        rs_row = sbuf("rs_row", [1, 512], F32R)
        acc_q = sbuf("acc_q", [128, NT_OWN], F32)
        acc_k = sbuf("acc_k", [128, NT_SEQ], F32)
        rcol = sbuf("rcol", [128, NT_SEQ], F32)
        gq_cols = sbuf("gq_cols", [128, NH], F32)
        gk_cols = sbuf("gk_cols", [128, NH], F32)
        m8 = sbuf("m8", [128, 8], F32)
        eps5 = sbuf("eps5", [128, 1], F32)
        eps6 = sbuf("eps6", [128, 1], F32)
        lnisq = sbuf("lnisq", [128, 1], F32)
        zero1 = sbuf("zero1", [128, 1], F32)
        lg8 = sbuf("lg8", [128, 8], F32)
        g2 = sbuf("g2", [128, 8], F32)
        small = {nm: sbuf(nm, [128, 1], F32) for nm in
                 ("s1", "s2", "mu", "msq", "var", "sd", "rstd", "nbias",
                  "m1", "m2", "rt1")}
        small2 = {nm: sbuf(nm + "_b", [128, 1], F32) for nm in
                  ("s1", "s2", "mu", "msq", "var", "sd", "rstd", "nbias")}
        mparts = sbuf("mparts", [128, 8], F32)
        negm = sbuf("negm", [128, 1], F32)
        S = small
        S2 = small2

        def b0f(lo, n):
            return b0[:].bitcast(F32)[:, lo:lo + n]

        def b0b(lo, n):
            return b0[:, lo:lo + n]

        def b1f(lo, n):
            return b1[:].bitcast(F32)[:, lo:lo + n]

        def b1b(lo, n):
            return b1[:, lo:lo + n]

        ps = [ctx.enter_context(nc.psum_tensor(f"ps{i}", [128, 512], F32))
              for i in range(8)]

        block = ctx.enter_context(nc.Block())

        @block.sync
        def _(sy):
            gd = nc.gpsimd
            pe = nc.tensor
            dv = nc.vector
            ac = nc.scalar
            ENG = {"pe": pe, "dv": dv, "ac": ac, "gd": gd, "sy": sy}

            def barrier(*ekeys):
                for ek in ekeys:
                    e = ENG[ek]
                    for s in ALL_SEMS:
                        if s == "so":
                            continue
                        L.wait(e, ek, s, L.cnt[s])

            gd.memset(ones_f[:], 1.0)
            gd.memset(onescol_bf[:], 1.0)
            v0 = L.inc(gd.memset(ident_f[:], 0.0), "gc")
            L.wait(gd, "gd", "gc", v0)
            ins = gd.affine_select(out=ident_f[:], in_=ident_f[:],
                                   compare_op=OP.not_equal, fill=1.0, base=0,
                                   pattern=[[-1, 128]], channel_multiplier=1)
            L.inc(ins, "gc")
            dv.memset(ones_bfr[:], 1.0)
            dv.memset(m8[:], 3.0e38)
            dv.memset(eps5[:], LN_EPS)
            dv.memset(eps6[:], RMS_EPS)
            dv.memset(lnisq[:], float(np.log(ISQ)))
            dv.memset(zero1[:], 0.0)
            dv.memset(acc_q[:], 0.0)
            dv.memset(acc_k[:], 0.0)
            L.wait(dv, "dv", "gc", 1)
            L.inc(dv.tensor_copy(ones_fr[:], ones_f[:]), "dv")
            init_dv = L.cnt["dv"]

            def finish():
                for s in ALL_SEMS:
                    if s != "so":
                        L.wait(sy, "sy", s, L.cnt[s])
                for nm, dap in dbg.items():
                    L.inc(sy.dma_start(out=dap, in_=scratch[nm]), "so", 16)
                L.wait(sy, "sy", "so", L.cnt["so"])

            def ln_stats(rowx, rowsq, ld_sem, ld_val, sq_guard=0, SS=None):
                SS = SS or S
                L.wait(dv, "dv", ld_sem, ld_val)
                dv.reduce_sum(SS["s1"][:], rowx, axis=AX.X)
                L.wait(ac, "ac", ld_sem, ld_val)
                if sq_guard:
                    L.wait(ac, "ac", "dv", sq_guard)
                sq = L.inc(ac.activation(rowsq, rowx, AF.Square), "ac")
                L.wait(dv, "dv", "ac", sq)
                sed = L.inc(dv.reduce_sum(SS["s2"][:], rowsq, axis=AX.X), "dv")
                L.wait(dv, "dv", "dv", sed)
                dv.tensor_scalar_mul(SS["mu"][:], SS["s1"][:], 1.0 / D)
                v1 = L.inc(dv.tensor_scalar_mul(SS["s2"][:], SS["s2"][:], 1.0 / D), "dv")
                L.wait(dv, "dv", "dv", v1)
                v2 = L.inc(dv.tensor_mul(SS["msq"][:], SS["mu"][:], SS["mu"][:]), "dv")
                L.wait(dv, "dv", "dv", v2)
                st = L.inc(dv.tensor_sub(SS["var"][:], SS["s2"][:], SS["msq"][:]), "dv")
                L.wait(ac, "ac", "dv", st)
                L.wait(ac, "ac", "dv", init_dv)
                lnv = L.inc(ac.activation(SS["sd"][:], SS["var"][:], AF.Ln,
                                          bias=eps5[:]), "ac")
                L.wait(ac, "ac", "ac", lnv)
                sd = L.inc(ac.activation(SS["rstd"][:], SS["sd"][:], AF.Exp,
                                         scale=-0.5, bias=zero1[:]), "ac")
                L.wait(dv, "dv", "ac", sd)
                v3 = L.inc(dv.tensor_mul(SS["nbias"][:], SS["mu"][:], SS["rstd"][:]), "dv")
                L.wait(dv, "dv", "dv", v3)
                stats = L.inc(dv.tensor_scalar_mul(SS["nbias"][:], SS["nbias"][:],
                                                   -1.0), "dv")
                return stats, sd

            rax = [b0f(0, 4096), b0f(4096, 4096)]
            rsq = [b0f(8192, 4096), b0f(12288, 4096)]
            rno = [b0b(32768, 4096), b0b(36864, 4096)]
            rax_g = [0, 0]
            rsq_g = [0, 0]
            rno_g = [0, 0]
            for i in range(NT_SEQ):
                p = i % 2
                src = x_own if i < NT_OWN else x_oth
                r0 = (i % NT_OWN) * 128
                lxs = f"lx{p}"
                SSa = S if p == 0 else S2
                L.wait(sy, "sy", "ac", rax_g[p])
                ld = L.inc(sy.dma_start(out=rax[p], in_=src[r0:r0 + 128, :]), lxs, 16)
                stats, sd = ln_stats(rax[p], rsq[p], lxs, ld, rsq_g[p], SS=SSa)
                rsq_g[p] = L.cnt["dv"]
                L.wait(ac, "ac", "dv", stats)
                L.wait(ac, "ac", "ac", sd)
                L.wait(ac, "ac", f"sx{p}", rno_g[p])
                nm_ = L.inc(ac.activation(rno[p], rax[p], AF.Identity,
                                          bias=SSa["nbias"][:], scale=SSa["rstd"][:]),
                            "ac")
                rax_g[p] = nm_
                L.wait(sy, "sy", "ac", nm_)
                rno_g[p] = L.inc(
                    sy.dma_start(out=xn_sc[i * 128:(i + 1) * 128, :], in_=rno[p]),
                    f"sx{p}", 16)

            if DEBUG_STOP == "A":
                return finish()
            xnT = b0[:]
            barrier("sy", "gd", "pe", "dv", "ac")
            qkv_grp = 0
            grp_ev = {}
            b_store = {}
            jobs_by_half = {0: [("Wk", "bk"), ("Wv", "bv"), ("Wq", "bq")],
                            1: [("Wk", "bk"), ("Wv", "bv")]}
            jobidx = 0
            slab_use = {}
            last_rowfr_pe = 0
            grp_sq = {}
            last_rr = [0]
            last_ad = [0]
            for half in (0, 1):
                if half == 1:
                    L.wait(sy, "sy", "pe", L.cnt["pe"])
                    L.wait(sy, "sy", "sx0", L.cnt["sx0"])
                for ch in range(DC):
                    L.inc(sy.dma_start(
                        out=xnT[:, ch * TOK:(ch + 1) * TOK],
                        in_=xn_sc[half * TOK:(half + 1) * TOK,
                                  ch * 128:(ch + 1) * 128],
                        transpose=True), "tx0", 16)
                xnT_ld = L.cnt["tx0"]
                if half == 0:
                    L.wait(sy, "sy", "tx0", xnT_ld)
                    for ch in range(DC):
                        L.inc(sy.dma_start(
                            out=xnT_sc[ch, :, 0:TOK],
                            in_=xnT[:, ch * TOK:(ch + 1) * TOK]), "sx0", 16)
                for wnm, bnm in jobs_by_half[half]:
                    for ib in range(8):
                        wsem = f"w{jobidx % 2}"
                        if jobidx >= 2:
                            L.wait(gd, "gd", "pe", slab_use[jobidx - 2])
                        wsl = b1b(0, 16384) if jobidx % 2 == 0 else b1b(16384, 16384)
                        wld = L.inc(gd.dma_start(
                            out=wsl.rearrange("p (c f) -> p c f", f=512),
                            in_=W[wnm][:, ib * 512:(ib + 1) * 512].rearrange(
                                "(c p) f -> p c f", p=128)), wsem, 16)
                        L.wait(gd, "gd", "pe", last_rowfr_pe)
                        bld = L.inc(gd.dma_start(
                            out=rowf[:],
                            in_=VEC[bnm][0:1, ib * 512:(ib + 1) * 512]), "gb", 16)
                        L.wait(dv, "dv", "gb", bld)
                        brc = L.inc(dv.tensor_copy(rowfr[0:1, 0:512], rowf[:]),
                                    "dv")
                        for tt in range(NT_OWN):
                            g = qkv_grp
                            bank = g % 4
                            L.wait(pe, "pe", "dv", max(grp_ev.get(g - 4, 0), brc))
                            L.wait(pe, "pe", "ac", grp_sq.get(g - 4, 0))
                            L.wait(pe, "pe", wsem, wld)
                            L.wait(pe, "pe", "tx0", xnT_ld)
                            for ch in range(DC):
                                pe.matmul(ps[bank][:],
                                          xnT[:, ch * TOK + tt * 128:
                                              ch * TOK + (tt + 1) * 128],
                                          wsl[:, ch * 512:(ch + 1) * 512],
                                          start=(ch == 0), stop=False)
                            mm = pe.matmul(ps[bank][:], ones_fr[:],
                                           rowfr[0:1, 0:512],
                                           start=False, stop=True)
                            gend = L.inc(mm, "pe")
                            L.wait(dv, "dv", "pe", gend)
                            L.wait(dv, "dv", f"sb{bank}", b_store.get(bank, 0))
                            ev = L.inc(dv.tensor_copy(stage[bank][:], ps[bank][:]),
                                       "dv")
                            grp_ev[g] = ev
                            if wnm in ("Wq", "Wk"):
                                L.wait(ac, "ac", "pe", gend)
                                L.wait(ac, "ac", "dv", max(ev, last_rr[0]))
                                aq = L.inc(ac.activation(ps[4][:], ps[bank][:],
                                                         AF.Square), "ac")
                                grp_sq[g] = aq
                                L.wait(dv, "dv", "ac", aq)
                                L.wait(dv, "dv", "dv", last_ad[0])
                                rr = L.inc(dv.reduce_sum(S["s2"][:], ps[4][:],
                                                         axis=AX.X), "dv")
                                last_rr[0] = rr
                                L.wait(dv, "dv", "dv", rr)
                                col = tt if wnm == "Wq" else half * NT_OWN + tt
                                accT = acc_q if wnm == "Wq" else acc_k
                                ad = L.inc(dv.tensor_add(accT[:, col:col + 1],
                                                         accT[:, col:col + 1],
                                                         S["s2"][:]), "dv")
                                last_ad[0] = ad
                            L.wait(sy, "sy", "dv", ev)
                            trow = half * TOK + tt * 128
                            if wnm == "Wq":
                                dst = qr_sc[tt * 128:tt * 128 + 128,
                                            ib * 512:(ib + 1) * 512]
                            elif wnm == "Wk":
                                dst = kr_sc[trow:trow + 128, ib * 512:(ib + 1) * 512]
                            else:
                                dst = v_sc[trow:trow + 128, ib * 512:(ib + 1) * 512]
                            b_store[bank] = L.inc(
                                sy.dma_start(out=dst, in_=stage[bank][:]),
                                f"sb{bank}", 16)
                            qkv_grp += 1
                        slab_use[jobidx] = L.cnt["pe"]
                        last_rowfr_pe = L.cnt["pe"]
                        jobidx += 1

            if DEBUG_STOP == "B":
                return finish()
            barrier("pe", "dv", "ac", "sy")
            rowb = b1[0:1, :].bitcast(F32)
            rowb_g = 0
            for jj, (accT, ncol, use_isq) in enumerate(
                    ((acc_q, NT_OWN, True), (acc_k, NT_SEQ, False))):
                v1 = L.inc(dv.tensor_scalar_mul(accT[:, 0:ncol], accT[:, 0:ncol],
                                                1.0 / D), "dv")
                L.wait(ac, "ac", "dv", v1)
                lnv = L.inc(ac.activation(rcol[:, 0:ncol], accT[:, 0:ncol],
                                          AF.Ln, bias=eps6[:]), "ac")
                L.wait(ac, "ac", "ac", lnv)
                ex = L.inc(ac.activation(rcol[:, 0:ncol], rcol[:, 0:ncol], AF.Exp,
                                         scale=-0.5,
                                         bias=(lnisq[:] if use_isq else zero1[:])),
                           "ac")
                for j in range(ncol):
                    L.wait(pe, "pe", "ac", ex)
                    L.wait(pe, "pe", "gc", 2)
                    L.wait(pe, "pe", "dv", L.cnt["dv"])
                    tr = L.inc(pe.transpose(ps[7][0:1, 0:128], rcol[:, j:j + 1],
                                            ident_f[:]), "pe")
                    L.wait(dv, "dv", "pe", tr)
                    L.wait(dv, "dv", "sx0", rowb_g)
                    L.inc(dv.tensor_copy(rowb[0:1, j * 128:(j + 1) * 128],
                                         ps[7][0:1, 0:128]), "dv")
                L.wait(sy, "sy", "dv", L.cnt["dv"])
                rowb_g = L.inc(sy.dma_start(out=rstd_sc[jj][0:1, 0:ncol * 128],
                                            in_=rowb[0:1, 0:ncol * 128]), "sx0", 16)

            KT = [b0b(0, 4096), b0b(4096, 4096), b0b(8192, 4096)]
            QT = [b0b(12288, 2048), b0b(14336, 2048), b0b(16384, 2048)]
            VS = [b0b(18432, 4096), b0b(22528, 4096), b0b(26624, 4096)]
            PT = [b0b(30720 + 512 * i, 512) for i in range(4)]
            otst = [b0b(32768, 512), b0b(33280, 512)]

            def nrow(hh, qb):
                off = 33792 + ((hh % 2) * 4 + qb) * 512
                return b0[0:1, off:off + 512]

            rb_sb = b0f(19456, 512)
            rqb = b0f(20480, 2048)
            rkb = b0f(24576, 4096)
            grow = b0[0:1, :].bitcast(F32)[0:1, 24576:28672]
            barrier("sy", "pe", "dv", "ac", "gd")
            gld1 = L.inc(sy.dma_start(out=grow[0:1, 0:4096], in_=VEC["gq"]),
                         "tx0", 16)
            gcl = 0
            for h in range(NH):
                L.wait(pe, "pe", "tx0", gld1)
                L.wait(pe, "pe", "gc", 2)
                L.wait(pe, "pe", "dv", gcl)
                tr = L.inc(pe.transpose(ps[7][0:128, 0:1],
                                        grow[0:1, h * 128:(h + 1) * 128],
                                        ident_f[0:1, 0:1]), "pe")
                L.wait(dv, "dv", "pe", tr)
                gcl = L.inc(dv.tensor_copy(gq_cols[:, h:h + 1], ps[7][0:128, 0:1]),
                            "dv")
            L.wait(sy, "sy", "dv", gcl)
            gld2 = L.inc(sy.dma_start(out=grow[0:1, 0:4096], in_=VEC["gk"]),
                         "tx0", 16)
            for h in range(NH):
                L.wait(pe, "pe", "tx0", gld2)
                L.wait(pe, "pe", "dv", gcl)
                tr = L.inc(pe.transpose(ps[7][0:128, 0:1],
                                        grow[0:1, h * 128:(h + 1) * 128],
                                        ident_f[0:1, 0:1]), "pe")
                L.wait(dv, "dv", "pe", tr)
                gcl = L.inc(dv.tensor_copy(gk_cols[:, h:h + 1], ps[7][0:128, 0:1]),
                            "dv")
            L.wait(sy, "sy", "dv", gcl)
            L.wait(sy, "sy", "sx0", L.cnt["sx0"])
            rbl = L.inc(sy.dma_start(out=rqb,
                                     in_=rstd_sc[0][0:1, 0:TOK]
                                     .partition_broadcast(128)), "tx0", 16)
            rbl = L.inc(sy.dma_start(out=rkb,
                                     in_=rstd_sc[1].partition_broadcast(128)),
                        "tx0", 16)
            d_head_pe = {-1: L.cnt["pe"], -2: L.cnt["pe"], -3: L.cnt["pe"]}
            d_loads = {}
            d_apply = {}
            bank_free = {b: ("dv", 0) for b in range(8)}
            bank_free[7] = ("dv", gcl)
            pt_free = [0, 0, 0, 0]
            nr_ready = {}
            nr_user = {}
            ot_g = [0, 0]
            p1_pending = []
            negm_tr = [0]

            def pe_bank(b):
                s, v = bank_free[b]
                L.wait(pe, "pe", s, v)

            def d_load_head(h):
                if h >= NH:
                    return
                hp = h % 3
                lxs = f"lx{hp}"
                L.wait(sy, "sy", "pe", d_head_pe[h - 3])
                L.inc(sy.dma_start(out=KT[hp][:, :],
                                   in_=kr_sc[:, h * 128:(h + 1) * 128],
                                   transpose=True), lxs, 16)
                L.inc(sy.dma_start(out=QT[hp][:, :],
                                   in_=qr_sc[:, h * 128:(h + 1) * 128],
                                   transpose=True), lxs, 16)
                for part in range(4):
                    L.inc(sy.dma_start(
                        out=VS[hp][:, part * 1024:(part + 1) * 1024].rearrange(
                            "p (kt j) -> p kt j", j=128),
                        in_=v_sc[part * 1024:(part + 1) * 1024,
                                 h * 128:(h + 1) * 128].rearrange(
                            "(kt p) j -> p kt j", p=128)), lxs, 16)
                d_loads[h] = L.cnt[lxs]

            def d_apply_head(h):
                if h >= NH:
                    return
                hp = h % 3
                L.wait(dv, "dv", f"lx{hp}", d_loads[h])
                L.wait(dv, "dv", "tx0", rbl)
                L.inc(dv.scalar_tensor_tensor(KT[hp][:, :], KT[hp][:, :],
                                              gk_cols[:, h:h + 1], rkb,
                                              OP.mult, OP.mult), "dv")
                ap = L.inc(dv.scalar_tensor_tensor(QT[hp][:, :], QT[hp][:, :],
                                                   gq_cols[:, h:h + 1], rqb,
                                                   OP.mult, OP.mult), "dv")
                d_apply[h] = ap

            d_load_head(0)
            d_apply_head(0)
            d_load_head(1)
            d_apply_head(1)

            def emit_p1_slot(h, slot):
                hp = h % 3
                qb, r = divmod(slot, 32)
                qt, kb = divmod(r, 8)
                qcol = (qb * 4 + qt) * 128
                bank = slot % 2
                pe_bank(bank)
                L.wait(pe, "pe", "dv", d_apply[h])
                mm = pe.matmul(ps[bank][:], QT[hp][:, qcol:qcol + 128],
                               KT[hp][:, kb * 512:(kb + 1) * 512],
                               start=True, stop=True)
                pend = L.inc(mm, "pe")
                L.wait(dv, "dv", "pe", pend)
                rr = L.inc(dv.reduce_max(mparts[:, kb:kb + 1], ps[bank][:],
                                         axis=AX.X), "dv")
                bank_free[bank] = ("dv", rr)
                if kb == 7:
                    L.wait(dv, "dv", "dv", rr)
                    L.wait(dv, "dv", "pe", negm_tr[0])
                    nm_ = L.inc(dv.reduce_max(negm[:], mparts[:], axis=AX.X,
                                              negate=True), "dv")
                    L.wait(dv, "dv", "dv", nm_)
                    p1_pending.append((h, qb, qt, nm_))

            def flush_p1(h, qb, qt, nm_):
                L.wait(pe, "pe", "dv", nm_)
                pe_bank(7)
                tr = L.inc(pe.transpose(ps[7][0:1, 0:128], negm[:],
                                        ident_f[:]), "pe")
                negm_tr[0] = tr
                L.wait(dv, "dv", "pe", tr)
                L.wait(dv, "dv", "pe", nr_user.get((h - 2, qb), 0))
                cp = L.inc(dv.tensor_copy(
                    nrow(h, qb)[:, qt * 128:(qt + 1) * 128],
                    ps[7][0:1, 0:128]), "dv")
                bank_free[7] = ("dv", cp)
                if qt == 3:
                    nr_ready[(h, qb)] = cp

            def maybe_flush_p1(slack=2):
                while p1_pending and p1_pending[0][3] <= L.cnt["dv"] - slack:
                    flush_p1(*p1_pending.pop(0))

            def drain_p1():
                while p1_pending:
                    flush_p1(*p1_pending.pop(0))

            def emit_p2_kt(h, qb, kt):
                hp = h % 3
                qcol = qb * 512
                bank = 2 + (kt % 3)
                pe_bank(bank)
                L.wait(pe, "pe", "dv", nr_ready[(h, qb)])
                pe.matmul(ps[bank][:], KT[hp][:, kt * 128:(kt + 1) * 128],
                          QT[hp][:, qcol:qcol + 512], start=True, stop=False)
                mm = pe.matmul(ps[bank][:], ones_bfr[:], nrow(h, qb),
                               start=False, stop=True)
                gend = L.inc(mm, "pe")
                nr_user[(h, qb)] = gend
                L.wait(ac, "ac", "pe", max(gend, pt_free[kt % 4]))
                ex = L.inc(ac.activation(PT[kt % 4][:], ps[bank][:], AF.Exp), "ac")
                bank_free[bank] = ("ac", ex)
                return ex

            def emit_p2_sums_av(h, qb, kt, ex):
                hp = h % 3
                L.wait(pe, "pe", "ac", ex)
                if kt == 0:
                    pe_bank(5)
                    pe_bank(6)
                pe.matmul(ps[5][0:1, :], onescol_bf[:], PT[kt % 4][:],
                          start=(kt == 0), stop=(kt == NT_SEQ - 1))
                mm = pe.matmul(ps[6][:], VS[hp][:, kt * 128:(kt + 1) * 128],
                               PT[kt % 4][:],
                               start=(kt == 0), stop=(kt == NT_SEQ - 1))
                pt_free[kt % 4] = L.inc(mm, "pe")

            def emit_tail_dv(h, qb):
                avend = L.cnt["pe"]
                L.wait(dv, "dv", "pe", avend)
                rc0 = L.inc(dv.reciprocal(rowf[:], ps[5][0:1, :]), "dv")
                bank_free[5] = ("dv", rc0)
                L.wait(dv, "dv", "dv", rc0)
                rc = L.inc(dv.tensor_copy(rs_row[:], rowf[:]), "dv")
                L.wait(dv, "dv", "dv", rc)
                return rc

            def flush_tail(h, qb, rc):
                L.wait(pe, "pe", "dv", rc)
                pe_bank(7)
                bc = L.inc(pe.matmul(ps[7][:], ones_fr[:], rs_row[:],
                                     start=True, stop=True), "pe")
                L.wait(dv, "dv", "pe", bc)
                vrb = L.inc(dv.tensor_copy(rb_sb, ps[7][:]), "dv")
                bank_free[7] = ("dv", vrb)
                L.wait(dv, "dv", "dv", vrb)
                L.wait(dv, "dv", f"g{qb % 2}", ot_g[qb % 2])
                ot = L.inc(dv.tensor_mul(otst[qb % 2][:], ps[6][:], rb_sb), "dv")
                bank_free[6] = ("dv", ot)
                L.wait(gd, "gd", "dv", ot)
                for qt in range(4):
                    ins2 = gd.dma_start(
                        out=ot_sc[qb * 4 + qt, h, :, :],
                        in_=otst[qb % 2][:, qt * 128:(qt + 1) * 128])
                    L.inc(ins2, f"g{qb % 2}", 16)
                ot_g[qb % 2] = L.cnt[f"g{qb % 2}"]

            pending = None
            for h in range(NH + 1):
                p1_slots = list(range(128)) if h < NH else []
                p1_i = 0
                if h == 0:
                    for slot in p1_slots:
                        emit_p1_slot(h, slot)
                        maybe_flush_p1()
                    drain_p1()
                    continue
                d_load_head(h + 1)
                h2 = h - 1
                for qb in range(4):
                    prev_ex = None
                    for kt in range(NT_SEQ):
                        ex = emit_p2_kt(h2, qb, kt)
                        if kt == 0 and pending is not None:
                            flush_tail(*pending)
                            pending = None
                        if kt > 0:
                            emit_p2_sums_av(h2, qb, kt - 1, prev_ex)
                        prev_ex = ex
                        if p1_i < len(p1_slots):
                            emit_p1_slot(h, p1_slots[p1_i])
                            p1_i += 1
                        maybe_flush_p1()
                    emit_p2_sums_av(h2, qb, NT_SEQ - 1, prev_ex)
                    rc = emit_tail_dv(h2, qb)
                    pending = (h2, qb, rc)
                    if qb == 1:
                        d_apply_head(h + 1)
                while p1_i < len(p1_slots):
                    emit_p1_slot(h, p1_slots[p1_i])
                    p1_i += 1
                drain_p1()
                d_head_pe[h2] = L.cnt["pe"]
            flush_tail(*pending)
            pending = None

            if DEBUG_STOP == "D":
                return finish()
            wo2 = [b1b(0, 16384), b1b(16384, 16384)]
            ot2 = [b0b(0, 4096), b0b(4096, 4096)]
            fst2 = [b0f(4096, 512), b0f(4608, 512)]
            barrier("sy", "gd", "pe", "dv", "ac")
            d2_base = L.cnt["pe"]
            d2_pe = {}
            d2_ev = {}
            d2_st = [0, 0]
            d2_wl = {}
            d2_brc = {}

            def d2_wload(fb):
                if fb >= 8:
                    return
                L.wait(gd, "gd", "pe", d2_pe.get((fb - 2, NT_OWN - 1), d2_base))
                d2_wl[fb] = L.inc(gd.dma_start(
                    out=wo2[fb % 2].rearrange("p (c f) -> p c f", f=512),
                    in_=W["Wo"][:, fb * 512:(fb + 1) * 512].rearrange(
                        "(c p) f -> p c f", p=128)), f"w{fb % 2}", 16)
                L.wait(gd, "gd", "dv", d2_brc.get(fb - 1, 0))
                bld = L.inc(gd.dma_start(
                    out=rowf[:], in_=VEC["bo"][0:1, fb * 512:(fb + 1) * 512]),
                    "gb", 16)
                L.wait(dv, "dv", "gb", bld)
                L.wait(dv, "dv", "pe", d2_pe.get((fb - 2, NT_OWN - 1), d2_base))
                d2_brc[fb] = L.inc(dv.tensor_copy(
                    rowfr[0:1, (fb % 2) * 512:(fb % 2) * 512 + 512], rowf[:]), "dv")

            d2_wload(0)
            d2_wload(1)
            for fb in range(8):
                for tt in range(NT_OWN):
                    p = tt % 2
                    lxs = f"lx{p}"
                    L.wait(sy, "sy", "pe", d2_pe.get((fb, tt - 2),
                                                     d2_pe.get((fb - 1, NT_OWN - 1),
                                                               d2_base)))
                    l3 = L.inc(sy.dma_start(
                        out=ot2[p].rearrange("p (c j) -> p c j", j=128),
                        in_=ot_sc[tt].rearrange("c p j -> p c j")), lxs, 16)
                    bank = p
                    L.wait(pe, "pe", lxs, l3)
                    L.wait(pe, "pe", f"w{fb % 2}", d2_wl[fb])
                    prev_ev = d2_ev.get((fb, tt - 2),
                                        d2_ev.get((fb - 1, NT_OWN - 2 + tt), 0))
                    L.wait(pe, "pe", "dv", max(prev_ev, d2_brc[fb]))
                    for ch in range(NH):
                        pe.matmul(ps[bank][:],
                                  ot2[p][:, ch * 128:(ch + 1) * 128],
                                  wo2[fb % 2][:, ch * 512:(ch + 1) * 512],
                                  start=(ch == 0), stop=False)
                    mm = pe.matmul(ps[bank][:], ones_fr[:],
                                   rowfr[0:1, (fb % 2) * 512:(fb % 2) * 512 + 512],
                                   start=False, stop=True)
                    aend = L.inc(mm, "pe")
                    d2_pe[(fb, tt)] = aend
                    L.wait(dv, "dv", "pe", aend)
                    L.wait(dv, "dv", f"sx{p}", d2_st[p])
                    ev = L.inc(dv.tensor_copy(fst2[p], ps[bank][:]), "dv")
                    d2_ev[(fb, tt)] = ev
                    L.wait(ac, "ac", "dv", ev)
                    L.inc(ac.dma_start(out=at_sc[tt * 128:(tt + 1) * 128,
                                                 fb * 512:(fb + 1) * 512],
                                       in_=fst2[p]), f"sx{p}", 16)
                    d2_st[p] = L.cnt[f"sx{p}"]
                d2_wload(fb + 2)

            exT = [b0b(0, 16384), b0b(16384, 16384)]
            wsl_e = [b1b(0, 16384), b1b(16384, 16384)]
            gslab = b0b(32768, 256)
            rtrow = b0[0:1, 36864:40960].bitcast(F32)
            fstage = [b0f(20480 + 512 * i, 512) for i in range(4)]
            barrier("sy", "gd", "pe", "dv", "ac")
            e_base_pe = L.cnt["pe"]
            gld = L.inc(gd.dma_start(
                out=gslab.rearrange("p (c f) -> p c f", f=NE),
                in_=W["gate_w"].rearrange("(c p) f -> p c f", p=128)), "gb", 16)
            spans = [("up", "up_w", i) for i in range(INT // 512)] + \
                    [("fc1", "fc1_w", i) for i in range(LAT // 512)]
            nsteps = len(spans) * 4
            ex_ld = {}
            step_pe = {}
            wld_e = {}
            e_ev = {}
            e_sq = {}
            e_store = {}
            rt_done = 0

            def e_load(step):
                if step >= nsteps:
                    return
                tb = step % 4
                txs = f"tx{step % 2}"
                L.wait(sy, "sy", "pe", step_pe.get(step - 2, e_base_pe))
                L.inc(sy.dma_start(
                    out=exT[step % 2][:].rearrange("p (c f) -> p c f", f=512),
                    in_=xnT_sc[:, :, tb * 512:(tb + 1) * 512].rearrange(
                        "c p f -> p c f")), txs, 16)
                ex_ld[step] = L.cnt[txs]

            def e_wload(sp):
                if sp >= len(spans):
                    return
                kind, wname, idx = spans[sp]
                wsem = f"w{sp % 2}"
                L.wait(gd, "gd", "pe", step_pe.get((sp - 2) * 4 + 3, e_base_pe))
                wld_e[sp] = L.inc(gd.dma_start(
                    out=wsl_e[sp % 2].rearrange("p (c f) -> p c f", f=512),
                    in_=W[wname][:, idx * 512:(idx + 1) * 512].rearrange(
                        "(c p) f -> p c f", p=128)), wsem, 16)

            e_load(0)
            e_load(1)
            e_wload(0)
            e_wload(1)
            e_grp = 0
            for sp, (kind, wname, idx) in enumerate(spans):
                wsem = f"w{sp % 2}"
                for tb in range(4):
                    step = sp * 4 + tb
                    txs = f"tx{step % 2}"
                    for it in range(4):
                        g = e_grp
                        bank = g % 4
                        L.wait(pe, "pe", "dv", e_ev.get(g - 4, 0))
                        L.wait(pe, "pe", wsem, wld_e[sp])
                        L.wait(pe, "pe", txs, ex_ld[step])
                        for ch in range(DC - 1):
                            pe.matmul(ps[bank][:],
                                      wsl_e[sp % 2][:, ch * 512 + it * 128:
                                                    ch * 512 + (it + 1) * 128],
                                      exT[step % 2][:, ch * 512:(ch + 1) * 512],
                                      start=(ch == 0), stop=False)
                        ch = DC - 1
                        mm = pe.matmul(ps[bank][:],
                                       wsl_e[sp % 2][:, ch * 512 + it * 128:
                                                     ch * 512 + (it + 1) * 128],
                                       exT[step % 2][:, ch * 512:(ch + 1) * 512],
                                       start=False, stop=True)
                        gend = L.inc(mm, "pe")
                        if kind == "up":
                            L.wait(dv, "dv", "pe", gend)
                            L.wait(dv, "dv", "ac", e_sq.get(bank, 0))
                            mx = L.inc(dv.tensor_scalar_max(fstage[bank],
                                                            ps[bank][:], 0.0), "dv")
                            e_ev[g] = mx
                            L.wait(ac, "ac", "dv", mx)
                            L.wait(ac, "ac", f"g{bank}", e_store.get(bank, 0))
                            sq = L.inc(ac.activation(stage[bank][:], fstage[bank],
                                                     AF.Square), "ac")
                            e_sq[bank] = sq
                            L.wait(gd, "gd", "ac", sq)
                            itg = idx * 4 + it
                            for j in range(4):
                                ins2 = gd.dma_start(
                                    out=u_sc[tb * 4 + j, itg, :, :],
                                    in_=stage[bank][:, j * 128:(j + 1) * 128])
                                L.inc(ins2, f"g{bank}", 16)
                            e_store[bank] = L.cnt[f"g{bank}"]
                        else:
                            L.wait(dv, "dv", "pe", gend)
                            L.wait(dv, "dv", f"g{bank}", e_store.get(bank, 0))
                            ev = L.inc(dv.tensor_copy(stage[bank][:], ps[bank][:]),
                                       "dv")
                            e_ev[g] = ev
                            L.wait(gd, "gd", "dv", ev)
                            lt = idx * 4 + it
                            ins2 = gd.dma_start(
                                out=xl_sc[lt * 128:(lt + 1) * 128,
                                          tb * 512:(tb + 1) * 512],
                                in_=stage[bank][:])
                            e_store[bank] = L.inc(ins2, f"g{bank}", 16)
                        e_grp += 1
                    if sp == 0:
                        for q in range(4):
                            tt = tb * 4 + q
                            L.wait(pe, "pe", "gb", gld)
                            L.wait(pe, "pe", "dv", rt_done)
                            for ch in range(DC - 1):
                                pe.matmul(ps[4][:, 0:8],
                                          exT[step % 2][:, ch * 512 + q * 128:
                                                        ch * 512 + (q + 1) * 128],
                                          gslab[:, ch * 8:(ch + 1) * 8],
                                          start=(ch == 0), stop=False)
                            ch = DC - 1
                            mm = pe.matmul(ps[4][:, 0:8],
                                           exT[step % 2][:, ch * 512 + q * 128:
                                                         ch * 512 + (q + 1) * 128],
                                           gslab[:, ch * 8:(ch + 1) * 8],
                                           start=False, stop=True)
                            gend = L.inc(mm, "pe")
                            L.wait(dv, "dv", "pe", gend)
                            cp = L.inc(dv.tensor_copy(lg8[:], ps[4][:, 0:8]), "dv")
                            L.wait(dv, "dv", "dv", cp)
                            r1 = L.inc(dv.reduce_max(S["m1"][:], lg8[:], axis=AX.X),
                                       "dv")
                            L.wait(dv, "dv", "dv", r1)
                            vm8 = L.inc(dv.tensor_copy(m8[:, 0:1], S["m1"][:]), "dv")
                            L.wait(dv, "dv", "dv", vm8)
                            mr = L.inc(dv.match_replace(g2[:], m8[:], lg8[:], -1e30),
                                       "dv")
                            L.wait(dv, "dv", "dv", mr)
                            r2 = L.inc(dv.reduce_max(S["m2"][:], g2[:], axis=AX.X),
                                       "dv")
                            L.wait(dv, "dv", "dv", r2)
                            rt = L.inc(dv.tensor_add(S["rt1"][:], S["m1"][:],
                                                     S["m2"][:]), "dv")
                            L.wait(pe, "pe", "dv", rt)
                            L.wait(pe, "pe", "gc", 2)
                            tr = L.inc(pe.transpose(ps[7][0:1, 0:128], S["rt1"][:],
                                                    ident_f[:]), "pe")
                            L.wait(dv, "dv", "pe", tr)
                            rt_done = L.inc(
                                dv.tensor_copy(rtrow[:, tt * 128:(tt + 1) * 128],
                                               ps[7][0:1, 0:128]), "dv")
                    step_pe[step] = L.cnt["pe"]
                    e_load(step + 2)
                e_wload(sp + 2)
            L.wait(sy, "sy", "dv", rt_done)
            rt_st = L.inc(sy.dma_start(out=rt_sc, in_=rtrow), "sx0", 16)

            if DEBUG_STOP == "E":
                return finish()
            xlat = b0b(0, 32768)
            eslab = b1b(0, 32768)
            rb_full = b0f(16384, 2048)
            barrier("sy", "gd", "pe", "dv")
            xld = L.inc(sy.dma_start(
                out=xlat.rearrange("p (c t) -> p c t", t=TOK),
                in_=xl_sc.rearrange("(c p) t -> p c t", p=128)), "lx0", 16)
            rld = L.inc(sy.dma_start(out=rb_full,
                                     in_=rt_sc.partition_broadcast(128)), "lx1", 16)
            eld = L.inc(gd.dma_start(
                out=eslab.rearrange("p (c m) -> p c m", m=LAT),
                in_=W["expert_w"].rearrange("(c p) m -> p c m", p=128)), "w0", 16)
            f_grp = 0
            f_ev = {}
            f_store = {}
            for mt in range(LC):
                for tb in range(4):
                    g = f_grp
                    bank = g % 4
                    L.wait(pe, "pe", "dv", f_ev.get(g - 4, 0))
                    L.wait(pe, "pe", "lx0", xld)
                    L.wait(pe, "pe", "w0", eld)
                    gend = None
                    for ch in range(LC):
                        mm = pe.matmul(ps[bank][:],
                                       eslab[:, ch * 2048 + mt * 128:
                                             ch * 2048 + (mt + 1) * 128],
                                       xlat[:, ch * 2048 + tb * 512:
                                            ch * 2048 + (tb + 1) * 512],
                                       start=(ch == 0), stop=(ch == LC - 1))
                    gend = L.inc(mm, "pe")
                    L.wait(dv, "dv", "pe", gend)
                    L.wait(dv, "dv", "lx1", rld)
                    L.wait(dv, "dv", f"g{bank}", f_store.get(bank, 0))
                    ev = L.inc(dv.scalar_tensor_tensor(
                        stage[bank][:], ps[bank][:], 1.0,
                        rb_full[:, tb * 512:(tb + 1) * 512],
                        OP.mult, OP.mult), "dv")
                    f_ev[g] = ev
                    L.wait(gd, "gd", "dv", ev)
                    for j in range(4):
                        ins2 = gd.dma_start(out=moe_sc[tb * 4 + j, mt, :, :],
                                            in_=stage[bank][:, j * 128:(j + 1) * 128])
                        L.inc(ins2, f"g{bank}", 16)
                    f_store[bank] = L.cnt[f"g{bank}"]
                    f_grp += 1

            if DEBUG_STOP == "F":
                return finish()
            dsl2 = [b0b(0, 32768), b0b(32768, 32768)]
            u_tt = [b1b(0, 8192), b1b(8192, 8192)]
            moe_tt = [b1b(16384, 2048), b1b(18432, 2048)]
            f2slab = b1b(20480, 8192)
            fstage = [b1f(14336, 512), b1f(14848, 512)]
            barrier("sy", "gd", "pe", "dv", "ac")
            g_base_pe = L.cnt["pe"]
            g_pe = {}
            g_ev = {}
            g_st = [0, 0]
            g_wl = {}
            g_f2 = {}

            def g_wload(fb):
                if fb >= 8:
                    return
                L.wait(gd, "gd", "pe", g_pe.get((fb - 2, NT_OWN - 1), g_base_pe))
                g_wl[fb] = L.inc(gd.dma_start(
                    out=dsl2[fb % 2].rearrange("p (c f) -> p c f", f=512),
                    in_=W["down_w"][:, fb * 512:(fb + 1) * 512].rearrange(
                        "(c p) f -> p c f", p=128)), f"w{fb % 2}", 16)

            def g_f2load(fb):
                if fb >= 8:
                    return
                L.wait(gd, "gd", "pe", g_pe.get((fb - 1, NT_OWN - 1), g_base_pe))
                g_f2[fb] = L.inc(gd.dma_start(
                    out=f2slab.rearrange("p (c f) -> p c f", f=512),
                    in_=W["fc2_w"][:, fb * 512:(fb + 1) * 512].rearrange(
                        "(c p) f -> p c f", p=128)), "gb", 16)

            g_wload(0)
            g_f2load(0)
            g_wload(1)
            for fb in range(8):
                for tt in range(NT_OWN):
                    p = tt % 2
                    lxs = f"lx{p}"
                    L.wait(sy, "sy", "pe", g_pe.get((fb, tt - 2),
                                                    g_pe.get((fb - 1, NT_OWN - 1),
                                                             g_base_pe)))
                    L.inc(sy.dma_start(
                        out=u_tt[p].rearrange("p (c j) -> p c j", j=128),
                        in_=u_sc[tt].rearrange("c p j -> p c j")), lxs, 16)
                    l3 = L.inc(sy.dma_start(
                        out=moe_tt[p].rearrange("p (c j) -> p c j", j=128),
                        in_=moe_sc[tt].rearrange("c p j -> p c j")), lxs, 16)
                    zbank = 4 + p
                    L.wait(pe, "pe", lxs, l3)
                    L.wait(pe, "pe", f"w{fb % 2}", g_wl[fb])
                    L.wait(pe, "pe", "gb", g_f2[fb])
                    L.wait(pe, "pe", "dv", g_ev.get((fb, tt - 2),
                                                    g_ev.get((fb - 1,
                                                              NT_OWN - 2 + tt), 0)))
                    for ch in range(IC):
                        pe.matmul(ps[zbank][:],
                                  u_tt[p][:, ch * 128:(ch + 1) * 128],
                                  dsl2[fb % 2][:, ch * 512:(ch + 1) * 512],
                                  start=(ch == 0), stop=False)
                    for ch in range(LC):
                        mm = pe.matmul(ps[zbank][:],
                                       moe_tt[p][:, ch * 128:(ch + 1) * 128],
                                       f2slab[:, ch * 512:(ch + 1) * 512],
                                       start=False, stop=(ch == LC - 1))
                    zend = L.inc(mm, "pe")
                    g_pe[(fb, tt)] = zend
                    L.wait(dv, "dv", "pe", zend)
                    L.wait(dv, "dv", f"sx{p}", g_st[p])
                    ev = L.inc(dv.tensor_copy(fstage[p], ps[zbank][:]), "dv")
                    g_ev[(fb, tt)] = ev
                    L.wait(ac, "ac", "dv", ev)
                    L.inc(ac.dma_start(out=z_sc[tt * 128:(tt + 1) * 128,
                                                fb * 512:(fb + 1) * 512],
                                       in_=fstage[p]), f"sx{p}", 16)
                    g_st[p] = L.cnt[f"sx{p}"]
                g_wload(fb + 2)
                g_f2load(fb + 1)

            if DEBUG_STOP == "G":
                return finish()
            zrow = [b0f(0, 4096), b0f(4096, 4096)]
            arow = [b0f(8192, 4096), b0f(12288, 4096)]
            xrow = [b0f(16384, 4096), b0f(20480, 4096)]
            lnrow = [b0f(24576, 4096), b0f(28672, 4096)]
            sqrow = [b1f(8192, 4096), b1f(12288, 4096)]
            gbcH = b1f(0, 4096)
            gbcH2 = b1f(4096, 4096)
            barrier("sy", "dv", "ac")
            gl1 = L.inc(sy.dma_start(out=gbcH,
                                     in_=VEC["ln_g"].partition_broadcast(128)),
                        "tx0", 16)
            gl2 = L.inc(sy.dma_start(out=gbcH2,
                                     in_=VEC["ln_b"].partition_broadcast(128)),
                        "tx0", 16)
            h_out = {}
            h_norm = {}
            h_st = [0]
            for tt in range(NT_OWN):
                p = tt % 2
                lxs = f"lx{p}"
                SS = S if p == 0 else S2
                if tt >= 2:
                    L.wait(sy, "sy", "dv", h_out[tt - 2])
                    L.wait(sy, "sy", "so", h_st[0])
                L.inc(sy.dma_start(out=zrow[p],
                                   in_=z_sc[tt * 128:(tt + 1) * 128, :]), lxs, 16)
                L.inc(sy.dma_start(out=arow[p],
                                   in_=at_sc[tt * 128:(tt + 1) * 128, :]), lxs, 16)
                l3 = L.inc(sy.dma_start(out=xrow[p],
                                        in_=x_own[tt * 128:(tt + 1) * 128, :]),
                           lxs, 16)
                stats, sd = ln_stats(zrow[p], sqrow[p], lxs, l3,
                                     h_out.get(tt - 2, 0), SS=SS)
                L.wait(ac, "ac", "dv", stats)
                L.wait(ac, "ac", "ac", sd)
                L.wait(ac, "ac", "dv", h_out.get(tt - 2, 0))
                ln = L.inc(ac.activation(lnrow[p], zrow[p], AF.Identity,
                                         bias=SS["nbias"][:], scale=SS["rstd"][:]),
                           "ac")
                h_norm[tt] = ln
                L.wait(dv, "dv", "ac", ln)
                L.wait(dv, "dv", "tx0", gl2)
                w1 = L.inc(dv.tensor_mul(sqrow[p], lnrow[p], gbcH), "dv")
                L.wait(dv, "dv", "dv", w1)
                w2 = L.inc(dv.tensor_add(lnrow[p], sqrow[p], gbcH2), "dv")
                L.wait(dv, "dv", "dv", w2)
                w3 = L.inc(dv.tensor_add(sqrow[p], lnrow[p], arow[p]), "dv")
                L.wait(dv, "dv", "dv", w3)
                fo = L.inc(dv.tensor_add(zrow[p], sqrow[p], xrow[p]), "dv")
                h_out[tt] = fo
                L.wait(sy, "sy", "dv", fo)
                L.inc(sy.dma_start(out=out_hbm[tt * 128:(tt + 1) * 128, :],
                                   in_=zrow[p]), "so", 16)
                h_st[0] = L.cnt["so"]
            return finish()

    return nc


def kernel(**inputs):
    nc = build()
    in_maps = []
    x = np.ascontiguousarray(inputs["L0_attn_hidden_states"])
    row = lambda a: np.ascontiguousarray(np.asarray(a, dtype=np.float32).reshape(1, -1))
    base = {
        "Wq": inputs["L0_attn_Wq"], "Wk": inputs["L0_attn_Wk"],
        "Wv": inputs["L0_attn_Wv"], "Wo": inputs["L0_attn_Wo"],
        "bq": row(inputs["L0_attn_bq"]), "bk": row(inputs["L0_attn_bk"]),
        "bv": row(inputs["L0_attn_bv"]), "bo": row(inputs["L0_attn_bo"]),
        "gq": row(inputs["L0_attn_gq"]), "gk": row(inputs["L0_attn_gk"]),
        "gate_w": inputs["L0_ffn_gate_w"], "up_w": inputs["L0_ffn_up_w"],
        "down_w": inputs["L0_ffn_down_w"], "fc1_w": inputs["L0_ffn_fc1_w"],
        "fc2_w": inputs["L0_ffn_fc2_w"], "expert_w": inputs["L0_ffn_expert_w"],
        "ln_g": row(inputs["L0_ffn_ln_gamma"]), "ln_b": row(inputs["L0_ffn_ln_beta"]),
    }
    base = {k: np.ascontiguousarray(np.asarray(v, dtype=np.float32))
            for k, v in base.items()}
    for c in range(8):
        b, half = c // 2, c % 2
        m = dict(base)
        m["x_own"] = np.ascontiguousarray(x[b, half * TOK:(half + 1) * TOK, :])
        m["x_oth"] = np.ascontiguousarray(x[b, (1 - half) * TOK:(2 - half) * TOK, :])
        in_maps.append(m)
    res = run_bass_kernel_spmd(nc, in_maps, core_ids=list(range(8)))
    outs = [res.results[c]["out"] for c in range(8)]
    full = np.empty((B, SEQ, D), dtype=np.float32)
    for c in range(8):
        b, half = c // 2, c % 2
        full[b, half * TOK:(half + 1) * TOK, :] = outs[c]
    return full
